# version 01
# baseline (speedup 1.0000x reference)
import math
import numpy as np
import ml_dtypes
from contextlib import ExitStack

import concourse.bass as bass
import concourse.mybir as mybir
from concourse.bass_utils import run_bass_kernel_spmd

F32 = mybir.dt.float32
BF16 = mybir.dt.bfloat16
OP = mybir.AluOpType
AF = mybir.ActivationFunctionType
AX = mybir.AxisListType

T = 2048
D = 2048
S = 512
H = 16
JQ = 6144
HID = 8192
DP = D // 128
NCH = T // S
NGRP = 4

MAGIC = 12582912.0
BASE_LR_INV = float(np.log(np.expm1(0.01)))
NS_A, NS_B, NS_C = 3.4445, -4.775, 2.0315

STOP_AFTER = None
STOP_SUB = None
DEBUG_TAPS = []

DMA_QUEUE_ENGINE = {"sync": "sync", "scalq": "scalar", "gpq": "gpsimd"}
ENGINES = ("sync", "tensor", "vector", "scalar", "gpsimd")


class Step:
    __slots__ = ("eng", "fn", "reads", "writes", "dma", "idx", "need_inc",
                 "sem", "val", "tag")

    def __init__(self, eng, fn, reads, writes, dma, idx, tag):
        self.eng, self.fn = eng, fn
        self.reads, self.writes = tuple(reads), tuple(writes)
        self.dma, self.idx, self.tag = dma, idx, tag
        self.need_inc = False
        self.sem = None
        self.val = None


class Prog:
    def __init__(self):
        self.steps = []

    def add(self, eng, fn, reads=(), writes=(), dma=None, tag=""):
        st = Step(eng, fn, reads, writes, dma, len(self.steps), tag)
        self.steps.append(st)
        return st

    def resolve(self):
        deps = [[] for _ in self.steps]
        lw = {}
        rd = {}
        for st in self.steps:
            eng = DMA_QUEUE_ENGINE[st.dma] if st.dma else st.eng
            dset = {}
            for r in st.reads:
                w = lw.get(r)
                if w is not None:
                    dset[w.idx] = w
            for wres in st.writes:
                w = lw.get(wres)
                if w is not None:
                    dset[w.idx] = w
                for r in rd.get(wres, ()):
                    dset[r.idx] = r
            for d in dset.values():
                if d is st:
                    continue
                d_eng = DMA_QUEUE_ENGINE[d.dma] if d.dma else d.eng
                if d.dma is None and st.dma is None and d_eng == eng:
                    continue
                d.need_inc = True
                deps[st.idx].append(d)
            for r in st.reads:
                rd.setdefault(r, []).append(st)
            for wres in st.writes:
                lw[wres] = st
                rd[wres] = []
        return deps


def build_nc():
    nc = bass.Bass("TRN2", target_bir_lowering=False, debug=False,
                   detect_race_conditions=False)

    x_in = nc.dram_tensor("x", [T, D], F32, kind="ExternalInput").ap()
    w0_in = nc.dram_tensor("w0", [H, 128, 128], F32, kind="ExternalInput").ap()
    w1_in = nc.dram_tensor("w1", [H, 128, 128], F32, kind="ExternalInput").ap()
    w2_in = nc.dram_tensor("w2", [H, 128, 128], F32, kind="ExternalInput").ap()
    wqkv_in = nc.dram_tensor("wqkv", [D, JQ], F32, kind="ExternalInput").ap()
    wlr_in = nc.dram_tensor("wlr", [D, 48], F32, kind="ExternalInput").ap()
    blr_in = nc.dram_tensor("blr", [48], F32, kind="ExternalInput").ap()
    wproj_in = nc.dram_tensor("wproj", [D, D], F32, kind="ExternalInput").ap()
    onw_in = nc.dram_tensor("onw", [128], F32, kind="ExternalInput").ap()
    gw_in = nc.dram_tensor("gw", [HID, D], F32, kind="ExternalInput").ap()
    gb_in = nc.dram_tensor("gb", [HID], F32, kind="ExternalInput").ap()
    vw_in = nc.dram_tensor("vw", [HID, D], F32, kind="ExternalInput").ap()
    vb_in = nc.dram_tensor("vb", [HID], F32, kind="ExternalInput").ap()
    ow_in = nc.dram_tensor("ow", [D, HID], F32, kind="ExternalInput").ap()
    ob_in = nc.dram_tensor("ob", [D], F32, kind="ExternalInput").ap()

    out_hbm = nc.dram_tensor("out", [T, D], F32, kind="ExternalOutput").ap()

    scr_wq = nc.dram_tensor("scr_wq", [48, DP, 128, 128], BF16).ap()
    scr_wp = nc.dram_tensor("scr_wp", [DP, 128, D], BF16).ap()
    scr_ln1 = nc.dram_tensor("scr_ln1", [S, D], BF16).ap()
    scr_qkv = nc.dram_tensor("scr_qkv", [JQ, S], BF16).ap()
    scr_attn = nc.dram_tensor("scr_attn", [H, 4, 128, 128], BF16).ap()
    scr_x1 = nc.dram_tensor("scr_x1", [T, D], F32).ap()
    scr_x2q = nc.dram_tensor("scr_x2q", [T, D], BF16).ap()
    scr_tg = nc.dram_tensor("scr_tg", [HID, D], BF16).ap()
    scr_tv = nc.dram_tensor("scr_tv", [HID, D], BF16).ap()
    scr_to = nc.dram_tensor("scr_to", [D, HID], BF16).ap()
    scr_y = nc.dram_tensor("scr_y", [S, HID], BF16).ap()
    scr_dbgc = nc.dram_tensor("scr_dbgc", [8, 128, 16], F32).ap()
    scr_bias = nc.dram_tensor("scr_bias", [2 * HID + D], BF16).ap()

    scr_map = {"scr_wq": scr_wq, "scr_wp": scr_wp, "scr_ln1": scr_ln1,
               "scr_qkv": scr_qkv, "scr_attn": scr_attn, "scr_x1": scr_x1,
               "scr_x2q": scr_x2q, "scr_tg": scr_tg, "scr_tv": scr_tv,
               "scr_to": scr_to, "scr_y": scr_y, "scr_dbgc": scr_dbgc}
    dbg_outs = {}
    for name in DEBUG_TAPS:
        src = scr_map[name]
        dbg_outs[name] = nc.dram_tensor("dbg_" + name, list(src.shape),
                                        src.dtype, kind="ExternalOutput").ap()

    prog = Prog()
    P = prog.add
    ctx = ExitStack()
    with ctx:
        sb = lambda name, shape, dt: ctx.enter_context(nc.sbuf_tensor(name, shape, dt))

        identf = sb("identf", [128, 128], F32)
        ones_colb = sb("ones_colb", [128, 1], BF16)
        ones_colf = sb("ones_colf", [128, 1], F32)
        ones_rowb = sb("ones_rowb", [1, 128], BF16)
        ones_rowf = sb("ones_rowf", [1, 128], F32)
        onw_rowt = sb("onw_rowt", [1, 128], F32)
        onw_bc = sb("onw_bc", [128, 512], BF16)
        blr2 = sb("blr2", [128, 1], F32)
        RL = sb("RL", [128, 512], F32)
        SCLR = sb("SCLR", [128, 256], F32)
        scl = sb("scl", [128, 64], F32)
        lnc = sb("lnc", [128, 32], F32)
        grow = sb("grow", [1, 16], F32)
        gcol = sb("gcol", [128, 8], F32)
        coltmp = sb("coltmp", [128, 32], F32)
        coltp2 = sb("coltp2", [128, 32], F32)
        cst = sb("cst", [128, 4], F32)
        epsc = sb("epsc", [128, 4], F32)
        nc.const_aps.register(F32, 1e-5, epsc[:, 0:1])
        nc.const_aps.register(F32, 1e-10, epsc[:, 1:2])
        nc.const_aps.register(F32, 1e-14, epsc[:, 2:3])
        W0c = sb("W0c", [128, 2048], F32)
        W1c = sb("W1c", [128, 2048], F32)
        W2c = sb("W2c", [128, 2048], F32)
        U0 = sb("U0", [128, 2048], BF16)
        U2 = sb("U2", [128, 2048], BF16)
        U1x = sb("U1x", [128, 2048], BF16)
        U1a = U1x
        U1b = U1x
        WN = sb("WN", [128, 48], F32)
        wlr_sb = sb("wlr_sb", [128, DP * 48], BF16)

        psA = ctx.enter_context(nc.psum_tensor("psA", [128, 2048], F32))
        psB = ctx.enter_context(nc.psum_tensor("psB", [128, 2048], F32))

        def pA(i):
            return psA[:, i * 512:(i + 1) * 512]

        def pB(i):
            return psB[:, i * 512:(i + 1) * 512]

        PO = [sb(f"PO{i}", [128, 8192], BF16) for i in range(8)]
        ktok = [PO[0][:, i * 2048:(i + 1) * 2048] for i in range(4)]
        vtok = [PO[1][:, i * 2048:(i + 1) * 2048] for i in range(4)]
        ln1_fm = [PO[2][:, i * 512:(i + 1) * 512] for i in range(DP)]
        Xns = [PO[3][:, i * 2048:(i + 1) * 2048] for i in range(3)]
        Tt_ns = PO[3][:, 3 * 2048:4 * 2048]
        A_ns = PO[4][:, 0:2048]
        B_ns = PO[4][:, 2048:4096]
        ln1tok = PO[4][:, 4096:6144]
        wqbuf = [PO[4][:, 6144:8192], PO[5][:, 0:2048]]
        wpbuf = wqbuf
        dgba_h = PO[5][:, 2048:2560]
        dhbm_h = PO[5][:, 2560:3072]
        hidn_h = PO[5][:, 3072:3584]
        attn_st = PO[5][:, 3584:4096]
        mfm = PO[5][:, 4096:4608]
        afm = [PO[5][:, 4608:5120], PO[5][:, 5120:5632]]
        qkst = [PO[5][:, 5632:6144], PO[5][:, 6144:6656]]
        kqst = qkst
        vst = [PO[5][:, 6656:7168], PO[5][:, 7168:7680]]
        sqst = PO[5][:, 7680:8192]
        invnE = PO[6][:, 0:512]
        ew = [PO[6][:, 512 + i * 512:512 + (i + 1) * 512] for i in range(6)]
        tstage = [PO[6][:, 3584 + i * 512:3584 + (i + 1) * 512] for i in range(2)]
        f32t = PO[6][:, 4608:5632].bitcast(F32)
        rowb = ([PO[6][0:1, 5632 + i * 512:5632 + (i + 1) * 512] for i in range(3)])
        ygrp = [PO[i][:, 0:8192] for i in range(4)]
        gvsetG = PO[4][:, 0:8192]
        gvsetV = PO[5][:, 0:8192]
        x2qfm = [PO[6][:, i * 512:(i + 1) * 512] for i in range(DP)]
        yfm = [PO[7][:, i * 512:(i + 1) * 512] for i in range(4)]
        owbuf = [PO[7][:, 2048 + i * 1024:2048 + (i + 1) * 1024] for i in range(2)]
        ffnew = [PO[7][:, 4096 + i * 512:4096 + (i + 1) * 512] for i in range(2)]
        brow = [PO[7][0:1, 5120 + i * 512:5120 + (i + 1) * 512] for i in range(4)]
        rowfP = PO[7][0:1, 7168:8192].bitcast(F32)
        POOL_KEYS = tuple(
            [f"ktok{i}" for i in range(4)] + [f"vtok{i}" for i in range(4)]
            + [f"lnfm{i}" for i in range(DP)] + [f"Xns{i}" for i in range(3)]
            + ["Tt", "A_ns", "B_ns", "ln1tok", "wqb0", "wqb1", "wpb0", "wpb1",
               "dgba_h", "dhbm_h", "hidn_h", "attn_st", "mfm", "afm0", "afm1",
               "qkst0", "qkst1", "kqst0", "kqst1", "vst0", "vst1", "sqst", "invnE"]
            + [f"ygrp{i}" for i in range(4)] + [f"gvsetG{i}" for i in range(DP)]
            + [f"gvsetV{i}" for i in range(DP)] + [f"x2qfm{i}" for i in range(DP)]
            + [f"yfm{i}" for i in range(4)] + ["owb0", "owb1"]
            + [f"ew{i}" for i in range(6)] + ["tsg0", "tsg1", "f32t",
               "brow0", "brow1", "brow2", "brow3", "rowf0", "ffnew0", "ffnew1",
               "rowb0", "rowb1", "rowb2"])
        xa = [sb(f"xa{i}", [128, 2048], F32) for i in range(2)]
        winit = [xa[0][:, 0:128], xa[1][:, 0:128]]
        f32a = sb("f32a", [128, 2048], F32)
        f32c = sb("f32c", [128, 512], F32)
        gstage = [sb(f"gst{i}", [128, 512], F32) for i in range(2)]
        gacc = sb("gacc", [128, 768], F32)

        def seq(tag, *fns):
            pass

        def f_id(g):
            g.memset(identf[:], 0.0)
            g.affine_select(out=identf[:], in_=identf[:], compare_op=OP.not_equal,
                            fill=1.0, base=0, pattern=[[-1, 128]], channel_multiplier=1)
            g.memset(ones_colb[:], 1.0)
            g.memset(ones_colf[:], 1.0)
            g.memset(ones_rowb[:], 1.0)
            g.memset(ones_rowf[:], 1.0)
            g.memset(RL[:], 0.0)
            g.memset(cst[:], 127.0)
            g.memset(epsc[:, 0:1], 1e-5)
            g.memset(epsc[:, 1:2], 1e-10)
            g.memset(epsc[:, 2:3], 1e-14)
            g.memset(coltmp[:], 0.0)
            g.memset(coltp2[:], 0.0)
            g.memset(lnc[:], 0.0)
            return g.memset(f32c[:], 0.0)
        P("gpsimd", f_id, writes=("identf", "ones", "RL0", "f32c",
                                  "coltmp", "coltp2", "lnc", "cst"), tag="ident")

        for jb in range(48):
            P("gpsimd", (lambda g, jb=jb: g.dma_start(
                out=scr_wq[jb],
                in_=wqkv_in[:, jb * 128:(jb + 1) * 128].rearrange("(a p) j -> a p j", p=128))),
              writes=(f"swq{jb}",), dma="gpq", tag="wqcast")
        for db in range(DP):
            P("gpsimd", (lambda g, db=db: g.dma_start(
                out=scr_wp[db], in_=wproj_in[db * 128:(db + 1) * 128, :])),
              writes=(f"swp{db}",), dma="gpq", tag="wpcast")
        for db in range(DP):
            P("gpsimd", (lambda g, db=db: g.dma_start(
                out=wlr_sb[:, db * 48:(db + 1) * 48],
                in_=wlr_in[db * 128:(db + 1) * 128, :])),
              writes=("wlr_sb",), dma="gpq", tag="wlrcast")
        P("gpsimd", lambda g: g.dma_start(out=scr_bias[0:HID], in_=gb_in),
          writes=("sbias0",), dma="gpq", tag="bcast0")
        P("gpsimd", lambda g: g.dma_start(out=scr_bias[HID:2 * HID], in_=vb_in),
          writes=("sbias1",), dma="gpq", tag="bcast1")
        P("gpsimd", lambda g: g.dma_start(out=scr_bias[2 * HID:], in_=ob_in),
          writes=("sbias2",), dma="gpq", tag="bcast2")

        P("sync", lambda s: s.dma_start(out=blr2[0:48, 0:1],
                                        in_=blr_in.rearrange("(p o) -> p o", o=1)),
          writes=("blr2",), dma="sync", tag="blr")
        P("vector", lambda v: v.tensor_scalar(blr2[0:48, 0:1], blr2[0:48, 0:1],
                                              BASE_LR_INV, None, OP.add),
          reads=("blr2",), writes=("blr2",), tag="blradd")
        P("sync", lambda s: s.dma_start(out=onw_rowt[0:1, :],
                                        in_=onw_in.rearrange("(o j) -> o j", o=1)),
          writes=("onw_rowt",), dma="sync", tag="onwld")
        P("tensor", lambda t: t.matmul(psB[:, 0:128], ones_rowf[0:1, :], onw_rowt[0:1, :],
                                       start=True, stop=True),
          reads=("ones", "onw_rowt"), writes=("pB0",), tag="onwmm")

        def f_onwbc(v):
            for i in range(4):
                ins = v.tensor_copy(onw_bc[:, i * 128:(i + 1) * 128], psB[:, 0:128])
            return ins
        P("vector", f_onwbc, reads=("pB0",), writes=("onw_bc",), tag="onwbc")

        for wi, (w_in, wkey) in enumerate(((w0_in, "W0c"), (w1_in, "W1c"), (w2_in, "W2c"))):
            WC = (W0c, W1c, W2c)[wi]
            for h in range(H):
                idx = wi * H + h
                P("sync", (lambda s, w_in=w_in, idx=idx: s.dma_start(
                    out=winit[idx % 2][:], in_=w_in[idx % H])),
                  writes=(f"xa{idx % 2}",), dma="sync", tag="wld")
                P("tensor", (lambda t, idx=idx: t.transpose(
                    pA(idx % 2)[:, 0:128], winit[idx % 2][:], identf[:])),
                  reads=(f"xa{idx % 2}", "identf"), writes=(f"pA{idx % 2}",), tag="wtr")
                P("vector", (lambda v, WC=WC, h=h, idx=idx: v.tensor_copy(
                    WC[:, h * 128:(h + 1) * 128], pA(idx % 2)[:, 0:128])),
                  reads=(f"pA{idx % 2}",), writes=(wkey,), tag="wev")

        for wi, (WC, wkey) in enumerate(((W0c, "W0c"), (W1c, "W1c"), (W2c, "W2c"))):
            P("scalar", (lambda a, WC=WC: a.activation(f32a[:], WC[:], AF.Square)),
              reads=(wkey,), writes=("f32a",), tag="wsq")
            P("vector", (lambda v, wi=wi: v.tensor_reduce(
                WN[:, wi * 16:(wi + 1) * 16],
                f32a[:].rearrange("p (a b) -> p a b", b=128), AX.X, OP.add)),
              reads=("f32a",), writes=("WN",), tag="wnred")
        P("scalar", lambda a: a.activation(WN[:, 0:48], WN[:, 0:48], AF.Sqrt),
          reads=("WN",), writes=("WN",), tag="wnsqrt")

        def refresh_u1a():
            P("vector", lambda v: v.tensor_copy(U1a[:], W1c[:]),
              reads=("W1c",), writes=("U1",), tag="u1a")

        def emit_usage():
            for (WC, wkey, U, ukey) in ((W0c, "W0c", U0, "U0"), (W2c, "W2c", U2, "U2"),
                                        (W1c, "W1c", U1b, "U1")):
                for hq in range(4):
                    def ftr(t, WC=WC, hq=hq):
                        for i in range(4):
                            h = hq * 4 + i
                            ins = t.transpose(psB[:, h * 128 - hq * 512:(h + 1) * 128 - hq * 512]
                                              if False else pB(hq)[:, i * 128:(i + 1) * 128],
                                              WC[:, h * 128:(h + 1) * 128], identf[:])
                        return ins
                    P("tensor", ftr, reads=(wkey, "identf"), writes=(f"pB{hq}",), tag="utr")
                def fev(a, U=U):
                    return a.activation(U[:], psB[:], AF.Copy)
                P("scalar", fev, reads=("pB0", "pB1", "pB2", "pB3"), writes=(ukey,), tag="uev")
        emit_usage()
        refresh_u1a()

        bgq = []

        NGT = HID * D // (128 * 512)

        def tile_view(w_hbm):
            r, c = w_hbm.shape
            nb = c // 512
            wv = w_hbm.rearrange("(a p) (b d) -> a p b d", p=128, d=512)
            return lambda i: wv[i // nb][:, i % nb, :]

        def make_gamma_tile(w_hbm, wi, tag):
            tv = tile_view(w_hbm)
            def emit(i):
                P("sync", (lambda s, i=i: s.dma_start(out=gstage[i % 2][:], in_=tv(i))),
                  writes=(f"gsg{i % 2}",), dma="sync", tag=f"gld{tag}")
                P("scalar", (lambda a, i=i, col=wi * NGT + i: a.activation(
                    f32t[:], gstage[i % 2][:], AF.Abs, accum_out=gacc[:, col:col + 1])),
                  reads=(f"gsg{i % 2}",), writes=("f32t", "gacc"), tag=f"gab{tag}")
            return emit

        def make_tern_tile(w_hbm, scr, wi, tag):
            tv = tile_view(w_hbm)
            sv = tile_view(scr)
            def emit(i):
                P("sync", (lambda s, i=i: s.dma_start(out=gstage[i % 2][:], in_=tv(i))),
                  writes=(f"gsg{i % 2}",), dma="sync", tag=f"tld{tag}")
                P("scalar", (lambda a, i=i: a.activation(
                    f32t[:], gstage[i % 2][:], AF.Copy, bias=MAGIC,
                    scale=gcol[:, 4 + wi:5 + wi])),
                  reads=(f"gsg{i % 2}", "gcol"), writes=("f32t",), tag=f"tq1{tag}")
                def f(v, i=i):
                    v.tensor_scalar(f32t[:], f32t[:], -MAGIC, -1.0, OP.add, OP.max)
                    return v.tensor_scalar(tstage[i % 2][:], f32t[:], 1.0, None, OP.min)
                P("vector", f, reads=("f32t",), writes=(f"tsg{i % 2}", "f32t"), tag=f"tq2{tag}")
                P("sync", (lambda s, i=i: s.dma_start(out=sv(i), in_=tstage[i % 2][:])),
                  reads=(f"tsg{i % 2}",), writes=(f"{tag}_{i}",), dma="sync", tag=f"tst{tag}")
            return emit

        for i in range(NGT):
            bgq.append((make_gamma_tile(gw_in, 0, "g"), i))
        for i in range(NGT):
            bgq.append((make_gamma_tile(vw_in, 1, "v"), i))
        for i in range(NGT):
            bgq.append((make_gamma_tile(ow_in, 2, "o"), i))
        GAMMA_FIN = "GAMMA_FIN"
        bgq.append((GAMMA_FIN, 0))
        for i in range(NGT):
            bgq.append((make_tern_tile(gw_in, scr_tg, 0, "tg"), i))
        for i in range(NGT):
            bgq.append((make_tern_tile(vw_in, scr_tv, 1, "tv"), i))
        for i in range(NGT):
            bgq.append((make_tern_tile(ow_in, scr_to, 2, "to"), i))

        def emit_gamma_final():
            for wi in range(3):
                P("vector", (lambda v, wi=wi: v.tensor_reduce(
                    coltmp[:, wi:wi + 1], gacc[:, wi * NGT:(wi + 1) * NGT], AX.X, OP.add)),
                  reads=("gacc",), writes=("coltmp",), tag="gred")
            P("tensor", lambda t: t.matmul(pB(3)[0:1, 0:8], ones_colf[:, 0:1],
                                           coltmp[:, 0:8], start=True, stop=True),
              reads=("coltmp", "ones"), writes=("pB3",), tag="gmm")

            def f(v):
                return v.tensor_scalar(grow[0:1, 0:8], pB(3)[0:1, 0:8], 1.0 / (HID * D), 1e-5,
                                       OP.mult, OP.max)
            P("vector", f, reads=("pB3",), writes=("grow",), tag="gfin")
            P("tensor", lambda t: t.matmul(pB(3)[:, 0:8], ones_rowf[0:1, :], grow[0:1, 0:8],
                                           start=True, stop=True),
              reads=("grow", "ones"), writes=("pB3",), tag="gbc")

            P("scalar", lambda a: a.activation(gcol[:, 0:8], pB(3)[:, 0:8], AF.Copy),
              reads=("pB3",), writes=("gcol",), tag="gcp")
            P("vector", lambda v: v.reciprocal(gcol[:, 4:7], gcol[:, 0:3]),
              reads=("gcol",), writes=("gcol",), tag="ginv")
            P("scalar", lambda a: a.activation(grow[0:1, 8:11], grow[0:1, 0:3], AF.Copy),
              reads=("grow",), writes=("grow",), tag="gbnc")
            P("vector", lambda v: v.reciprocal(grow[0:1, 4:7], grow[0:1, 8:11]),
              reads=("grow",), writes=("grow",), tag="grinv")

        bg_pos = [0]

        def drain_bg(n):
            k = 0
            while k < n and bg_pos[0] < len(bgq):
                emitfn, i = bgq[bg_pos[0]]
                bg_pos[0] += 1
                if emitfn == GAMMA_FIN:
                    emit_gamma_final()
                else:
                    emitfn(i)
                    k += 1

        def emit_chunk(c):
            c0 = c * S
            for t8 in range(4):
                P("sync", (lambda s, t8=t8: s.dma_start(
                    out=xa[t8 % 2][:], in_=x_in[c0 + t8 * 128:c0 + (t8 + 1) * 128, :])),
                  writes=(f"xa{t8 % 2}",), dma="sync", tag="xld")
                P("scalar", (lambda a, t8=t8: a.activation(
                    f32a[:], xa[t8 % 2][:], AF.Square, accum_out=lnc[:, 1:2])),
                  reads=(f"xa{t8 % 2}",), writes=("f32a", "lnc"), tag="lnsq")
                P("vector", (lambda v, t8=t8: v.reduce_sum(lnc[:, 0:1], xa[t8 % 2][:], AX.X)),
                  reads=(f"xa{t8 % 2}",), writes=("lnc",), tag="lnsum")
                P("gpsimd", lambda g: g.tensor_copy(lnc[:, 20:21], lnc[:, 0:1]),
                  reads=("lnc",), writes=("lnc",), tag="lnecho")

                def f(v):
                    v.tensor_scalar(lnc[:, 2:3], lnc[:, 20:21], -1.0 / D, None, OP.mult)
                    return v.tensor_scalar(lnc[:, 3:4], lnc[:, 1:2], 1.0 / D, None, OP.mult)
                P("vector", f, reads=("lnc",), writes=("lnc",), tag="lnst")
                P("scalar", lambda a: a.activation(lnc[:, 21:22], lnc[:, 2:3], AF.Square),
                  reads=("lnc",), writes=("lnc",), tag="lnmu2")
                P("vector", lambda v: v.tensor_tensor(lnc[:, 4:5], lnc[:, 3:4],
                                                      lnc[:, 21:22], OP.subtract),
                  reads=("lnc",), writes=("lnc",), tag="lnv")
                P("scalar", lambda a: a.activation(lnc[:, 5:6], lnc[:, 4:5], AF.Sqrt,
                                                   bias=1e-5),
                  reads=("lnc",), writes=("lnc",), tag="lnsd")
                P("vector", lambda v: v.reciprocal(lnc[:, 6:7], lnc[:, 5:6]),
                  reads=("lnc",), writes=("lnc",), tag="lnrs")
                P("gpsimd", lambda g: g.tensor_tensor(lnc[:, 7:8], lnc[:, 2:3],
                                                      lnc[:, 6:7], OP.mult),
                  reads=("lnc",), writes=("lnc",), tag="lnb")
                P("scalar", (lambda a, t8=t8: a.activation(
                    ln1tok[:], xa[t8 % 2][:], AF.Identity,
                    bias=lnc[:, 7:8], scale=lnc[:, 6:7])),
                  reads=(f"xa{t8 % 2}", "lnc"), writes=("ln1tok",), tag="lnout")
                P("scalar", (lambda s, t8=t8: s.dma_start(
                    out=scr_ln1[t8 * 128:(t8 + 1) * 128, :], in_=ln1tok[:])),
                  reads=("ln1tok",), writes=(f"sln{t8}",), dma="scalq", tag="lnst8")
                if DEBUG_TAPS and c == 0:
                    P("scalar", (lambda s, t8=t8: s.dma_start(
                        out=scr_dbgc[t8], in_=lnc[:, 0:16])),
                      reads=("lnc",), writes=(f"DBGC{t8}",), dma="scalq", tag="dbgc")

            if STOP_SUB == "ln":
                return
            for dp in range(DP):
                P("sync", (lambda s, dp=dp: s.dma_start(
                    out=ln1_fm[dp][:], in_=scr_ln1[:, dp * 128:(dp + 1) * 128],
                    transpose=True)),
                  reads=("sln0", "sln1", "sln2", "sln3"), writes=(f"lnfm{dp}",),
                  dma="sync", tag="lnflip")

            jb_order = list(range(16, 32)) + list(range(0, 16)) + list(range(32, 48))
            for oi, jb in enumerate(jb_order):
                P("sync", (lambda s, jb=jb, oi=oi: s.dma_start(
                    out=wqbuf[oi % 2][:].rearrange("p (a j) -> p a j", a=DP),
                    in_=scr_wq[jb].transpose([1, 0, 2]))),
                  reads=(f"swq{jb}",), writes=(f"wqb{oi % 2}",), dma="sync", tag="wqld")

                def fmm(t, jb=jb, oi=oi):
                    for dp in range(DP):
                        ins = t.matmul(pA(oi % 4), wqbuf[oi % 2][:, dp * 128:(dp + 1) * 128],
                                       ln1_fm[dp][:], start=(dp == 0), stop=(dp == DP - 1))
                    return ins
                P("tensor", fmm,
                  reads=tuple([f"wqb{oi % 2}"] + [f"lnfm{dp}" for dp in range(DP)]),
                  writes=(f"pA{oi % 4}",), tag=f"qkvmm{jb}")
                P("scalar", (lambda a, oi=oi: a.activation(
                    qkst[oi % 2][:], pA(oi % 4), AF.Silu)),
                  reads=(f"pA{oi % 4}",), writes=(f"qkst{oi % 2}",), tag="qkvsilu")
                kind = 0 if jb < 16 else (1 if jb < 32 else 2)
                if kind != 2:
                    h = jb % 16
                    P("scalar", (lambda a, oi=oi: a.activation(
                        sqst[:], qkst[oi % 2][:], AF.Square)),
                      reads=(f"qkst{oi % 2}",), writes=("sqst",), tag="qsq")
                    P("tensor", (lambda t, oi=oi: t.matmul(
                        pB(2 + oi % 2)[0:1, :], ones_colb[:, 0:1], sqst[:],
                        start=True, stop=True)),
                      reads=("sqst", "ones"), writes=(f"pB{2 + oi % 2}",), tag="qnmm")
                    P("vector", (lambda v, oi=oi: v.tensor_copy(
                        rowb[oi % 2][0:1, :], pB(2 + oi % 2)[0:1, :])),
                      reads=(f"pB{2 + oi % 2}",), writes=(f"rowb{oi % 2}",), tag="qnrow")
                    P("tensor", (lambda t, oi=oi: t.matmul(
                        pB(2 + oi % 2)[:, :], ones_rowb[0:1, :], rowb[oi % 2][0:1, :],
                        start=True, stop=True)),
                      reads=(f"rowb{oi % 2}", "ones"), writes=(f"pB{2 + oi % 2}",), tag="qnex")
                    P("scalar", (lambda a, oi=oi: a.activation(
                        f32c[:], pB(2 + oi % 2)[:, :], AF.Sqrt)),
                      reads=(f"pB{2 + oi % 2}",), writes=("f32c",), tag="qnsqrt")
                    def fqn(v):
                        v.tensor_scalar(f32c[:], f32c[:], 1e-5, None, OP.add)
                        return v.reciprocal(f32c[:], f32c[:])
                    P("vector", fqn, reads=("f32c",), writes=("f32c",), tag="qnrec")
                    P("vector", (lambda v, oi=oi: v.tensor_tensor(
                        qkst[oi % 2][:], qkst[oi % 2][:], f32c[:], OP.mult)),
                      reads=(f"qkst{oi % 2}", "f32c"), writes=(f"qkst{oi % 2}",), tag="qnorm")
                P("scalar", (lambda s, jb=jb, oi=oi: s.dma_start(
                    out=scr_qkv[jb * 128:(jb + 1) * 128, :], in_=qkst[oi % 2][:])),
                  reads=(f"qkst{oi % 2}",), writes=(f"sqkv{jb}",), dma="scalq", tag="qkvst")
                drain_bg(2)

            if STOP_SUB == "qkv":
                return
            def fmm(t):
                for dp in range(DP):
                    ins = t.matmul(pB(2)[0:48, :], wlr_sb[:, dp * 48:(dp + 1) * 48],
                                   ln1_fm[dp][:], start=(dp == 0), stop=(dp == DP - 1))
                return ins
            P("tensor", fmm,
              reads=tuple(["wlr_sb"] + [f"lnfm{dp}" for dp in range(DP)]),
              writes=("pB2",), tag="lrmm")
            P("scalar", lambda a: a.activation(RL[0:48, :], pB(2)[0:48, :], AF.Exp,
                                               bias=blr2[0:48, 0:1]),
              reads=("pB2", "blr2"), writes=("RL0",), tag="lrexp")
            P("scalar", lambda a: a.activation(RL[0:48, :], RL[0:48, :], AF.Ln, bias=1.0),
              reads=("RL0",), writes=("RL0",), tag="lrln")
            for t8 in range(4):
                P("tensor", (lambda t, t8=t8: t.transpose(
                    pB(3)[:, 0:128], RL[:, t8 * 128:(t8 + 1) * 128], identf[:])),
                  reads=("RL0", "identf"), writes=("pB3",), tag="lrtr")
                P("vector", (lambda v, t8=t8: v.tensor_copy(
                    SCLR[:, t8 * 64:t8 * 64 + 48], pB(3)[:, 0:48])),
                  reads=("pB3",), writes=("SCLR",), tag="lrsc")

            if STOP_SUB == "lr":
                return
            for t8 in range(4):
                P("sync", (lambda s, t8=t8: s.dma_start(
                    out=ktok[t8][:], in_=scr_qkv[2048:4096, t8 * 128:(t8 + 1) * 128],
                    transpose=True)),
                  reads=tuple(f"sqkv{16 + h}" for h in range(16)),
                  writes=(f"ktok{t8}",), dma="sync", tag="kflip")
                P("sync", (lambda s, t8=t8: s.dma_start(
                    out=vtok[t8][:], in_=scr_qkv[4096:6144, t8 * 128:(t8 + 1) * 128],
                    transpose=True)),
                  reads=tuple(f"sqkv{32 + h}" for h in range(16)),
                  writes=(f"vtok{t8}",), dma="sync", tag="vflip")

            if STOP_SUB == "flip":
                return
            for h in range(H):
                hs = slice(h * 128, (h + 1) * 128)
                P("sync", (lambda s, h=h: s.dma_start(
                    out=kqst[h % 2][:], in_=scr_qkv[(16 + h) * 128:(17 + h) * 128, :])),
                  reads=(f"sqkv{16 + h}",), writes=(f"kqst{h % 2}",), dma="sync", tag="krld")
                P("sync", (lambda s, h=h: s.dma_start(
                    out=vst[h % 2][:], in_=scr_qkv[(32 + h) * 128:(33 + h) * 128, :])),
                  reads=(f"sqkv{32 + h}",), writes=(f"vst{h % 2}",), dma="sync", tag="vrld")

                slg = h % 2
                slh = 2 + h % 2
                sld = h % 2

                def fmm(t, h=h, hs=hs, slg=slg, slh=slh, sld=sld):
                    for t8 in range(4):
                        t.matmul(pA(slg)[:, t8 * 128:(t8 + 1) * 128],
                                 kqst[h % 2][:, t8 * 128:(t8 + 1) * 128], U0[:, hs],
                                 start=True, stop=True)
                    for t8 in range(4):
                        t.matmul(pA(slh)[:, t8 * 128:(t8 + 1) * 128],
                                 kqst[h % 2][:, t8 * 128:(t8 + 1) * 128], U2[:, hs],
                                 start=True, stop=True)
                    for t8 in range(4):
                        ins = t.matmul(pB(sld)[:, t8 * 128:(t8 + 1) * 128],
                                       vst[h % 2][:, t8 * 128:(t8 + 1) * 128], U1a[:, hs],
                                       start=True, stop=True)
                    return ins
                P("tensor", fmm,
                  reads=(f"kqst{h % 2}", f"vst{h % 2}", "U0", "U2", "U1"),
                  writes=(f"pA{slg}", f"pA{slh}", f"pB{sld}"), tag=f"fwdk{h}")

                P("scalar", (lambda a, slg=slg: a.activation(ew[0][:], pA(slg), AF.Copy)),
                  reads=(f"pA{slg}",), writes=("ew0",), tag="evg")
                P("scalar", (lambda a, slh=slh: a.activation(ew[1][:], pA(slh), AF.Copy)),
                  reads=(f"pA{slh}",), writes=("ew1",), tag="evh")
                P("scalar", (lambda a, sld=sld: a.activation(ew[2][:], pB(sld), AF.Copy)),
                  reads=(f"pB{sld}",), writes=("ew2",), tag="evd")
                P("scalar", lambda a: a.activation(ew[3][:], ew[0][:], AF.Silu),
                  reads=("ew0",), writes=("ew3",), tag="sg")
                P("scalar", lambda a: a.activation(ew[4][:], ew[0][:], AF.Sigmoid),
                  reads=("ew0",), writes=("ew4",), tag="sig")

                def fv1(v, h=h):
                    v.tensor_tensor(ew[5][:], ew[3][:], ew[1][:], OP.mult)
                    for t8 in range(4):
                        v.tensor_scalar(hidn_h[:, t8 * 128:(t8 + 1) * 128],
                                        ew[5][:, t8 * 128:(t8 + 1) * 128],
                                        SCLR[:, t8 * 64 + 16 + h:t8 * 64 + 17 + h], None, OP.mult)
                    v.tensor_tensor(ew[5][:], ew[2][:], ew[3][:], OP.mult)
                    for t8 in range(4):
                        v.tensor_scalar(dhbm_h[:, t8 * 128:(t8 + 1) * 128],
                                        ew[5][:, t8 * 128:(t8 + 1) * 128],
                                        SCLR[:, t8 * 64 + 32 + h:t8 * 64 + 33 + h], None, OP.mult)
                    v.tensor_tensor(ew[2][:], ew[2][:], ew[1][:], OP.mult)
                    v.tensor_tensor(ew[1][:], ew[0][:], ew[4][:], OP.mult)
                    v.scalar_tensor_tensor(ew[0][:], ew[0][:], 1.0, ew[1][:], OP.add, OP.subtract)
                    v.tensor_tensor(ew[2][:], ew[2][:], ew[4][:], OP.mult)
                    v.tensor_tensor(ew[2][:], ew[2][:], ew[0][:], OP.mult)
                    for t8 in range(4):
                        ins = v.tensor_scalar(dgba_h[:, t8 * 128:(t8 + 1) * 128],
                                              ew[2][:, t8 * 128:(t8 + 1) * 128],
                                              SCLR[:, t8 * 64 + h:t8 * 64 + 1 + h], None, OP.mult)
                    return ins
                P("vector", fv1,
                  reads=("ew0", "ew1", "ew2", "ew3", "ew4", "SCLR"),
                  writes=("ew0", "ew1", "ew2", "ew5", "hidn_h", "dhbm_h", "dgba_h"),
                  tag=f"ttew{h}")

                gslot = 2 + h % 2

                def fg(t, h=h, hs=hs, gslot=gslot):
                    for t8 in range(4):
                        ts8 = slice(t8 * 128, (t8 + 1) * 128)
                        t.matmul(pB(gslot)[:, 0:128], dgba_h[:, ts8], ktok[t8][:, hs],
                                 start=(t8 == 0), stop=(t8 == 3))
                    for t8 in range(4):
                        ts8 = slice(t8 * 128, (t8 + 1) * 128)
                        t.matmul(pB(gslot)[:, 128:256], vtok[t8][:, hs], hidn_h[:, ts8],
                                 start=(t8 == 0), stop=(t8 == 3))
                    for t8 in range(4):
                        ts8 = slice(t8 * 128, (t8 + 1) * 128)
                        ins = t.matmul(pB(gslot)[:, 256:384], dhbm_h[:, ts8], ktok[t8][:, hs],
                                       start=(t8 == 0), stop=(t8 == 3))
                    return ins
                P("tensor", fg,
                  reads=tuple(["dgba_h", "hidn_h", "dhbm_h"]
                              + [f"ktok{i}" for i in range(4)]
                              + [f"vtok{i}" for i in range(4)]),
                  writes=(f"pB{gslot}",), tag=f"gmm{h}")

                def fge(v, h=h, hs=hs, gslot=gslot):
                    v.tensor_copy(Xns[0][:, hs], pB(gslot)[:, 0:128])
                    v.tensor_copy(Xns[1][:, hs], pB(gslot)[:, 128:256])
                    return v.tensor_copy(Xns[2][:, hs], pB(gslot)[:, 256:384])
                P("vector", fge, reads=(f"pB{gslot}",),
                  writes=("Xns0", "Xns1", "Xns2"), tag=f"gev{h}")
                if h % 4 == 3:
                    drain_bg(1)

            if STOP_SUB == "fwd":
                return
            for w in range(3):
                xk = f"Xns{w}"
                P("scalar", (lambda a, w=w: a.activation(f32a[:], Xns[w][:], AF.Square)),
                  reads=(xk,), writes=("f32a",), tag="nsq")
                P("vector", lambda v: v.tensor_reduce(
                    coltmp[:, 0:16], f32a[:].rearrange("p (a b) -> p a b", b=128),
                    AX.X, OP.add),
                  reads=("f32a",), writes=("coltmp",), tag="nsred")
                P("scalar", lambda a: a.activation(ew[5][:, 0:16], coltmp[:, 0:16], AF.Copy),
                  reads=("coltmp",), writes=("ew5",), tag="nscast")
                P("tensor", lambda t: t.matmul(pB(3)[0:1, 0:16], ones_colb[:, 0:1],
                                               ew[5][:, 0:16], start=True, stop=True),
                  reads=("ew5", "ones"), writes=("pB3",), tag="nsfmm")
                P("scalar", lambda a: a.activation(f32c[0:1, 0:16], pB(3)[0:1, 0:16],
                                                   AF.Sqrt, bias=1e-14),
                  reads=("pB3",), writes=("f32c",), tag="nsfsq")
                P("vector", lambda v: v.reciprocal(f32c[0:1, 16:32], f32c[0:1, 0:16]),
                  reads=("f32c",), writes=("f32c",), tag="nsfrec")
                P("scalar", lambda a: a.activation(rowb[2][0:1, 0:16], f32c[0:1, 16:32],
                                                   AF.Copy),
                  reads=("f32c",), writes=("rowb2",), tag="nsfrow")
                P("tensor", lambda t: t.matmul(pB(3)[:, 0:16], ones_rowb[0:1, :],
                                               rowb[2][0:1, 0:16], start=True, stop=True),
                  reads=("rowb2", "ones"), writes=("pB3",), tag="nsfex")
                P("scalar", lambda a: a.activation(coltmp[:, 0:16], pB(3)[:, 0:16], AF.Copy),
                  reads=("pB3",), writes=("coltmp",), tag="nsfcol")

                def fsc(v, w=w):
                    for h in range(H):
                        ins = v.tensor_scalar(Xns[w][:, h * 128:(h + 1) * 128],
                                              Xns[w][:, h * 128:(h + 1) * 128],
                                              coltmp[:, h:h + 1], None, OP.mult)
                    return ins
                P("vector", fsc, reads=(xk, "coltmp"), writes=(xk,), tag="nsscale")

                for it in range(2):
                    P("scalar", (lambda a, w=w: a.activation(f32a[:], Xns[w][:], AF.Copy)),
                      reads=(xk,), writes=("f32a",), tag="nscp")

                    def ftr(t):
                        for h in range(H):
                            ins = t.transpose(
                                psA[:, h * 128:(h + 1) * 128],
                                f32a[:, h * 128:(h + 1) * 128], identf[:])
                        return ins
                    P("tensor", ftr, reads=("f32a", "identf"),
                      writes=("pA0", "pA1", "pA2", "pA3"), tag="nstr")
                    P("vector", lambda v: v.tensor_copy(Tt_ns[:], psA[:]),
                      reads=("pA0", "pA1", "pA2", "pA3"), writes=("Tt",), tag="nstte")

                    def fa(t, w=w):
                        for h in range(H):
                            hs2 = slice(h * 128, (h + 1) * 128)
                            ins = t.matmul(psB[:, hs2], Xns[w][:, hs2], Xns[w][:, hs2],
                                           start=True, stop=True)
                        return ins
                    P("tensor", fa, reads=(xk,),
                      writes=("pB0", "pB1", "pB2", "pB3"), tag="nsA")
                    P("vector", lambda v: v.tensor_copy(A_ns[:], psB[:]),
                      reads=("pB0", "pB1", "pB2", "pB3"), writes=("A_ns",), tag="nsAe")

                    def fa2(t):
                        for h in range(H):
                            hs2 = slice(h * 128, (h + 1) * 128)
                            ins = t.matmul(psA[:, hs2], A_ns[:, hs2], A_ns[:, hs2],
                                           start=True, stop=True)
                        return ins
                    P("tensor", fa2, reads=("A_ns",),
                      writes=("pA0", "pA1", "pA2", "pA3"), tag="nsA2")
                    P("scalar", lambda a: a.activation(B_ns[:], psA[:], AF.Copy, scale=NS_C),
                      reads=("pA0", "pA1", "pA2", "pA3"), writes=("B_ns",), tag="nsA2e")
                    P("vector", lambda v: v.scalar_tensor_tensor(
                        B_ns[:], A_ns[:], NS_B, B_ns[:], OP.mult, OP.add),
                      reads=("A_ns", "B_ns"), writes=("B_ns",), tag="nsB")

                    def fxb(t):
                        for h in range(H):
                            hs2 = slice(h * 128, (h + 1) * 128)
                            ins = t.matmul(psB[:, hs2], Tt_ns[:, hs2], B_ns[:, hs2],
                                           start=True, stop=True)
                        return ins
                    P("tensor", fxb, reads=("Tt", "B_ns"),
                      writes=("pB0", "pB1", "pB2", "pB3"), tag="nsXB")
                    P("vector", (lambda v, w=w: v.scalar_tensor_tensor(
                        Xns[w][:], Xns[w][:], NS_A, psB[:], OP.mult, OP.add)),
                      reads=(xk, "pB0", "pB1", "pB2", "pB3"), writes=(xk,), tag="nsX")

                WC, wkey = ((W0c, "W0c"), (W1c, "W1c"), (W2c, "W2c"))[w]
                P("vector", (lambda v, WC=WC, w=w: v.tensor_tensor(
                    WC[:], WC[:], Xns[w][:], OP.add)),
                  reads=(wkey, xk), writes=(wkey,), tag="wup")
                P("scalar", (lambda a, WC=WC: a.activation(f32a[:], WC[:], AF.Square)),
                  reads=(wkey,), writes=("f32a",), tag="wnsq")
                P("vector", lambda v: v.tensor_reduce(
                    coltmp[:, 0:16], f32a[:].rearrange("p (a b) -> p a b", b=128),
                    AX.X, OP.add),
                  reads=("f32a",), writes=("coltmp",), tag="wnred")
                P("scalar", lambda a: a.activation(coltmp[:, 16:32], coltmp[:, 0:16],
                                                   AF.Sqrt, bias=1e-10),
                  reads=("coltmp",), writes=("coltmp",), tag="wnsq2")
                P("vector", lambda v: v.reciprocal(coltmp[:, 0:16], coltmp[:, 16:32]),
                  reads=("coltmp",), writes=("coltmp",), tag="wnrec")
                P("gpsimd", (lambda g, w=w: g.tensor_tensor(
                    coltmp[:, 16:32], coltmp[:, 0:16],
                    WN[:, w * 16:(w + 1) * 16], OP.mult)),
                  reads=("coltmp", "WN"), writes=("coltmp",), tag="wnsc")

                def fws(v, WC=WC):
                    for h in range(H):
                        ins = v.tensor_scalar(WC[:, h * 128:(h + 1) * 128],
                                              WC[:, h * 128:(h + 1) * 128],
                                              coltmp[:, 16 + h:17 + h], None, OP.mult)
                    return ins
                P("vector", fws, reads=(wkey, "coltmp"), writes=(wkey,), tag="wsc")

            emit_usage()
            drain_bg(2)

            if STOP_SUB == "ns":
                return
            for h in range(H):
                hs = slice(h * 128, (h + 1) * 128)
                P("sync", (lambda s, h=h: s.dma_start(
                    out=kqst[h % 2][:], in_=scr_qkv[h * 128:(h + 1) * 128, :])),
                  reads=(f"sqkv{h}",), writes=(f"kqst{h % 2}",), dma="sync", tag="qrld")
                slg = h % 2
                slh = 2 + h % 2

                def fmm(t, h=h, hs=hs, slg=slg, slh=slh):
                    for t8 in range(4):
                        t.matmul(pA(slg)[:, t8 * 128:(t8 + 1) * 128],
                                 kqst[h % 2][:, t8 * 128:(t8 + 1) * 128], U0[:, hs],
                                 start=True, stop=True)
                    for t8 in range(4):
                        ins = t.matmul(pA(slh)[:, t8 * 128:(t8 + 1) * 128],
                                       kqst[h % 2][:, t8 * 128:(t8 + 1) * 128], U2[:, hs],
                                       start=True, stop=True)
                    return ins
                P("tensor", fmm, reads=(f"kqst{h % 2}", "U0", "U2"),
                  writes=(f"pA{slg}", f"pA{slh}"), tag=f"fwdq{h}")
                P("scalar", (lambda a, slg=slg: a.activation(ew[0][:], pA(slg), AF.Silu)),
                  reads=(f"pA{slg}",), writes=("ew0",), tag="qsilu")
                P("scalar", (lambda a, slh=slh: a.activation(ew[1][:], pA(slh), AF.Copy)),
                  reads=(f"pA{slh}",), writes=("ew1",), tag="qcp")
                P("vector", lambda v: v.tensor_tensor(f32c[:], ew[0][:], ew[1][:], OP.mult),
                  reads=("ew0", "ew1"), writes=("f32c",), tag="mtok")

                def ftr(t):
                    for t8 in range(4):
                        ins = t.transpose(pB(2)[:, t8 * 128:(t8 + 1) * 128],
                                          f32c[:, t8 * 128:(t8 + 1) * 128], identf[:])
                    return ins
                P("tensor", ftr, reads=("f32c", "identf"), writes=("pB2",), tag="mtr")
                P("scalar", lambda a: a.activation(mfm[:], pB(2), AF.Copy),
                  reads=("pB2",), writes=("mfm",), tag="mev")

                def fo(t, hs=hs):
                    for t8 in range(4):
                        ins = t.matmul(pB(3)[:, t8 * 128:(t8 + 1) * 128],
                                       mfm[:, t8 * 128:(t8 + 1) * 128], U1b[:, hs],
                                       start=True, stop=True)
                    return ins
                P("tensor", fo, reads=("mfm", "U1"), writes=("pB3",), tag=f"omm{h}")
                P("scalar", lambda a: a.activation(f32c[:], pB(3), AF.Square),
                  reads=("pB3",), writes=("f32c",), tag="osq")
                P("vector", lambda v: v.tensor_reduce(
                    coltp2[:, 0:4], f32c[:].rearrange("p (a b) -> p a b", b=128),
                    AX.X, OP.add),
                  reads=("f32c",), writes=("coltp2",), tag="ored")
                P("scalar", lambda a: a.activation(coltp2[:, 4:8], coltp2[:, 0:4],
                                                   AF.Sqrt, bias=1e-5, scale=1.0 / 128),
                  reads=("coltp2",), writes=("coltp2",), tag="orsd")
                P("vector", lambda v: v.reciprocal(coltp2[:, 8:12], coltp2[:, 4:8]),
                  reads=("coltp2",), writes=("coltp2",), tag="orsq")

                def fsc(a):
                    for t8 in range(4):
                        ins = a.activation(attn_st[:, t8 * 128:(t8 + 1) * 128],
                                           pB(3)[:, t8 * 128:(t8 + 1) * 128],
                                           AF.Copy, scale=coltp2[:, 8 + t8:9 + t8])
                    return ins
                P("scalar", fsc, reads=("pB3", "coltp2"),
                  writes=("attn_st",), tag="oscale")
                P("vector", lambda v: v.tensor_tensor(attn_st[:], attn_st[:], onw_bc[:],
                                                      OP.mult),
                  reads=("attn_st", "onw_bc"), writes=("attn_st",), tag="oscl2")
                P("scalar", (lambda s, h=h: s.dma_start(
                    out=scr_attn[h].transpose([1, 0, 2]),
                    in_=attn_st[:].rearrange("p (a j) -> p a j", a=4))),
                  reads=("attn_st",), writes=(f"sattn{h}",), dma="scalq", tag="attnst")
                if h % 4 == 3:
                    drain_bg(1)

            if STOP_SUB == "q":
                return
            refresh_u1a()

            for tp in range(2):
                t8a, t8b = 2 * tp, 2 * tp + 1
                for dp in range(DP):
                    P("sync", (lambda s, dp=dp: s.dma_start(
                        out=afm[dp % 2][:], in_=scr_attn[dp].rearrange("a p j -> (a p) j"),
                        transpose=True)),
                      reads=(f"sattn{dp}",), writes=(f"afm{dp % 2}",), dma="sync", tag="aflip")
                    P("sync", (lambda s, dp=dp: s.dma_start(
                        out=wpbuf[dp % 2][:], in_=scr_wp[dp])),
                      reads=(f"swp{dp}",), writes=(f"wpb{dp % 2}",), dma="sync", tag="wpld")

                    def fmm(t, dp=dp, t8a=t8a, t8b=t8b):
                        for do in range(4):
                            t.matmul(pA(do), afm[dp % 2][:, t8a * 128:(t8a + 1) * 128],
                                     wpbuf[dp % 2][:, do * 512:(do + 1) * 512],
                                     start=(dp == 0), stop=(dp == DP - 1))
                        for do in range(4):
                            ins = t.matmul(pB(do), afm[dp % 2][:, t8b * 128:(t8b + 1) * 128],
                                           wpbuf[dp % 2][:, do * 512:(do + 1) * 512],
                                           start=(dp == 0), stop=(dp == DP - 1))
                        return ins
                    P("tensor", fmm, reads=(f"afm{dp % 2}", f"wpb{dp % 2}"),
                      writes=("pA0", "pA1", "pA2", "pA3", "pB0", "pB1", "pB2", "pB3"),
                      tag=f"projmm{tp}_{dp}")
                for (t8, ps, pkeys) in ((t8a, psA, ("pA0", "pA1", "pA2", "pA3")),
                                        (t8b, psB, ("pB0", "pB1", "pB2", "pB3"))):
                    gt8 = c * 4 + t8
                    P("sync", (lambda s, t8=t8: s.dma_start(
                        out=xa[t8 % 2][:], in_=x_in[c0 + t8 * 128:c0 + (t8 + 1) * 128, :])),
                      writes=(f"xa{t8 % 2}",), dma="sync", tag="xrld")
                    P("vector", (lambda v, ps=ps, t8=t8: v.scalar_tensor_tensor(
                        xa[t8 % 2][:], ps[:], 1.0, xa[t8 % 2][:], OP.mult, OP.add)),
                      reads=pkeys + (f"xa{t8 % 2}",), writes=(f"xa{t8 % 2}",), tag="resadd")
                    P("scalar", (lambda s, t8=t8: s.dma_start(
                        out=scr_x1[c0 + t8 * 128:c0 + (t8 + 1) * 128, :], in_=xa[t8 % 2][:])),
                      reads=(f"xa{t8 % 2}",), writes=(f"sx1_{gt8}",), dma="scalq", tag="x1st")
                    P("scalar", (lambda a, t8=t8: a.activation(
                        f32a[:], xa[t8 % 2][:], AF.Square, accum_out=lnc[:, 1:2])),
                      reads=(f"xa{t8 % 2}",), writes=("f32a", "lnc"), tag="l2sq")

                    def f(v, t8=t8):
                        v.reduce_sum(lnc[:, 0:1], xa[t8 % 2][:], AX.X)
                        v.reduce_max(lnc[:, 8:9], xa[t8 % 2][:], AX.X)
                        return v.tensor_reduce(lnc[:, 9:10], xa[t8 % 2][:], AX.X, OP.min)
                    P("vector", f, reads=(f"xa{t8 % 2}",), writes=("lnc",), tag="l2red")

                    def f(g):
                        g.tensor_copy(lnc[:, 20:21], lnc[:, 0:1])
                        g.tensor_copy(lnc[:, 21:22], lnc[:, 8:9])
                        return g.tensor_copy(lnc[:, 22:23], lnc[:, 9:10])
                    P("gpsimd", f, reads=("lnc",), writes=("lnc",), tag="l2echo")

                    def f(v):
                        v.tensor_scalar(lnc[:, 2:3], lnc[:, 20:21], -1.0 / D, None, OP.mult)
                        return v.tensor_scalar(lnc[:, 3:4], lnc[:, 1:2], 1.0 / D, None, OP.mult)
                    P("vector", f, reads=("lnc",), writes=("lnc",), tag="l2st")
                    P("scalar", lambda a: a.activation(lnc[:, 23:24], lnc[:, 2:3], AF.Square),
                      reads=("lnc",), writes=("lnc",), tag="l2mu2")

                    def f(v):
                        v.tensor_tensor(lnc[:, 4:5], lnc[:, 3:4], lnc[:, 23:24], OP.subtract)
                        v.tensor_tensor(lnc[:, 10:11], lnc[:, 21:22], lnc[:, 2:3], OP.add)
                        return v.scalar_tensor_tensor(lnc[:, 11:12], lnc[:, 2:3], -1.0,
                                                      lnc[:, 22:23], OP.mult, OP.subtract)
                    P("vector", f, reads=("lnc",), writes=("lnc",), tag="l2v")
                    P("scalar", lambda a: a.activation(lnc[:, 5:6], lnc[:, 4:5], AF.Sqrt,
                                                       bias=1e-5),
                      reads=("lnc",), writes=("lnc",), tag="l2sd")
                    def f(v):
                        v.reciprocal(lnc[:, 6:7], lnc[:, 5:6])
                        return v.tensor_tensor(lnc[:, 12:13], lnc[:, 10:11],
                                               lnc[:, 11:12], OP.max)
                    P("vector", f, reads=("lnc",), writes=("lnc",), tag="l2rs")
                    P("gpsimd", lambda g: g.tensor_tensor(lnc[:, 13:14], lnc[:, 6:7],
                                                          lnc[:, 12:13], OP.mult),
                      reads=("lnc",), writes=("lnc",), tag="l2gr")
                    P("vector", lambda v: v.tensor_scalar(lnc[:, 14:15], lnc[:, 13:14],
                                                          1e-5, None, OP.max),
                      reads=("lnc",), writes=("lnc",), tag="l2gx")

                    def f(a, gt8=gt8):
                        a.activation(scl[:, gt8:gt8 + 1], lnc[:, 14:15], AF.Identity,
                                     scale=1.0 / 127)
                        return a.activation(lnc[:, 24:25], lnc[:, 14:15], AF.Identity)
                    P("scalar", f, reads=("lnc",), writes=("lnc", "scl"), tag="l2sc")
                    P("vector", lambda v: v.reciprocal(lnc[:, 15:16], lnc[:, 24:25]),
                      reads=("lnc",), writes=("lnc",), tag="l2ig")
                    P("scalar", (lambda a, gt8=gt8: a.activation(
                        scl[:, 16 + gt8:17 + gt8], lnc[:, 15:16], AF.Identity, scale=127.0)),
                      reads=("lnc",), writes=("scl",), tag="l2sx")
                    P("gpsimd", lambda g: g.tensor_tensor(lnc[:, 16:17], lnc[:, 6:7],
                                                          lnc[:, 15:16], OP.mult),
                      reads=("lnc",), writes=("lnc",), tag="l2ra")
                    P("vector", lambda v: v.tensor_scalar(lnc[:, 17:18], lnc[:, 16:17],
                                                          127.0, None, OP.mult),
                      reads=("lnc",), writes=("lnc",), tag="l2a")
                    P("gpsimd", lambda g: g.tensor_tensor(lnc[:, 18:19], lnc[:, 2:3],
                                                          lnc[:, 17:18], OP.mult),
                      reads=("lnc",), writes=("lnc",), tag="l2b")
                    P("scalar", (lambda a, t8=t8: a.activation(
                        f32a[:], xa[t8 % 2][:], AF.Identity, bias=lnc[:, 18:19],
                        scale=lnc[:, 17:18])),
                      reads=(f"xa{t8 % 2}", "lnc"), writes=("f32a",), tag="l2aff")

                    def f(v):
                        v.tensor_scalar(f32a[:], f32a[:], MAGIC, -MAGIC, OP.add, OP.add)
                        return v.tensor_scalar(ln1tok[:], f32a[:], -128.0, 127.0, OP.max, OP.min)
                    P("vector", f, reads=("f32a",), writes=("f32a", "ln1tok"), tag="l2rnd")
                    P("scalar", (lambda s, t8=t8: s.dma_start(
                        out=scr_x2q[c0 + t8 * 128:c0 + (t8 + 1) * 128, :], in_=ln1tok[:])),
                      reads=("ln1tok",), writes=(f"sx2q_{gt8}",), dma="scalq", tag="x2qst")
                drain_bg(2)

        def emit_ffn_group(g):
            g0 = g * S
            for dp in range(DP):
                P("sync", (lambda s, dp=dp, g0=g0: s.dma_start(
                    out=x2qfm[dp][:], in_=scr_x2q[g0:g0 + S, dp * 128:(dp + 1) * 128],
                    transpose=True)),
                  reads=tuple(f"sx2q_{g * 4 + i}" for i in range(4)),
                  writes=(f"x2qfm{dp}",), dma="sync", tag="x2flip")

            for t8 in range(4):
                P("tensor", (lambda t, t8=t8, g=g: t.transpose(
                    pB(3)[0:1, t8 * 128:(t8 + 1) * 128],
                    scl[:, 16 + g * 4 + t8:17 + g * 4 + t8], identf[:])),
                  reads=("scl", "identf"), writes=("pB3",), tag="sxtr")
            P("scalar", lambda a: a.activation(rowfP[0:1, :], pB(3)[0:1, 0:512], AF.Copy),
              reads=("pB3",), writes=("rowf0",), tag="sxrow")

            def f(v):
                v.tensor_scalar(rowb[0][0:1, :], rowfP[0:1, :], grow[0:1, 4:5], None,
                                OP.mult)
                return v.tensor_scalar(rowb[1][0:1, :], rowfP[0:1, :], grow[0:1, 5:6],
                                       None, OP.mult)
            P("vector", f, reads=("rowf0", "grow"),
              writes=("rowb0", "rowb1"), tag="invscg")
            def f(v, g=g):
                for t8 in range(4):
                    v.tensor_tensor(coltmp[:, t8:t8 + 1], gcol[:, 0:1],
                                    scl[:, g * 4 + t8:g * 4 + t8 + 1], OP.mult)
                    ins = v.tensor_tensor(coltmp[:, 4 + t8:5 + t8], gcol[:, 1:2],
                                          scl[:, g * 4 + t8:g * 4 + t8 + 1], OP.mult)
                return ins
            P("vector", f, reads=("gcol", "scl"), writes=("coltmp",), tag="scgv")

            for ht in range(16):
                hrows = slice(ht * 512, (ht + 1) * 512)
                for dp in range(DP):
                    P("sync", (lambda s, ht=ht, dp=dp: s.dma_start(
                        out=gvsetG[:, dp * 512:(dp + 1) * 512],
                        in_=scr_tg[ht * 512:(ht + 1) * 512, dp * 128:(dp + 1) * 128],
                        transpose=True)),
                      reads=tuple(f"tg_{(ht * 4 + r) * 2 + dp // 8}" for r in range(4)),
                      writes=(f"gvsetG{dp}",), dma="sync", tag="gldG")
                for dp in range(DP):
                    P("sync", (lambda s, ht=ht, dp=dp: s.dma_start(
                        out=gvsetV[:, dp * 512:(dp + 1) * 512],
                        in_=scr_tv[ht * 512:(ht + 1) * 512, dp * 128:(dp + 1) * 128],
                        transpose=True)),
                      reads=tuple(f"tv_{(ht * 4 + r) * 2 + dp // 8}" for r in range(4)),
                      writes=(f"gvsetV{dp}",), dma="sync", tag="gldV")
                P("sync", (lambda s, ht=ht: s.dma_start(
                    out=brow[0][0:1, :],
                    in_=scr_bias[ht * 512:(ht + 1) * 512].rearrange("(o j) -> o j", o=1))),
                  reads=("sbias0",), writes=("brow0",), dma="sync", tag="gbld")
                P("sync", (lambda s, ht=ht: s.dma_start(
                    out=brow[1][0:1, :],
                    in_=scr_bias[HID + ht * 512:HID + (ht + 1) * 512].rearrange("(o j) -> o j", o=1))),
                  reads=("sbias1",), writes=("brow1",), dma="sync", tag="vbld")

                def fmm(t, ht=ht):
                    for t8 in range(4):
                        for dp in range(DP):
                            t.matmul(pA(t8), x2qfm[dp][:, t8 * 128:(t8 + 1) * 128],
                                     gvsetG[:, dp * 512:(dp + 1) * 512],
                                     start=(dp == 0), stop=False)
                        t.matmul(pA(t8), rowb[0][0:1, t8 * 128:(t8 + 1) * 128],
                                 brow[0][0:1, :], start=False, stop=True)
                        for dp in range(DP):
                            t.matmul(pB(t8), x2qfm[dp][:, t8 * 128:(t8 + 1) * 128],
                                     gvsetV[:, dp * 512:(dp + 1) * 512],
                                     start=(dp == 0), stop=False)
                        ins = t.matmul(pB(t8), rowb[1][0:1, t8 * 128:(t8 + 1) * 128],
                                       brow[1][0:1, :], start=False, stop=True)
                    return ins
                P("tensor", fmm,
                  reads=tuple([f"x2qfm{dp}" for dp in range(DP)]
                              + [f"gvsetG{dp}" for dp in range(DP)]
                              + [f"gvsetV{dp}" for dp in range(DP)]
                              + ["brow0", "brow1", "rowb0", "rowb1"]),
                  writes=("pA0", "pA1", "pA2", "pA3", "pB0", "pB1", "pB2", "pB3"),
                  tag=f"ffnmm{g}_{ht}")

                for t8 in range(4):
                    P("scalar", (lambda a, t8=t8: a.activation(
                        ffnew[0][:], pA(t8), AF.Silu, scale=coltmp[:, t8:t8 + 1])),
                      reads=(f"pA{t8}", "coltmp"), writes=("ffnew0",), tag="ffsilu")
                    P("scalar", (lambda a, t8=t8: a.activation(
                        ffnew[1][:], pB(t8), AF.Copy, scale=coltmp[:, 4 + t8:5 + t8])),
                      reads=(f"pB{t8}", "coltmp"), writes=("ffnew1",), tag="ffval")
                    P("vector", (lambda v, t8=t8, ht=ht: v.tensor_tensor(
                        ygrp[t8][:, ht * 512:(ht + 1) * 512], ffnew[0][:], ffnew[1][:], OP.mult)),
                      reads=("ffnew0", "ffnew1"), writes=(f"ygrp{t8}",), tag="ffy")
                if ht % 4 == 3:
                    drain_bg(1)

            for t8 in range(4):
                P("vector", (lambda v, t8=t8: v.reduce_max(
                    coltp2[:, t8:t8 + 1], ygrp[t8][:], AX.X, apply_absolute_value=True)),
                  reads=(f"ygrp{t8}",), writes=("coltp2",), tag="gymax")
            P("gpsimd", lambda g: g.tensor_copy(coltp2[:, 16:20], coltp2[:, 0:4]),
              reads=("coltp2",), writes=("coltp2",), tag="gyecho")

            P("vector", lambda v: v.tensor_scalar(coltp2[:, 0:4], coltp2[:, 16:20],
                                                  1e-5, None, OP.max),
              reads=("coltp2",), writes=("coltp2",), tag="gyclip")

            def f(a):
                a.activation(scl[:, 32:36], coltp2[:, 0:4], AF.Identity, scale=1.0 / 127)
                return a.activation(coltp2[:, 20:24], coltp2[:, 0:4], AF.Identity)
            P("scalar", f, reads=("coltp2",), writes=("coltp2", "scl"), tag="gysc")
            P("vector", lambda v: v.reciprocal(coltp2[:, 24:28], coltp2[:, 20:24]),
              reads=("coltp2",), writes=("coltp2",), tag="gyrec")
            P("gpsimd", lambda g: g.tensor_tensor(coltp2[:, 8:12], coltp2[:, 24:28],
                                                  cst[:, 0:4], OP.mult),
              reads=("coltp2", "cst"), writes=("coltp2",), tag="gys")

            for t8 in range(4):
                for half in range(4):
                    hsl = slice(half * 2048, (half + 1) * 2048)
                    P("scalar", (lambda a, t8=t8, hsl=hsl: a.activation(
                        f32a[:], ygrp[t8][:, hsl], AF.Copy, bias=MAGIC,
                        scale=coltp2[:, 8 + t8:9 + t8])),
                      reads=(f"ygrp{t8}", "coltp2"), writes=("f32a",), tag="yq1")

                    def f(v, t8=t8, hsl=hsl):
                        v.tensor_scalar(f32a[:], f32a[:], -MAGIC, -128.0, OP.add, OP.max)
                        return v.tensor_scalar(ygrp[t8][:, hsl], f32a[:], 127.0, None, OP.min)
                    P("vector", f, reads=("f32a",), writes=(f"ygrp{t8}", "f32a"), tag="yq2")
                P("scalar", (lambda s, t8=t8: s.dma_start(
                    out=scr_y[t8 * 128:(t8 + 1) * 128, :], in_=ygrp[t8][:])),
                  reads=(f"ygrp{t8}",), writes=(f"sy{t8}",), dma="scalq", tag="yst")

            for t8 in range(4):
                P("tensor", (lambda t, t8=t8: t.transpose(
                    pB(3)[0:1, t8 * 128:(t8 + 1) * 128],
                    coltp2[:, 8 + t8:9 + t8], identf[:])),
                  reads=("coltp2", "identf"), writes=("pB3",), tag="sytr")
            P("scalar", lambda a: a.activation(rowfP[0:1, :], pB(3)[0:1, 0:512], AF.Copy),
              reads=("pB3",), writes=("rowf0",), tag="syrow")

            def f(v):
                v.tensor_scalar(rowb[2][0:1, :], rowfP[0:1, :], grow[0:1, 6:7], None,
                                OP.mult)
                for t8 in range(4):
                    ins = v.tensor_tensor(coltmp[:, 8 + t8:9 + t8], gcol[:, 2:3],
                                          scl[:, 32 + t8:33 + t8], OP.mult)
                return ins
            P("vector", f, reads=("rowf0", "grow", "gcol", "scl"),
              writes=("rowb2", "coltmp"), tag="invsco")
            for half in range(2):
                do0 = half * 1024
                P("sync", (lambda s, do0=do0: s.dma_start(
                    out=brow[2][0:1, :],
                    in_=scr_bias[2 * HID + do0:2 * HID + do0 + 512].rearrange("(o j) -> o j", o=1))),
                  reads=("sbias2",), writes=("brow2b",), dma="sync", tag="obld0")
                P("sync", (lambda s, do0=do0: s.dma_start(
                    out=brow[3][0:1, :],
                    in_=scr_bias[2 * HID + do0 + 512:2 * HID + do0 + 1024].rearrange("(o j) -> o j", o=1))),
                  reads=("sbias2",), writes=("brow3b",), dma="sync", tag="obld1")
                for hc in range(64):
                    P("sync", (lambda s, hc=hc, g0=g0: s.dma_start(
                        out=yfm[hc % 4][:],
                        in_=scr_y[:, hc * 128:(hc + 1) * 128], transpose=True)),
                      reads=("sy0", "sy1", "sy2", "sy3"), writes=(f"yfm{hc % 4}",),
                      dma="sync", tag="yflip")
                    P("sync", (lambda s, hc=hc, do0=do0: s.dma_start(
                        out=owbuf[hc % 2][:],
                        in_=scr_to[do0:do0 + 1024, hc * 128:(hc + 1) * 128],
                        transpose=True)),
                      reads=tuple(f"to_{r * 8 + hc // 8}" for r in range(do0 // 128,
                                                                          do0 // 128 + 8)),
                      writes=(f"owb{hc % 2}",), dma="sync", tag="owld")

                    def fmm(t, hc=hc):
                        for t8 in range(4):
                            ps = (pA(t8), pB(t8))
                            for dq in range(2):
                                ins = t.matmul(
                                    (psA if dq == 0 else psB)[:, t8 * 512:(t8 + 1) * 512],
                                    yfm[hc % 4][:, t8 * 128:(t8 + 1) * 128],
                                    owbuf[hc % 2][:, dq * 512:(dq + 1) * 512],
                                    start=(hc == 0), stop=False)
                        return ins
                    P("tensor", fmm, reads=(f"yfm{hc % 4}", f"owb{hc % 2}"),
                      writes=("pA0", "pA1", "pA2", "pA3", "pB0", "pB1", "pB2", "pB3"),
                      tag=f"offmm{g}_{half}_{hc}")

                def fbias(t):
                    for t8 in range(4):
                        for dq in range(2):
                            ins = t.matmul(
                                (psA if dq == 0 else psB)[:, t8 * 512:(t8 + 1) * 512],
                                rowb[2][0:1, t8 * 128:(t8 + 1) * 128],
                                brow[2 + dq][0:1, :],
                                start=False, stop=True)
                    return ins
                P("tensor", fbias, reads=("rowb2", "brow2b", "brow3b"),
                  writes=("pA0", "pA1", "pA2", "pA3", "pB0", "pB1", "pB2", "pB3"),
                  tag=f"obias{g}_{half}")

                for t8 in range(4):
                    gt8 = g * 4 + t8
                    P("sync", (lambda s, t8=t8, g0=g0, do0=do0: s.dma_start(
                        out=xa[t8 % 2][:, 0:1024],
                        in_=scr_x1[g0 + t8 * 128:g0 + (t8 + 1) * 128, do0:do0 + 1024])),
                      reads=(f"sx1_{gt8}",), writes=(f"xa{t8 % 2}",), dma="sync", tag="x1ld")
                    def fev2(v, t8=t8):
                        v.scalar_tensor_tensor(
                            xa[t8 % 2][:, 0:512], psA[:, t8 * 512:(t8 + 1) * 512],
                            coltmp[:, 8 + t8:9 + t8], xa[t8 % 2][:, 0:512], OP.mult, OP.add)
                        return v.scalar_tensor_tensor(
                            xa[t8 % 2][:, 512:1024], psB[:, t8 * 512:(t8 + 1) * 512],
                            coltmp[:, 8 + t8:9 + t8], xa[t8 % 2][:, 512:1024], OP.mult, OP.add)
                    P("vector", fev2,
                      reads=(f"pA{t8}", f"pB{t8}", "coltmp", f"xa{t8 % 2}"),
                      writes=(f"xa{t8 % 2}",), tag="oev")
                    P("scalar", (lambda s, t8=t8, g0=g0, do0=do0: s.dma_start(
                        out=out_hbm[g0 + t8 * 128:g0 + (t8 + 1) * 128, do0:do0 + 1024],
                        in_=xa[t8 % 2][:, 0:1024])),
                      reads=(f"xa{t8 % 2}",), writes=(f"OUT_{g}_{half}_{t8}",),
                      dma="scalq", tag="outst")
                drain_bg(2)

        if STOP_AFTER != "prologue":
            for c in range(NCH):
                emit_chunk(c)
                if STOP_AFTER == f"chunk{c}":
                    break
            if STOP_AFTER is None or STOP_AFTER == "ffn":
                drain_bg(10**9)
                P("vector", lambda v: v.engine_nop(), reads=(), writes=POOL_KEYS,
                  tag="phase_barrier")
                for g in range(NGRP):
                    emit_ffn_group(g)

        if DEBUG_TAPS:
            allkeys = set()
            for st in prog.steps:
                allkeys.update(st.writes)
            for name in DEBUG_TAPS:
                src = scr_map[name]
                P("gpsimd", (lambda g, src=src, name=name: g.dma_start(
                    out=dbg_outs[name], in_=src)),
                  reads=tuple(allkeys), writes=(f"DBG_{name}",), dma="gpq", tag="dbg")

        out_keys = [k for st in prog.steps for k in st.writes
                    if k.startswith("OUT_") or k.startswith("DBG_")]
        P("gpsimd", lambda g: g.nop(), reads=tuple(set(out_keys)), tag="final")

        emit_all(nc, prog, ctx)

    return nc, dbg_outs


DMA_POOL = 16


def emit_all(nc, prog, ctx):
    deps = prog.resolve()
    sems = {}
    for e in ("tensor", "vector", "scalar", "gpsimd"):
        sems[e] = [ctx.enter_context(nc.semaphore(f"se_{e}")), 0]
    pools = {}
    for q in ("sync", "scalq", "gpq"):
        pools[q] = [[ctx.enter_context(nc.semaphore(f"sq_{q}{i}")), 0]
                    for i in range(DMA_POOL)]
    qidx = {q: 0 for q in pools}

    selfwait = {}
    for st in prog.steps:
        if st.dma is not None:
            pool = pools[st.dma]
            i = qidx[st.dma] % DMA_POOL
            qidx[st.dma] += 1
            sem = pool[i]
            if sem[1] > 0:
                selfwait[st.idx] = (sem[0], sem[1])
            sem[1] += 16
            st.sem, st.val = sem[0], sem[1]
            st.need_inc = True
        elif st.need_inc:
            se = sems[st.eng]
            se[1] += 1
            st.sem, st.val = se[0], se[1]

    warm = ctx.enter_context(nc.sbuf_tensor("warmup", [128, 16], BF16))
    swarm = ctx.enter_context(nc.semaphore("swarm"))
    block = ctx.enter_context(nc.Block())
    per_eng = {e: [] for e in ENGINES}
    for st in prog.steps:
        eng = DMA_QUEUE_ENGINE[st.dma] if st.dma else st.eng
        per_eng[eng].append(st)

    for eng_name in ENGINES:
        steps = per_eng[eng_name]
        if not steps:
            continue

        def body(eng, steps=steps, eng_name=eng_name):
            if eng_name == "gpsimd":
                eng.memset(warm[:], 0.0).then_inc(swarm, 1)
            elif eng_name == "vector":
                eng.memset(warm[0:128, 0:1], 0.0)
            elif eng_name == "scalar":
                eng.memzero(warm[0:128, 2:4])
            elif eng_name == "tensor":
                eng.wait_ge(swarm, 1)
                eng.ldweights(warm[:, 0:16])
            waited = {}
            for st in steps:
                for d in deps[st.idx]:
                    if d.sem is None:
                        continue
                    key = id(d.sem)
                    if waited.get(key, -1) >= d.val:
                        continue
                    eng.wait_ge(d.sem, d.val)
                    waited[key] = d.val
                if st.idx in selfwait:
                    ssem, sval = selfwait[st.idx]
                    if waited.get(id(ssem), -1) < sval:
                        eng.wait_ge(ssem, sval)
                        waited[id(ssem)] = sval
                ins = st.fn(eng)
                if st.need_inc:
                    assert ins is not None, f"step {st.tag} needs inc, returned None"
                    if st.dma is None and eng_name in ("vector", "scalar", "gpsimd"):
                        if eng_name == "vector":
                            eng.memset(warm[0:128, 4:5], 0.0).then_inc(st.sem, 1)
                        elif eng_name == "scalar":
                            eng.memzero(warm[0:128, 6:8]).then_inc(st.sem, 1)
                        else:
                            eng.memset(warm[0:128, 8:9], 0.0).then_inc(st.sem, 1)
                    else:
                        ins.then_inc(st.sem, 16 if st.dma else 1)
        getattr(block, eng_name)(body)


def kernel(**inputs):
    x = np.ascontiguousarray(np.asarray(inputs["L0_attn_x"], dtype=np.float32))
    base = {
        "w0": np.ascontiguousarray(inputs["L0_attn_w0"]),
        "w1": np.ascontiguousarray(inputs["L0_attn_w1"]),
        "w2": np.ascontiguousarray(inputs["L0_attn_w2"]),
        "wqkv": np.ascontiguousarray(inputs["L0_attn_W_qkv"]),
        "wlr": np.ascontiguousarray(inputs["L0_attn_W_lr"]),
        "blr": np.ascontiguousarray(inputs["L0_attn_b_lr"]),
        "wproj": np.ascontiguousarray(inputs["L0_attn_W_proj"]),
        "onw": np.ascontiguousarray(inputs["L0_attn_onorm_w"]),
        "gw": np.ascontiguousarray(inputs["L0_ffn_gate_w"]),
        "gb": np.ascontiguousarray(inputs["L0_ffn_gate_b"]),
        "vw": np.ascontiguousarray(inputs["L0_ffn_val_w"]),
        "vb": np.ascontiguousarray(inputs["L0_ffn_val_b"]),
        "ow": np.ascontiguousarray(inputs["L0_ffn_out_w"]),
        "ob": np.ascontiguousarray(inputs["L0_ffn_out_b"]),
    }
    nc, _ = build_nc()
    in_maps = [dict(base, x=np.ascontiguousarray(x[i])) for i in range(8)]
    res = run_bass_kernel_spmd(nc, in_maps, core_ids=list(range(8)))
    return np.stack([r["out"] for r in res.results], axis=0)


# version 02
# speedup vs baseline: 1.1527x; 1.1527x over previous
import math
import numpy as np
import ml_dtypes
from contextlib import ExitStack

import concourse.bass as bass
import concourse.mybir as mybir
from concourse.bass_utils import run_bass_kernel_spmd

F32 = mybir.dt.float32
BF16 = mybir.dt.bfloat16
OP = mybir.AluOpType
AF = mybir.ActivationFunctionType
AX = mybir.AxisListType

T = 2048
D = 2048
S = 512
H = 16
JQ = 6144
HID = 8192
DP = D // 128
NCH = T // S
NGRP = 4

MAGIC = 12582912.0
BASE_LR_INV = float(np.log(np.expm1(0.01)))
NS_A, NS_B, NS_C = 3.4445, -4.775, 2.0315

STOP_AFTER = None
STOP_SUB = None
DEBUG_TAPS = []

DMA_QUEUE_ENGINE = {"sync": "sync", "scalq": "scalar", "gpq": "gpsimd"}
ENGINES = ("sync", "tensor", "vector", "scalar", "gpsimd")


class Step:
    __slots__ = ("eng", "fn", "reads", "writes", "dma", "idx", "need_inc",
                 "sem", "val", "tag")

    def __init__(self, eng, fn, reads, writes, dma, idx, tag):
        self.eng, self.fn = eng, fn
        self.reads, self.writes = tuple(reads), tuple(writes)
        self.dma, self.idx, self.tag = dma, idx, tag
        self.need_inc = False
        self.sem = None
        self.val = None


class Prog:
    def __init__(self):
        self.steps = []

    def add(self, eng, fn, reads=(), writes=(), dma=None, tag=""):
        st = Step(eng, fn, reads, writes, dma, len(self.steps), tag)
        self.steps.append(st)
        return st

    def resolve(self):
        deps = [[] for _ in self.steps]
        lw = {}
        rd = {}
        for st in self.steps:
            eng = DMA_QUEUE_ENGINE[st.dma] if st.dma else st.eng
            dset = {}
            for r in st.reads:
                w = lw.get(r)
                if w is not None:
                    dset[w.idx] = w
            for wres in st.writes:
                w = lw.get(wres)
                if w is not None:
                    dset[w.idx] = w
                for r in rd.get(wres, ()):
                    dset[r.idx] = r
            for d in dset.values():
                if d is st:
                    continue
                d_eng = DMA_QUEUE_ENGINE[d.dma] if d.dma else d.eng
                if d.dma is None and st.dma is None and d_eng == eng:
                    continue
                d.need_inc = True
                deps[st.idx].append(d)
            for r in st.reads:
                rd.setdefault(r, []).append(st)
            for wres in st.writes:
                lw[wres] = st
                rd[wres] = []
        return deps


def build_nc():
    nc = bass.Bass("TRN2", target_bir_lowering=False, debug=False,
                   detect_race_conditions=False)

    x_in = nc.dram_tensor("x", [T, D], F32, kind="ExternalInput").ap()
    w0_in = nc.dram_tensor("w0", [H, 128, 128], F32, kind="ExternalInput").ap()
    w1_in = nc.dram_tensor("w1", [H, 128, 128], F32, kind="ExternalInput").ap()
    w2_in = nc.dram_tensor("w2", [H, 128, 128], F32, kind="ExternalInput").ap()
    wqkv_in = nc.dram_tensor("wqkv", [D, JQ], F32, kind="ExternalInput").ap()
    wlr_in = nc.dram_tensor("wlr", [D, 48], F32, kind="ExternalInput").ap()
    blr_in = nc.dram_tensor("blr", [48], F32, kind="ExternalInput").ap()
    wproj_in = nc.dram_tensor("wproj", [D, D], F32, kind="ExternalInput").ap()
    onw_in = nc.dram_tensor("onw", [128], F32, kind="ExternalInput").ap()
    gw_in = nc.dram_tensor("gw", [HID, D], F32, kind="ExternalInput").ap()
    gb_in = nc.dram_tensor("gb", [HID], F32, kind="ExternalInput").ap()
    vw_in = nc.dram_tensor("vw", [HID, D], F32, kind="ExternalInput").ap()
    vb_in = nc.dram_tensor("vb", [HID], F32, kind="ExternalInput").ap()
    ow_in = nc.dram_tensor("ow", [D, HID], F32, kind="ExternalInput").ap()
    ob_in = nc.dram_tensor("ob", [D], F32, kind="ExternalInput").ap()

    out_hbm = nc.dram_tensor("out", [T, D], F32, kind="ExternalOutput").ap()

    scr_wq = nc.dram_tensor("scr_wq", [48, DP, 128, 128], BF16).ap()
    scr_wp = nc.dram_tensor("scr_wp", [DP, 128, D], BF16).ap()
    scr_ln1 = nc.dram_tensor("scr_ln1", [S, D], BF16).ap()
    scr_qkv = nc.dram_tensor("scr_qkv", [JQ, S], BF16).ap()
    scr_attn = nc.dram_tensor("scr_attn", [H, 4, 128, 128], BF16).ap()
    scr_x1 = nc.dram_tensor("scr_x1", [T, D], F32).ap()
    scr_x2q = nc.dram_tensor("scr_x2q", [T, D], BF16).ap()
    scr_tg = nc.dram_tensor("scr_tg", [HID, D], BF16).ap()
    scr_tv = nc.dram_tensor("scr_tv", [HID, D], BF16).ap()
    scr_to = nc.dram_tensor("scr_to", [D, HID], BF16).ap()
    scr_y = nc.dram_tensor("scr_y", [S, HID], BF16).ap()
    scr_dbgc = nc.dram_tensor("scr_dbgc", [8, 128, 16], F32).ap()
    scr_bias = nc.dram_tensor("scr_bias", [2 * HID + D], BF16).ap()

    scr_map = {"scr_wq": scr_wq, "scr_wp": scr_wp, "scr_ln1": scr_ln1,
               "scr_qkv": scr_qkv, "scr_attn": scr_attn, "scr_x1": scr_x1,
               "scr_x2q": scr_x2q, "scr_tg": scr_tg, "scr_tv": scr_tv,
               "scr_to": scr_to, "scr_y": scr_y, "scr_dbgc": scr_dbgc}
    dbg_outs = {}
    for name in DEBUG_TAPS:
        src = scr_map[name]
        dbg_outs[name] = nc.dram_tensor("dbg_" + name, list(src.shape),
                                        src.dtype, kind="ExternalOutput").ap()

    prog = Prog()
    P = prog.add
    ctx = ExitStack()
    with ctx:
        sb = lambda name, shape, dt: ctx.enter_context(nc.sbuf_tensor(name, shape, dt))

        identf = sb("identf", [128, 128], F32)
        ones_colb = sb("ones_colb", [128, 1], BF16)
        ones_colf = sb("ones_colf", [128, 1], F32)
        ones_rowb = sb("ones_rowb", [1, 128], BF16)
        ones_rowf = sb("ones_rowf", [1, 128], F32)
        onw_rowt = sb("onw_rowt", [1, 128], F32)
        onw_bc = sb("onw_bc", [128, 512], BF16)
        blr2 = sb("blr2", [128, 1], F32)
        RL = sb("RL", [128, 512], F32)
        SCLR = sb("SCLR", [128, 256], F32)
        scl = sb("scl", [128, 64], F32)
        lnc = sb("lnc", [128, 32], F32)
        grow = sb("grow", [1, 16], F32)
        gcol = sb("gcol", [128, 8], F32)
        coltmp = sb("coltmp", [128, 32], F32)
        coltp2 = sb("coltp2", [128, 32], F32)
        cst = sb("cst", [128, 4], F32)
        epsc = sb("epsc", [128, 4], F32)
        nc.const_aps.register(F32, 1e-5, epsc[:, 0:1])
        nc.const_aps.register(F32, 1e-10, epsc[:, 1:2])
        nc.const_aps.register(F32, 1e-14, epsc[:, 2:3])
        W0c = sb("W0c", [128, 2048], F32)
        W1c = sb("W1c", [128, 2048], F32)
        W2c = sb("W2c", [128, 2048], F32)
        U0 = sb("U0", [128, 2048], BF16)
        U2 = sb("U2", [128, 2048], BF16)
        U1x = sb("U1x", [128, 2048], BF16)
        U1a = U1x
        U1b = U1x
        WN = sb("WN", [128, 48], F32)
        wlr_sb = sb("wlr_sb", [128, DP * 48], BF16)

        psA = ctx.enter_context(nc.psum_tensor("psA", [128, 2048], F32))
        psB = ctx.enter_context(nc.psum_tensor("psB", [128, 2048], F32))

        def pA(i):
            return psA[:, i * 512:(i + 1) * 512]

        def pB(i):
            return psB[:, i * 512:(i + 1) * 512]

        PO = [sb(f"PO{i}", [128, 8192], BF16) for i in range(8)]
        ktok = [PO[0][:, i * 2048:(i + 1) * 2048] for i in range(4)]
        vtok = [PO[1][:, i * 2048:(i + 1) * 2048] for i in range(4)]
        ln1_fm = [PO[2][:, i * 512:(i + 1) * 512] for i in range(DP)]
        Xns = [PO[3][:, i * 2048:(i + 1) * 2048] for i in range(3)]
        Tt_ns = PO[3][:, 3 * 2048:4 * 2048]
        A_ns = PO[4][:, 0:2048]
        B_ns = PO[4][:, 2048:4096]
        ln1tok = PO[4][:, 4096:6144]
        wqbuf = [PO[4][:, 6144:8192], PO[5][:, 0:2048]]
        wpbuf = wqbuf
        dgba_h = PO[5][:, 2048:2560]
        dhbm_h = PO[5][:, 2560:3072]
        hidn_h = PO[5][:, 3072:3584]
        attn_st = PO[5][:, 3584:4096]
        mfm = PO[5][:, 4096:4608]
        afm = [PO[5][:, 4608:5120], PO[5][:, 5120:5632]]
        qkst = [PO[5][:, 5632:6144], PO[5][:, 6144:6656]]
        kqst = qkst
        vst = [PO[5][:, 6656:7168], PO[5][:, 7168:7680]]
        sqst = PO[5][:, 7680:8192]
        invnE = PO[6][:, 0:512]
        ew = [PO[6][:, 512 + i * 512:512 + (i + 1) * 512] for i in range(6)]
        tstage = [PO[6][:, 3584 + i * 512:3584 + (i + 1) * 512] for i in range(4)]
        f32t = [PO[7][:, i * 1024:(i + 1) * 1024].bitcast(F32) for i in range(4)]
        gstage = [PO[7][:, 4096 + i * 1024:4096 + (i + 1) * 1024].bitcast(F32)
                  for i in range(4)]
        rowb = ([PO[6][0:1, 5632 + i * 512:5632 + (i + 1) * 512] for i in range(3)])
        ygrp = [PO[i][:, 0:8192] for i in range(4)]
        gvsetG = PO[4][:, 0:8192]
        gvsetV = PO[5][:, 0:8192]
        x2qfm = [PO[6][:, i * 512:(i + 1) * 512] for i in range(DP)]
        yfm = [PO[7][:, i * 512:(i + 1) * 512] for i in range(4)]
        owbuf = [PO[7][:, 2048 + i * 1024:2048 + (i + 1) * 1024] for i in range(2)]
        ffnew = [PO[7][:, 4096 + i * 512:4096 + (i + 1) * 512] for i in range(2)]
        brow = [PO[7][0:1, 5120 + i * 512:5120 + (i + 1) * 512] for i in range(4)]
        rowfP = PO[7][0:1, 7168:8192].bitcast(F32)
        POOL_KEYS = tuple(
            [f"ktok{i}" for i in range(4)] + [f"vtok{i}" for i in range(4)]
            + [f"lnfm{i}" for i in range(DP)] + [f"Xns{i}" for i in range(3)]
            + ["Tt", "A_ns", "B_ns", "ln1tok", "wqb0", "wqb1", "wpb0", "wpb1",
               "dgba_h", "dhbm_h", "hidn_h", "attn_st", "mfm", "afm0", "afm1",
               "qkst0", "qkst1", "kqst0", "kqst1", "vst0", "vst1", "sqst", "invnE"]
            + [f"ygrp{i}" for i in range(4)] + [f"gvsetG{i}" for i in range(DP)]
            + [f"gvsetV{i}" for i in range(DP)] + [f"x2qfm{i}" for i in range(DP)]
            + [f"yfm{i}" for i in range(4)] + ["owb0", "owb1"]
            + [f"ew{i}" for i in range(6)]
            + [f"tsg{i}" for i in range(4)] + [f"f32t{i}" for i in range(4)]
            + [f"gsg{i}" for i in range(4)] + [
               "brow0", "brow1", "brow2", "brow3", "rowf0", "ffnew0", "ffnew1",
               "rowb0", "rowb1", "rowb2"])
        xa = [sb(f"xa{i}", [128, 2048], F32) for i in range(2)]
        winit = [xa[0][:, 0:128], xa[1][:, 0:128]]
        f32a = sb("f32a", [128, 2048], F32)
        f32c = sb("f32c", [128, 512], F32)
        gacc = sb("gacc", [128, 768], F32)

        def seq(tag, *fns):
            pass

        def f_id(g):
            g.memset(identf[:], 0.0)
            g.affine_select(out=identf[:], in_=identf[:], compare_op=OP.not_equal,
                            fill=1.0, base=0, pattern=[[-1, 128]], channel_multiplier=1)
            g.memset(ones_colb[:], 1.0)
            g.memset(ones_colf[:], 1.0)
            g.memset(ones_rowb[:], 1.0)
            g.memset(ones_rowf[:], 1.0)
            g.memset(RL[:], 0.0)
            g.memset(cst[:], 127.0)
            g.memset(epsc[:, 0:1], 1e-5)
            g.memset(epsc[:, 1:2], 1e-10)
            g.memset(epsc[:, 2:3], 1e-14)
            g.memset(coltmp[:], 0.0)
            g.memset(coltp2[:], 0.0)
            g.memset(lnc[:], 0.0)
            return g.memset(f32c[:], 0.0)
        P("gpsimd", f_id, writes=("identf", "ones", "RL0", "f32c",
                                  "coltmp", "coltp2", "lnc", "cst"), tag="ident")

        for jb in range(48):
            P("gpsimd", (lambda g, jb=jb: g.dma_start(
                out=scr_wq[jb],
                in_=wqkv_in[:, jb * 128:(jb + 1) * 128].rearrange("(a p) j -> a p j", p=128))),
              writes=(f"swq{jb}",), dma="gpq", tag="wqcast")
        for db in range(DP):
            P("gpsimd", (lambda g, db=db: g.dma_start(
                out=scr_wp[db], in_=wproj_in[db * 128:(db + 1) * 128, :])),
              writes=(f"swp{db}",), dma="gpq", tag="wpcast")
        for db in range(DP):
            P("gpsimd", (lambda g, db=db: g.dma_start(
                out=wlr_sb[:, db * 48:(db + 1) * 48],
                in_=wlr_in[db * 128:(db + 1) * 128, :])),
              writes=("wlr_sb",), dma="gpq", tag="wlrcast")
        P("gpsimd", lambda g: g.dma_start(out=scr_bias[0:HID], in_=gb_in),
          writes=("sbias0",), dma="gpq", tag="bcast0")
        P("gpsimd", lambda g: g.dma_start(out=scr_bias[HID:2 * HID], in_=vb_in),
          writes=("sbias1",), dma="gpq", tag="bcast1")
        P("gpsimd", lambda g: g.dma_start(out=scr_bias[2 * HID:], in_=ob_in),
          writes=("sbias2",), dma="gpq", tag="bcast2")

        P("sync", lambda s: s.dma_start(out=blr2[0:48, 0:1],
                                        in_=blr_in.rearrange("(p o) -> p o", o=1)),
          writes=("blr2",), dma="sync", tag="blr")
        P("vector", lambda v: v.tensor_scalar(blr2[0:48, 0:1], blr2[0:48, 0:1],
                                              BASE_LR_INV, None, OP.add),
          reads=("blr2",), writes=("blr2",), tag="blradd")
        P("sync", lambda s: s.dma_start(out=onw_rowt[0:1, :],
                                        in_=onw_in.rearrange("(o j) -> o j", o=1)),
          writes=("onw_rowt",), dma="sync", tag="onwld")
        P("tensor", lambda t: t.matmul(psB[:, 0:128], ones_rowf[0:1, :], onw_rowt[0:1, :],
                                       start=True, stop=True),
          reads=("ones", "onw_rowt"), writes=("pB0",), tag="onwmm")

        def f_onwbc(v):
            for i in range(4):
                ins = v.tensor_copy(onw_bc[:, i * 128:(i + 1) * 128], psB[:, 0:128])
            return ins
        P("vector", f_onwbc, reads=("pB0",), writes=("onw_bc",), tag="onwbc")

        for wi, (w_in, wkey) in enumerate(((w0_in, "W0c"), (w1_in, "W1c"), (w2_in, "W2c"))):
            WC = (W0c, W1c, W2c)[wi]
            for h in range(H):
                idx = wi * H + h
                P("sync", (lambda s, w_in=w_in, idx=idx: s.dma_start(
                    out=winit[idx % 2][:], in_=w_in[idx % H])),
                  writes=(f"xa{idx % 2}",), dma="sync", tag="wld")
                P("tensor", (lambda t, idx=idx: t.transpose(
                    pA(idx % 2)[:, 0:128], winit[idx % 2][:], identf[:])),
                  reads=(f"xa{idx % 2}", "identf"), writes=(f"pA{idx % 2}",), tag="wtr")
                P("vector", (lambda v, WC=WC, h=h, idx=idx: v.tensor_copy(
                    WC[:, h * 128:(h + 1) * 128], pA(idx % 2)[:, 0:128])),
                  reads=(f"pA{idx % 2}",), writes=(wkey,), tag="wev")

        for wi, (WC, wkey) in enumerate(((W0c, "W0c"), (W1c, "W1c"), (W2c, "W2c"))):
            P("scalar", (lambda a, WC=WC: a.activation(f32a[:], WC[:], AF.Square)),
              reads=(wkey,), writes=("f32a",), tag="wsq")
            P("vector", (lambda v, wi=wi: v.tensor_reduce(
                WN[:, wi * 16:(wi + 1) * 16],
                f32a[:].rearrange("p (a b) -> p a b", b=128), AX.X, OP.add)),
              reads=("f32a",), writes=("WN",), tag="wnred")
        P("scalar", lambda a: a.activation(WN[:, 0:48], WN[:, 0:48], AF.Sqrt),
          reads=("WN",), writes=("WN",), tag="wnsqrt")

        def refresh_u1a():
            P("vector", lambda v: v.tensor_copy(U1a[:], W1c[:]),
              reads=("W1c",), writes=("U1",), tag="u1a")

        def emit_usage():
            for (WC, wkey, U, ukey) in ((W0c, "W0c", U0, "U0"), (W2c, "W2c", U2, "U2"),
                                        (W1c, "W1c", U1b, "U1")):
                for hq in range(4):
                    def ftr(t, WC=WC, hq=hq):
                        for i in range(4):
                            h = hq * 4 + i
                            ins = t.transpose(psB[:, h * 128 - hq * 512:(h + 1) * 128 - hq * 512]
                                              if False else pB(hq)[:, i * 128:(i + 1) * 128],
                                              WC[:, h * 128:(h + 1) * 128], identf[:])
                        return ins
                    P("tensor", ftr, reads=(wkey, "identf"), writes=(f"pB{hq}",), tag="utr")
                def fev(a, U=U):
                    return a.activation(U[:], psB[:], AF.Copy)
                P("scalar", fev, reads=("pB0", "pB1", "pB2", "pB3"), writes=(ukey,), tag="uev")
        emit_usage()
        refresh_u1a()

        bgq = []

        NGT = HID * D // (128 * 512)

        def tile_view(w_hbm):
            r, c = w_hbm.shape
            nb = c // 512
            wv = w_hbm.rearrange("(a p) (b d) -> a p b d", p=128, d=512)
            return lambda i: wv[i // nb][:, i % nb, :]

        def make_gamma_tile(w_hbm, wi, tag):
            tv = tile_view(w_hbm)
            def emit(i):
                P("sync", (lambda s, i=i: s.dma_start(out=gstage[i % 4][:], in_=tv(i))),
                  writes=(f"gsg{i % 4}",), dma="sync", tag=f"gld{tag}")
                P("scalar", (lambda a, i=i, col=wi * NGT + i: a.activation(
                    f32t[i % 4][:], gstage[i % 4][:], AF.Abs,
                    accum_out=gacc[:, col:col + 1])),
                  reads=(f"gsg{i % 4}",), writes=(f"f32t{i % 4}", "gacc"),
                  tag=f"gab{tag}")
            return emit

        def make_tern_tile(w_hbm, scr, wi, tag):
            tv = tile_view(w_hbm)
            sv = tile_view(scr)
            def emit(i):
                P("sync", (lambda s, i=i: s.dma_start(out=gstage[i % 4][:], in_=tv(i))),
                  writes=(f"gsg{i % 4}",), dma="sync", tag=f"tld{tag}")
                P("scalar", (lambda a, i=i: a.activation(
                    f32t[i % 4][:], gstage[i % 4][:], AF.Copy, bias=MAGIC,
                    scale=gcol[:, 4 + wi:5 + wi])),
                  reads=(f"gsg{i % 4}", "gcol"), writes=(f"f32t{i % 4}",),
                  tag=f"tq1{tag}")
                def f(v, i=i):
                    v.tensor_scalar(f32t[i % 4][:], f32t[i % 4][:], -MAGIC, -1.0,
                                    OP.add, OP.max)
                    return v.tensor_scalar(tstage[i % 4][:], f32t[i % 4][:], 1.0,
                                           None, OP.min)
                P("vector", f, reads=(f"f32t{i % 4}",),
                  writes=(f"tsg{i % 4}", f"f32t{i % 4}"), tag=f"tq2{tag}")
                P("scalar", (lambda s, i=i: s.dma_start(out=sv(i), in_=tstage[i % 4][:])),
                  reads=(f"tsg{i % 4}",), writes=(f"{tag}_{i}",), dma="scalq",
                  tag=f"tst{tag}")
            return emit

        for i in range(NGT):
            bgq.append((make_gamma_tile(gw_in, 0, "g"), i))
        for i in range(NGT):
            bgq.append((make_gamma_tile(vw_in, 1, "v"), i))
        for i in range(NGT):
            bgq.append((make_gamma_tile(ow_in, 2, "o"), i))
        GAMMA_FIN = "GAMMA_FIN"
        bgq.append((GAMMA_FIN, 0))
        for i in range(NGT):
            bgq.append((make_tern_tile(gw_in, scr_tg, 0, "tg"), i))
        for i in range(NGT):
            bgq.append((make_tern_tile(vw_in, scr_tv, 1, "tv"), i))
        for i in range(NGT):
            bgq.append((make_tern_tile(ow_in, scr_to, 2, "to"), i))

        def emit_gamma_final():
            for wi in range(3):
                P("vector", (lambda v, wi=wi: v.tensor_reduce(
                    coltmp[:, wi:wi + 1], gacc[:, wi * NGT:(wi + 1) * NGT], AX.X, OP.add)),
                  reads=("gacc",), writes=("coltmp",), tag="gred")
            P("tensor", lambda t: t.matmul(pB(3)[0:1, 0:8], ones_colf[:, 0:1],
                                           coltmp[:, 0:8], start=True, stop=True),
              reads=("coltmp", "ones"), writes=("pB3",), tag="gmm")

            def f(v):
                return v.tensor_scalar(grow[0:1, 0:8], pB(3)[0:1, 0:8], 1.0 / (HID * D), 1e-5,
                                       OP.mult, OP.max)
            P("vector", f, reads=("pB3",), writes=("grow",), tag="gfin")
            P("tensor", lambda t: t.matmul(pB(3)[:, 0:8], ones_rowf[0:1, :], grow[0:1, 0:8],
                                           start=True, stop=True),
              reads=("grow", "ones"), writes=("pB3",), tag="gbc")

            P("scalar", lambda a: a.activation(gcol[:, 0:8], pB(3)[:, 0:8], AF.Copy),
              reads=("pB3",), writes=("gcol",), tag="gcp")
            P("vector", lambda v: v.reciprocal(gcol[:, 4:7], gcol[:, 0:3]),
              reads=("gcol",), writes=("gcol",), tag="ginv")
            P("scalar", lambda a: a.activation(grow[0:1, 8:11], grow[0:1, 0:3], AF.Copy),
              reads=("grow",), writes=("grow",), tag="gbnc")
            P("vector", lambda v: v.reciprocal(grow[0:1, 4:7], grow[0:1, 8:11]),
              reads=("grow",), writes=("grow",), tag="grinv")

        bg_pos = [0]

        def drain_bg(n):
            k = 0
            while k < n and bg_pos[0] < len(bgq):
                emitfn, i = bgq[bg_pos[0]]
                bg_pos[0] += 1
                if emitfn == GAMMA_FIN:
                    emit_gamma_final()
                else:
                    emitfn(i)
                    k += 1

        def emit_chunk(c):
            c0 = c * S
            for t8 in range(4):
                P("sync", (lambda s, t8=t8: s.dma_start(
                    out=xa[t8 % 2][:], in_=x_in[c0 + t8 * 128:c0 + (t8 + 1) * 128, :])),
                  writes=(f"xa{t8 % 2}",), dma="sync", tag="xld")
                P("scalar", (lambda a, t8=t8: a.activation(
                    f32a[:], xa[t8 % 2][:], AF.Square, accum_out=lnc[:, 1:2])),
                  reads=(f"xa{t8 % 2}",), writes=("f32a", "lnc"), tag="lnsq")
                P("vector", (lambda v, t8=t8: v.reduce_sum(lnc[:, 0:1], xa[t8 % 2][:], AX.X)),
                  reads=(f"xa{t8 % 2}",), writes=("lnc",), tag="lnsum")
                P("gpsimd", lambda g: g.tensor_copy(lnc[:, 20:21], lnc[:, 0:1]),
                  reads=("lnc",), writes=("lnc",), tag="lnecho")

                def f(v):
                    v.tensor_scalar(lnc[:, 2:3], lnc[:, 20:21], -1.0 / D, None, OP.mult)
                    return v.tensor_scalar(lnc[:, 3:4], lnc[:, 1:2], 1.0 / D, None, OP.mult)
                P("vector", f, reads=("lnc",), writes=("lnc",), tag="lnst")
                P("scalar", lambda a: a.activation(lnc[:, 21:22], lnc[:, 2:3], AF.Square),
                  reads=("lnc",), writes=("lnc",), tag="lnmu2")
                P("vector", lambda v: v.tensor_tensor(lnc[:, 4:5], lnc[:, 3:4],
                                                      lnc[:, 21:22], OP.subtract),
                  reads=("lnc",), writes=("lnc",), tag="lnv")
                P("scalar", lambda a: a.activation(lnc[:, 5:6], lnc[:, 4:5], AF.Sqrt,
                                                   bias=1e-5),
                  reads=("lnc",), writes=("lnc",), tag="lnsd")
                P("vector", lambda v: v.reciprocal(lnc[:, 6:7], lnc[:, 5:6]),
                  reads=("lnc",), writes=("lnc",), tag="lnrs")
                P("gpsimd", lambda g: g.tensor_tensor(lnc[:, 7:8], lnc[:, 2:3],
                                                      lnc[:, 6:7], OP.mult),
                  reads=("lnc",), writes=("lnc",), tag="lnb")
                P("scalar", (lambda a, t8=t8: a.activation(
                    ln1tok[:], xa[t8 % 2][:], AF.Identity,
                    bias=lnc[:, 7:8], scale=lnc[:, 6:7])),
                  reads=(f"xa{t8 % 2}", "lnc"), writes=("ln1tok",), tag="lnout")
                P("scalar", (lambda s, t8=t8: s.dma_start(
                    out=scr_ln1[t8 * 128:(t8 + 1) * 128, :], in_=ln1tok[:])),
                  reads=("ln1tok",), writes=(f"sln{t8}",), dma="scalq", tag="lnst8")
                if DEBUG_TAPS and c == 0:
                    P("scalar", (lambda s, t8=t8: s.dma_start(
                        out=scr_dbgc[t8], in_=lnc[:, 0:16])),
                      reads=("lnc",), writes=(f"DBGC{t8}",), dma="scalq", tag="dbgc")

            if STOP_SUB == "ln":
                return
            for dp in range(DP):
                P("sync", (lambda s, dp=dp: s.dma_start(
                    out=ln1_fm[dp][:], in_=scr_ln1[:, dp * 128:(dp + 1) * 128],
                    transpose=True)),
                  reads=("sln0", "sln1", "sln2", "sln3"), writes=(f"lnfm{dp}",),
                  dma="sync", tag="lnflip")

            jb_order = list(range(16, 32)) + list(range(0, 16)) + list(range(32, 48))
            for oi, jb in enumerate(jb_order):
                P("sync", (lambda s, jb=jb, oi=oi: s.dma_start(
                    out=wqbuf[oi % 2][:].rearrange("p (a j) -> p a j", a=DP),
                    in_=scr_wq[jb].transpose([1, 0, 2]))),
                  reads=(f"swq{jb}",), writes=(f"wqb{oi % 2}",), dma="sync", tag="wqld")

                def fmm(t, jb=jb, oi=oi):
                    for dp in range(DP):
                        ins = t.matmul(pA(oi % 4), wqbuf[oi % 2][:, dp * 128:(dp + 1) * 128],
                                       ln1_fm[dp][:], start=(dp == 0), stop=(dp == DP - 1))
                    return ins
                P("tensor", fmm,
                  reads=tuple([f"wqb{oi % 2}"] + [f"lnfm{dp}" for dp in range(DP)]),
                  writes=(f"pA{oi % 4}",), tag=f"qkvmm{jb}")
                P("scalar", (lambda a, oi=oi: a.activation(
                    qkst[oi % 2][:], pA(oi % 4), AF.Silu)),
                  reads=(f"pA{oi % 4}",), writes=(f"qkst{oi % 2}",), tag="qkvsilu")
                kind = 0 if jb < 16 else (1 if jb < 32 else 2)
                if kind != 2:
                    h = jb % 16
                    P("scalar", (lambda a, oi=oi: a.activation(
                        sqst[:], qkst[oi % 2][:], AF.Square)),
                      reads=(f"qkst{oi % 2}",), writes=("sqst",), tag="qsq")
                    P("tensor", (lambda t, oi=oi: t.matmul(
                        pB(2 + oi % 2)[0:1, :], ones_colb[:, 0:1], sqst[:],
                        start=True, stop=True)),
                      reads=("sqst", "ones"), writes=(f"pB{2 + oi % 2}",), tag="qnmm")
                    P("vector", (lambda v, oi=oi: v.tensor_copy(
                        rowb[oi % 2][0:1, :], pB(2 + oi % 2)[0:1, :])),
                      reads=(f"pB{2 + oi % 2}",), writes=(f"rowb{oi % 2}",), tag="qnrow")
                    P("tensor", (lambda t, oi=oi: t.matmul(
                        pB(2 + oi % 2)[:, :], ones_rowb[0:1, :], rowb[oi % 2][0:1, :],
                        start=True, stop=True)),
                      reads=(f"rowb{oi % 2}", "ones"), writes=(f"pB{2 + oi % 2}",), tag="qnex")
                    P("scalar", (lambda a, oi=oi: a.activation(
                        f32c[:], pB(2 + oi % 2)[:, :], AF.Sqrt)),
                      reads=(f"pB{2 + oi % 2}",), writes=("f32c",), tag="qnsqrt")
                    def fqn(v):
                        v.tensor_scalar(f32c[:], f32c[:], 1e-5, None, OP.add)
                        return v.reciprocal(f32c[:], f32c[:])
                    P("vector", fqn, reads=("f32c",), writes=("f32c",), tag="qnrec")
                    P("vector", (lambda v, oi=oi: v.tensor_tensor(
                        qkst[oi % 2][:], qkst[oi % 2][:], f32c[:], OP.mult)),
                      reads=(f"qkst{oi % 2}", "f32c"), writes=(f"qkst{oi % 2}",), tag="qnorm")
                P("scalar", (lambda s, jb=jb, oi=oi: s.dma_start(
                    out=scr_qkv[jb * 128:(jb + 1) * 128, :], in_=qkst[oi % 2][:])),
                  reads=(f"qkst{oi % 2}",), writes=(f"sqkv{jb}",), dma="scalq", tag="qkvst")
                drain_bg(6)

            if STOP_SUB == "qkv":
                return
            def fmm(t):
                for dp in range(DP):
                    ins = t.matmul(pB(2)[0:48, :], wlr_sb[:, dp * 48:(dp + 1) * 48],
                                   ln1_fm[dp][:], start=(dp == 0), stop=(dp == DP - 1))
                return ins
            P("tensor", fmm,
              reads=tuple(["wlr_sb"] + [f"lnfm{dp}" for dp in range(DP)]),
              writes=("pB2",), tag="lrmm")
            P("scalar", lambda a: a.activation(RL[0:48, :], pB(2)[0:48, :], AF.Exp,
                                               bias=blr2[0:48, 0:1]),
              reads=("pB2", "blr2"), writes=("RL0",), tag="lrexp")
            P("scalar", lambda a: a.activation(RL[0:48, :], RL[0:48, :], AF.Ln, bias=1.0),
              reads=("RL0",), writes=("RL0",), tag="lrln")
            for t8 in range(4):
                P("tensor", (lambda t, t8=t8: t.transpose(
                    pB(3)[:, 0:128], RL[:, t8 * 128:(t8 + 1) * 128], identf[:])),
                  reads=("RL0", "identf"), writes=("pB3",), tag="lrtr")
                P("vector", (lambda v, t8=t8: v.tensor_copy(
                    SCLR[:, t8 * 64:t8 * 64 + 48], pB(3)[:, 0:48])),
                  reads=("pB3",), writes=("SCLR",), tag="lrsc")

            if STOP_SUB == "lr":
                return
            for t8 in range(4):
                P("sync", (lambda s, t8=t8: s.dma_start(
                    out=ktok[t8][:], in_=scr_qkv[2048:4096, t8 * 128:(t8 + 1) * 128],
                    transpose=True)),
                  reads=tuple(f"sqkv{16 + h}" for h in range(16)),
                  writes=(f"ktok{t8}",), dma="sync", tag="kflip")
                P("sync", (lambda s, t8=t8: s.dma_start(
                    out=vtok[t8][:], in_=scr_qkv[4096:6144, t8 * 128:(t8 + 1) * 128],
                    transpose=True)),
                  reads=tuple(f"sqkv{32 + h}" for h in range(16)),
                  writes=(f"vtok{t8}",), dma="sync", tag="vflip")

            if STOP_SUB == "flip":
                return
            for h in range(H):
                hs = slice(h * 128, (h + 1) * 128)
                P("sync", (lambda s, h=h: s.dma_start(
                    out=kqst[h % 2][:], in_=scr_qkv[(16 + h) * 128:(17 + h) * 128, :])),
                  reads=(f"sqkv{16 + h}",), writes=(f"kqst{h % 2}",), dma="sync", tag="krld")
                P("sync", (lambda s, h=h: s.dma_start(
                    out=vst[h % 2][:], in_=scr_qkv[(32 + h) * 128:(33 + h) * 128, :])),
                  reads=(f"sqkv{32 + h}",), writes=(f"vst{h % 2}",), dma="sync", tag="vrld")

                slg = h % 2
                slh = 2 + h % 2
                sld = h % 2

                def fmm(t, h=h, hs=hs, slg=slg, slh=slh, sld=sld):
                    for t8 in range(4):
                        t.matmul(pA(slg)[:, t8 * 128:(t8 + 1) * 128],
                                 kqst[h % 2][:, t8 * 128:(t8 + 1) * 128], U0[:, hs],
                                 start=True, stop=True)
                    for t8 in range(4):
                        t.matmul(pA(slh)[:, t8 * 128:(t8 + 1) * 128],
                                 kqst[h % 2][:, t8 * 128:(t8 + 1) * 128], U2[:, hs],
                                 start=True, stop=True)
                    for t8 in range(4):
                        ins = t.matmul(pB(sld)[:, t8 * 128:(t8 + 1) * 128],
                                       vst[h % 2][:, t8 * 128:(t8 + 1) * 128], U1a[:, hs],
                                       start=True, stop=True)
                    return ins
                P("tensor", fmm,
                  reads=(f"kqst{h % 2}", f"vst{h % 2}", "U0", "U2", "U1"),
                  writes=(f"pA{slg}", f"pA{slh}", f"pB{sld}"), tag=f"fwdk{h}")

                P("scalar", (lambda a, slg=slg: a.activation(ew[0][:], pA(slg), AF.Copy)),
                  reads=(f"pA{slg}",), writes=("ew0",), tag="evg")
                P("scalar", (lambda a, slh=slh: a.activation(ew[1][:], pA(slh), AF.Copy)),
                  reads=(f"pA{slh}",), writes=("ew1",), tag="evh")
                P("scalar", (lambda a, sld=sld: a.activation(ew[2][:], pB(sld), AF.Copy)),
                  reads=(f"pB{sld}",), writes=("ew2",), tag="evd")
                P("scalar", lambda a: a.activation(ew[3][:], ew[0][:], AF.Silu),
                  reads=("ew0",), writes=("ew3",), tag="sg")
                P("scalar", lambda a: a.activation(ew[4][:], ew[0][:], AF.Sigmoid),
                  reads=("ew0",), writes=("ew4",), tag="sig")

                def fv1(v, h=h):
                    v.tensor_tensor(ew[5][:], ew[3][:], ew[1][:], OP.mult)
                    for t8 in range(4):
                        v.tensor_scalar(hidn_h[:, t8 * 128:(t8 + 1) * 128],
                                        ew[5][:, t8 * 128:(t8 + 1) * 128],
                                        SCLR[:, t8 * 64 + 16 + h:t8 * 64 + 17 + h], None, OP.mult)
                    v.tensor_tensor(ew[5][:], ew[2][:], ew[3][:], OP.mult)
                    for t8 in range(4):
                        v.tensor_scalar(dhbm_h[:, t8 * 128:(t8 + 1) * 128],
                                        ew[5][:, t8 * 128:(t8 + 1) * 128],
                                        SCLR[:, t8 * 64 + 32 + h:t8 * 64 + 33 + h], None, OP.mult)
                    v.tensor_tensor(ew[2][:], ew[2][:], ew[1][:], OP.mult)
                    v.tensor_tensor(ew[1][:], ew[0][:], ew[4][:], OP.mult)
                    v.scalar_tensor_tensor(ew[0][:], ew[0][:], 1.0, ew[1][:], OP.add, OP.subtract)
                    v.tensor_tensor(ew[2][:], ew[2][:], ew[4][:], OP.mult)
                    v.tensor_tensor(ew[2][:], ew[2][:], ew[0][:], OP.mult)
                    for t8 in range(4):
                        ins = v.tensor_scalar(dgba_h[:, t8 * 128:(t8 + 1) * 128],
                                              ew[2][:, t8 * 128:(t8 + 1) * 128],
                                              SCLR[:, t8 * 64 + h:t8 * 64 + 1 + h], None, OP.mult)
                    return ins
                P("vector", fv1,
                  reads=("ew0", "ew1", "ew2", "ew3", "ew4", "SCLR"),
                  writes=("ew0", "ew1", "ew2", "ew5", "hidn_h", "dhbm_h", "dgba_h"),
                  tag=f"ttew{h}")

                gslot = 2 + h % 2

                def fg(t, h=h, hs=hs, gslot=gslot):
                    for t8 in range(4):
                        ts8 = slice(t8 * 128, (t8 + 1) * 128)
                        t.matmul(pB(gslot)[:, 0:128], dgba_h[:, ts8], ktok[t8][:, hs],
                                 start=(t8 == 0), stop=(t8 == 3))
                    for t8 in range(4):
                        ts8 = slice(t8 * 128, (t8 + 1) * 128)
                        t.matmul(pB(gslot)[:, 128:256], vtok[t8][:, hs], hidn_h[:, ts8],
                                 start=(t8 == 0), stop=(t8 == 3))
                    for t8 in range(4):
                        ts8 = slice(t8 * 128, (t8 + 1) * 128)
                        ins = t.matmul(pB(gslot)[:, 256:384], dhbm_h[:, ts8], ktok[t8][:, hs],
                                       start=(t8 == 0), stop=(t8 == 3))
                    return ins
                P("tensor", fg,
                  reads=tuple(["dgba_h", "hidn_h", "dhbm_h"]
                              + [f"ktok{i}" for i in range(4)]
                              + [f"vtok{i}" for i in range(4)]),
                  writes=(f"pB{gslot}",), tag=f"gmm{h}")

                def fge(v, h=h, hs=hs, gslot=gslot):
                    v.tensor_copy(Xns[0][:, hs], pB(gslot)[:, 0:128])
                    v.tensor_copy(Xns[1][:, hs], pB(gslot)[:, 128:256])
                    return v.tensor_copy(Xns[2][:, hs], pB(gslot)[:, 256:384])
                P("vector", fge, reads=(f"pB{gslot}",),
                  writes=("Xns0", "Xns1", "Xns2"), tag=f"gev{h}")
                if h % 4 == 3:
                    drain_bg(8)

            if STOP_SUB == "fwd":
                return
            for w in range(3):
                xk = f"Xns{w}"
                P("scalar", (lambda a, w=w: a.activation(f32a[:], Xns[w][:], AF.Square)),
                  reads=(xk,), writes=("f32a",), tag="nsq")
                P("vector", lambda v: v.tensor_reduce(
                    coltmp[:, 0:16], f32a[:].rearrange("p (a b) -> p a b", b=128),
                    AX.X, OP.add),
                  reads=("f32a",), writes=("coltmp",), tag="nsred")
                P("scalar", lambda a: a.activation(ew[5][:, 0:16], coltmp[:, 0:16], AF.Copy),
                  reads=("coltmp",), writes=("ew5",), tag="nscast")
                P("tensor", lambda t: t.matmul(pB(3)[0:1, 0:16], ones_colb[:, 0:1],
                                               ew[5][:, 0:16], start=True, stop=True),
                  reads=("ew5", "ones"), writes=("pB3",), tag="nsfmm")
                P("scalar", lambda a: a.activation(f32c[0:1, 0:16], pB(3)[0:1, 0:16],
                                                   AF.Sqrt, bias=1e-14),
                  reads=("pB3",), writes=("f32c",), tag="nsfsq")
                P("vector", lambda v: v.reciprocal(f32c[0:1, 16:32], f32c[0:1, 0:16]),
                  reads=("f32c",), writes=("f32c",), tag="nsfrec")
                P("scalar", lambda a: a.activation(rowb[2][0:1, 0:16], f32c[0:1, 16:32],
                                                   AF.Copy),
                  reads=("f32c",), writes=("rowb2",), tag="nsfrow")
                P("tensor", lambda t: t.matmul(pB(3)[:, 0:16], ones_rowb[0:1, :],
                                               rowb[2][0:1, 0:16], start=True, stop=True),
                  reads=("rowb2", "ones"), writes=("pB3",), tag="nsfex")
                P("scalar", lambda a: a.activation(coltmp[:, 0:16], pB(3)[:, 0:16], AF.Copy),
                  reads=("pB3",), writes=("coltmp",), tag="nsfcol")

                def fsc(v, w=w):
                    for h in range(H):
                        ins = v.tensor_scalar(Xns[w][:, h * 128:(h + 1) * 128],
                                              Xns[w][:, h * 128:(h + 1) * 128],
                                              coltmp[:, h:h + 1], None, OP.mult)
                    return ins
                P("vector", fsc, reads=(xk, "coltmp"), writes=(xk,), tag="nsscale")

                for it in range(2):
                    P("scalar", (lambda a, w=w: a.activation(f32a[:], Xns[w][:], AF.Copy)),
                      reads=(xk,), writes=("f32a",), tag="nscp")

                    def ftr(t):
                        for h in range(H):
                            ins = t.transpose(
                                psA[:, h * 128:(h + 1) * 128],
                                f32a[:, h * 128:(h + 1) * 128], identf[:])
                        return ins
                    P("tensor", ftr, reads=("f32a", "identf"),
                      writes=("pA0", "pA1", "pA2", "pA3"), tag="nstr")
                    P("vector", lambda v: v.tensor_copy(Tt_ns[:], psA[:]),
                      reads=("pA0", "pA1", "pA2", "pA3"), writes=("Tt",), tag="nstte")

                    def fa(t, w=w):
                        for h in range(H):
                            hs2 = slice(h * 128, (h + 1) * 128)
                            ins = t.matmul(psB[:, hs2], Xns[w][:, hs2], Xns[w][:, hs2],
                                           start=True, stop=True)
                        return ins
                    P("tensor", fa, reads=(xk,),
                      writes=("pB0", "pB1", "pB2", "pB3"), tag="nsA")
                    P("vector", lambda v: v.tensor_copy(A_ns[:], psB[:]),
                      reads=("pB0", "pB1", "pB2", "pB3"), writes=("A_ns",), tag="nsAe")

                    def fa2(t):
                        for h in range(H):
                            hs2 = slice(h * 128, (h + 1) * 128)
                            ins = t.matmul(psA[:, hs2], A_ns[:, hs2], A_ns[:, hs2],
                                           start=True, stop=True)
                        return ins
                    P("tensor", fa2, reads=("A_ns",),
                      writes=("pA0", "pA1", "pA2", "pA3"), tag="nsA2")
                    P("scalar", lambda a: a.activation(B_ns[:], psA[:], AF.Copy, scale=NS_C),
                      reads=("pA0", "pA1", "pA2", "pA3"), writes=("B_ns",), tag="nsA2e")
                    P("vector", lambda v: v.scalar_tensor_tensor(
                        B_ns[:], A_ns[:], NS_B, B_ns[:], OP.mult, OP.add),
                      reads=("A_ns", "B_ns"), writes=("B_ns",), tag="nsB")

                    def fxb(t):
                        for h in range(H):
                            hs2 = slice(h * 128, (h + 1) * 128)
                            ins = t.matmul(psB[:, hs2], Tt_ns[:, hs2], B_ns[:, hs2],
                                           start=True, stop=True)
                        return ins
                    P("tensor", fxb, reads=("Tt", "B_ns"),
                      writes=("pB0", "pB1", "pB2", "pB3"), tag="nsXB")
                    P("vector", (lambda v, w=w: v.scalar_tensor_tensor(
                        Xns[w][:], Xns[w][:], NS_A, psB[:], OP.mult, OP.add)),
                      reads=(xk, "pB0", "pB1", "pB2", "pB3"), writes=(xk,), tag="nsX")

                WC, wkey = ((W0c, "W0c"), (W1c, "W1c"), (W2c, "W2c"))[w]
                P("vector", (lambda v, WC=WC, w=w: v.tensor_tensor(
                    WC[:], WC[:], Xns[w][:], OP.add)),
                  reads=(wkey, xk), writes=(wkey,), tag="wup")
                P("scalar", (lambda a, WC=WC: a.activation(f32a[:], WC[:], AF.Square)),
                  reads=(wkey,), writes=("f32a",), tag="wnsq")
                P("vector", lambda v: v.tensor_reduce(
                    coltmp[:, 0:16], f32a[:].rearrange("p (a b) -> p a b", b=128),
                    AX.X, OP.add),
                  reads=("f32a",), writes=("coltmp",), tag="wnred")
                P("scalar", lambda a: a.activation(coltmp[:, 16:32], coltmp[:, 0:16],
                                                   AF.Sqrt, bias=1e-10),
                  reads=("coltmp",), writes=("coltmp",), tag="wnsq2")
                P("vector", lambda v: v.reciprocal(coltmp[:, 0:16], coltmp[:, 16:32]),
                  reads=("coltmp",), writes=("coltmp",), tag="wnrec")
                P("gpsimd", (lambda g, w=w: g.tensor_tensor(
                    coltmp[:, 16:32], coltmp[:, 0:16],
                    WN[:, w * 16:(w + 1) * 16], OP.mult)),
                  reads=("coltmp", "WN"), writes=("coltmp",), tag="wnsc")

                def fws(v, WC=WC):
                    for h in range(H):
                        ins = v.tensor_scalar(WC[:, h * 128:(h + 1) * 128],
                                              WC[:, h * 128:(h + 1) * 128],
                                              coltmp[:, 16 + h:17 + h], None, OP.mult)
                    return ins
                P("vector", fws, reads=(wkey, "coltmp"), writes=(wkey,), tag="wsc")

            emit_usage()
            drain_bg(8)

            if STOP_SUB == "ns":
                return
            for h in range(H):
                hs = slice(h * 128, (h + 1) * 128)
                P("sync", (lambda s, h=h: s.dma_start(
                    out=kqst[h % 2][:], in_=scr_qkv[h * 128:(h + 1) * 128, :])),
                  reads=(f"sqkv{h}",), writes=(f"kqst{h % 2}",), dma="sync", tag="qrld")
                slg = h % 2
                slh = 2 + h % 2

                def fmm(t, h=h, hs=hs, slg=slg, slh=slh):
                    for t8 in range(4):
                        t.matmul(pA(slg)[:, t8 * 128:(t8 + 1) * 128],
                                 kqst[h % 2][:, t8 * 128:(t8 + 1) * 128], U0[:, hs],
                                 start=True, stop=True)
                    for t8 in range(4):
                        ins = t.matmul(pA(slh)[:, t8 * 128:(t8 + 1) * 128],
                                       kqst[h % 2][:, t8 * 128:(t8 + 1) * 128], U2[:, hs],
                                       start=True, stop=True)
                    return ins
                P("tensor", fmm, reads=(f"kqst{h % 2}", "U0", "U2"),
                  writes=(f"pA{slg}", f"pA{slh}"), tag=f"fwdq{h}")
                P("scalar", (lambda a, slg=slg: a.activation(ew[0][:], pA(slg), AF.Silu)),
                  reads=(f"pA{slg}",), writes=("ew0",), tag="qsilu")
                P("scalar", (lambda a, slh=slh: a.activation(ew[1][:], pA(slh), AF.Copy)),
                  reads=(f"pA{slh}",), writes=("ew1",), tag="qcp")
                P("vector", lambda v: v.tensor_tensor(f32c[:], ew[0][:], ew[1][:], OP.mult),
                  reads=("ew0", "ew1"), writes=("f32c",), tag="mtok")

                def ftr(t):
                    for t8 in range(4):
                        ins = t.transpose(pB(2)[:, t8 * 128:(t8 + 1) * 128],
                                          f32c[:, t8 * 128:(t8 + 1) * 128], identf[:])
                    return ins
                P("tensor", ftr, reads=("f32c", "identf"), writes=("pB2",), tag="mtr")
                P("scalar", lambda a: a.activation(mfm[:], pB(2), AF.Copy),
                  reads=("pB2",), writes=("mfm",), tag="mev")

                def fo(t, hs=hs):
                    for t8 in range(4):
                        ins = t.matmul(pB(3)[:, t8 * 128:(t8 + 1) * 128],
                                       mfm[:, t8 * 128:(t8 + 1) * 128], U1b[:, hs],
                                       start=True, stop=True)
                    return ins
                P("tensor", fo, reads=("mfm", "U1"), writes=("pB3",), tag=f"omm{h}")
                P("scalar", lambda a: a.activation(f32c[:], pB(3), AF.Square),
                  reads=("pB3",), writes=("f32c",), tag="osq")
                P("vector", lambda v: v.tensor_reduce(
                    coltp2[:, 0:4], f32c[:].rearrange("p (a b) -> p a b", b=128),
                    AX.X, OP.add),
                  reads=("f32c",), writes=("coltp2",), tag="ored")
                P("scalar", lambda a: a.activation(coltp2[:, 4:8], coltp2[:, 0:4],
                                                   AF.Sqrt, bias=1e-5, scale=1.0 / 128),
                  reads=("coltp2",), writes=("coltp2",), tag="orsd")
                P("vector", lambda v: v.reciprocal(coltp2[:, 8:12], coltp2[:, 4:8]),
                  reads=("coltp2",), writes=("coltp2",), tag="orsq")

                def fsc(a):
                    for t8 in range(4):
                        ins = a.activation(attn_st[:, t8 * 128:(t8 + 1) * 128],
                                           pB(3)[:, t8 * 128:(t8 + 1) * 128],
                                           AF.Copy, scale=coltp2[:, 8 + t8:9 + t8])
                    return ins
                P("scalar", fsc, reads=("pB3", "coltp2"),
                  writes=("attn_st",), tag="oscale")
                P("vector", lambda v: v.tensor_tensor(attn_st[:], attn_st[:], onw_bc[:],
                                                      OP.mult),
                  reads=("attn_st", "onw_bc"), writes=("attn_st",), tag="oscl2")
                P("scalar", (lambda s, h=h: s.dma_start(
                    out=scr_attn[h].transpose([1, 0, 2]),
                    in_=attn_st[:].rearrange("p (a j) -> p a j", a=4))),
                  reads=("attn_st",), writes=(f"sattn{h}",), dma="scalq", tag="attnst")
                if h % 4 == 3:
                    drain_bg(8)

            if STOP_SUB == "q":
                return
            refresh_u1a()

            for tp in range(2):
                t8a, t8b = 2 * tp, 2 * tp + 1
                for dp in range(DP):
                    P("sync", (lambda s, dp=dp: s.dma_start(
                        out=afm[dp % 2][:], in_=scr_attn[dp].rearrange("a p j -> (a p) j"),
                        transpose=True)),
                      reads=(f"sattn{dp}",), writes=(f"afm{dp % 2}",), dma="sync", tag="aflip")
                    P("sync", (lambda s, dp=dp: s.dma_start(
                        out=wpbuf[dp % 2][:], in_=scr_wp[dp])),
                      reads=(f"swp{dp}",), writes=(f"wpb{dp % 2}",), dma="sync", tag="wpld")

                    def fmm(t, dp=dp, t8a=t8a, t8b=t8b):
                        for do in range(4):
                            t.matmul(pA(do), afm[dp % 2][:, t8a * 128:(t8a + 1) * 128],
                                     wpbuf[dp % 2][:, do * 512:(do + 1) * 512],
                                     start=(dp == 0), stop=(dp == DP - 1))
                        for do in range(4):
                            ins = t.matmul(pB(do), afm[dp % 2][:, t8b * 128:(t8b + 1) * 128],
                                           wpbuf[dp % 2][:, do * 512:(do + 1) * 512],
                                           start=(dp == 0), stop=(dp == DP - 1))
                        return ins
                    P("tensor", fmm, reads=(f"afm{dp % 2}", f"wpb{dp % 2}"),
                      writes=("pA0", "pA1", "pA2", "pA3", "pB0", "pB1", "pB2", "pB3"),
                      tag=f"projmm{tp}_{dp}")
                for (t8, ps, pkeys) in ((t8a, psA, ("pA0", "pA1", "pA2", "pA3")),
                                        (t8b, psB, ("pB0", "pB1", "pB2", "pB3"))):
                    gt8 = c * 4 + t8
                    P("sync", (lambda s, t8=t8: s.dma_start(
                        out=xa[t8 % 2][:], in_=x_in[c0 + t8 * 128:c0 + (t8 + 1) * 128, :])),
                      writes=(f"xa{t8 % 2}",), dma="sync", tag="xrld")
                    P("vector", (lambda v, ps=ps, t8=t8: v.scalar_tensor_tensor(
                        xa[t8 % 2][:], ps[:], 1.0, xa[t8 % 2][:], OP.mult, OP.add)),
                      reads=pkeys + (f"xa{t8 % 2}",), writes=(f"xa{t8 % 2}",), tag="resadd")
                    P("scalar", (lambda s, t8=t8: s.dma_start(
                        out=scr_x1[c0 + t8 * 128:c0 + (t8 + 1) * 128, :], in_=xa[t8 % 2][:])),
                      reads=(f"xa{t8 % 2}",), writes=(f"sx1_{gt8}",), dma="scalq", tag="x1st")
                    P("scalar", (lambda a, t8=t8: a.activation(
                        f32a[:], xa[t8 % 2][:], AF.Square, accum_out=lnc[:, 1:2])),
                      reads=(f"xa{t8 % 2}",), writes=("f32a", "lnc"), tag="l2sq")

                    def f(v, t8=t8):
                        v.reduce_sum(lnc[:, 0:1], xa[t8 % 2][:], AX.X)
                        v.reduce_max(lnc[:, 8:9], xa[t8 % 2][:], AX.X)
                        return v.tensor_reduce(lnc[:, 9:10], xa[t8 % 2][:], AX.X, OP.min)
                    P("vector", f, reads=(f"xa{t8 % 2}",), writes=("lnc",), tag="l2red")

                    def f(g):
                        g.tensor_copy(lnc[:, 20:21], lnc[:, 0:1])
                        g.tensor_copy(lnc[:, 21:22], lnc[:, 8:9])
                        return g.tensor_copy(lnc[:, 22:23], lnc[:, 9:10])
                    P("gpsimd", f, reads=("lnc",), writes=("lnc",), tag="l2echo")

                    def f(v):
                        v.tensor_scalar(lnc[:, 2:3], lnc[:, 20:21], -1.0 / D, None, OP.mult)
                        return v.tensor_scalar(lnc[:, 3:4], lnc[:, 1:2], 1.0 / D, None, OP.mult)
                    P("vector", f, reads=("lnc",), writes=("lnc",), tag="l2st")
                    P("scalar", lambda a: a.activation(lnc[:, 23:24], lnc[:, 2:3], AF.Square),
                      reads=("lnc",), writes=("lnc",), tag="l2mu2")

                    def f(v):
                        v.tensor_tensor(lnc[:, 4:5], lnc[:, 3:4], lnc[:, 23:24], OP.subtract)
                        v.tensor_tensor(lnc[:, 10:11], lnc[:, 21:22], lnc[:, 2:3], OP.add)
                        return v.scalar_tensor_tensor(lnc[:, 11:12], lnc[:, 2:3], -1.0,
                                                      lnc[:, 22:23], OP.mult, OP.subtract)
                    P("vector", f, reads=("lnc",), writes=("lnc",), tag="l2v")
                    P("scalar", lambda a: a.activation(lnc[:, 5:6], lnc[:, 4:5], AF.Sqrt,
                                                       bias=1e-5),
                      reads=("lnc",), writes=("lnc",), tag="l2sd")
                    def f(v):
                        v.reciprocal(lnc[:, 6:7], lnc[:, 5:6])
                        return v.tensor_tensor(lnc[:, 12:13], lnc[:, 10:11],
                                               lnc[:, 11:12], OP.max)
                    P("vector", f, reads=("lnc",), writes=("lnc",), tag="l2rs")
                    P("gpsimd", lambda g: g.tensor_tensor(lnc[:, 13:14], lnc[:, 6:7],
                                                          lnc[:, 12:13], OP.mult),
                      reads=("lnc",), writes=("lnc",), tag="l2gr")
                    P("vector", lambda v: v.tensor_scalar(lnc[:, 14:15], lnc[:, 13:14],
                                                          1e-5, None, OP.max),
                      reads=("lnc",), writes=("lnc",), tag="l2gx")

                    def f(a, gt8=gt8):
                        a.activation(scl[:, gt8:gt8 + 1], lnc[:, 14:15], AF.Identity,
                                     scale=1.0 / 127)
                        return a.activation(lnc[:, 24:25], lnc[:, 14:15], AF.Identity)
                    P("scalar", f, reads=("lnc",), writes=("lnc", "scl"), tag="l2sc")
                    P("vector", lambda v: v.reciprocal(lnc[:, 15:16], lnc[:, 24:25]),
                      reads=("lnc",), writes=("lnc",), tag="l2ig")
                    P("scalar", (lambda a, gt8=gt8: a.activation(
                        scl[:, 16 + gt8:17 + gt8], lnc[:, 15:16], AF.Identity, scale=127.0)),
                      reads=("lnc",), writes=("scl",), tag="l2sx")
                    P("gpsimd", lambda g: g.tensor_tensor(lnc[:, 16:17], lnc[:, 6:7],
                                                          lnc[:, 15:16], OP.mult),
                      reads=("lnc",), writes=("lnc",), tag="l2ra")
                    P("vector", lambda v: v.tensor_scalar(lnc[:, 17:18], lnc[:, 16:17],
                                                          127.0, None, OP.mult),
                      reads=("lnc",), writes=("lnc",), tag="l2a")
                    P("gpsimd", lambda g: g.tensor_tensor(lnc[:, 18:19], lnc[:, 2:3],
                                                          lnc[:, 17:18], OP.mult),
                      reads=("lnc",), writes=("lnc",), tag="l2b")
                    P("scalar", (lambda a, t8=t8: a.activation(
                        f32a[:], xa[t8 % 2][:], AF.Identity, bias=lnc[:, 18:19],
                        scale=lnc[:, 17:18])),
                      reads=(f"xa{t8 % 2}", "lnc"), writes=("f32a",), tag="l2aff")

                    def f(v):
                        v.tensor_scalar(f32a[:], f32a[:], MAGIC, -MAGIC, OP.add, OP.add)
                        return v.tensor_scalar(ln1tok[:], f32a[:], -128.0, 127.0, OP.max, OP.min)
                    P("vector", f, reads=("f32a",), writes=("f32a", "ln1tok"), tag="l2rnd")
                    P("scalar", (lambda s, t8=t8: s.dma_start(
                        out=scr_x2q[c0 + t8 * 128:c0 + (t8 + 1) * 128, :], in_=ln1tok[:])),
                      reads=("ln1tok",), writes=(f"sx2q_{gt8}",), dma="scalq", tag="x2qst")
                drain_bg(8)

        def emit_ffn_group(g):
            g0 = g * S
            for dp in range(DP):
                P("sync", (lambda s, dp=dp, g0=g0: s.dma_start(
                    out=x2qfm[dp][:], in_=scr_x2q[g0:g0 + S, dp * 128:(dp + 1) * 128],
                    transpose=True)),
                  reads=tuple(f"sx2q_{g * 4 + i}" for i in range(4)),
                  writes=(f"x2qfm{dp}",), dma="sync", tag="x2flip")

            for t8 in range(4):
                P("tensor", (lambda t, t8=t8, g=g: t.transpose(
                    pB(3)[0:1, t8 * 128:(t8 + 1) * 128],
                    scl[:, 16 + g * 4 + t8:17 + g * 4 + t8], identf[:])),
                  reads=("scl", "identf"), writes=("pB3",), tag="sxtr")
            P("scalar", lambda a: a.activation(rowfP[0:1, :], pB(3)[0:1, 0:512], AF.Copy),
              reads=("pB3",), writes=("rowf0",), tag="sxrow")

            def f(v):
                v.tensor_scalar(rowb[0][0:1, :], rowfP[0:1, :], grow[0:1, 4:5], None,
                                OP.mult)
                return v.tensor_scalar(rowb[1][0:1, :], rowfP[0:1, :], grow[0:1, 5:6],
                                       None, OP.mult)
            P("vector", f, reads=("rowf0", "grow"),
              writes=("rowb0", "rowb1"), tag="invscg")
            def f(v, g=g):
                for t8 in range(4):
                    v.tensor_tensor(coltmp[:, t8:t8 + 1], gcol[:, 0:1],
                                    scl[:, g * 4 + t8:g * 4 + t8 + 1], OP.mult)
                    ins = v.tensor_tensor(coltmp[:, 4 + t8:5 + t8], gcol[:, 1:2],
                                          scl[:, g * 4 + t8:g * 4 + t8 + 1], OP.mult)
                return ins
            P("vector", f, reads=("gcol", "scl"), writes=("coltmp",), tag="scgv")

            for ht in range(16):
                hrows = slice(ht * 512, (ht + 1) * 512)
                for dp in range(DP):
                    P("sync", (lambda s, ht=ht, dp=dp: s.dma_start(
                        out=gvsetG[:, dp * 512:(dp + 1) * 512],
                        in_=scr_tg[ht * 512:(ht + 1) * 512, dp * 128:(dp + 1) * 128],
                        transpose=True)),
                      reads=tuple(f"tg_{(ht * 4 + r) * 2 + dp // 8}" for r in range(4)),
                      writes=(f"gvsetG{dp}",), dma="sync", tag="gldG")
                for dp in range(DP):
                    P("sync", (lambda s, ht=ht, dp=dp: s.dma_start(
                        out=gvsetV[:, dp * 512:(dp + 1) * 512],
                        in_=scr_tv[ht * 512:(ht + 1) * 512, dp * 128:(dp + 1) * 128],
                        transpose=True)),
                      reads=tuple(f"tv_{(ht * 4 + r) * 2 + dp // 8}" for r in range(4)),
                      writes=(f"gvsetV{dp}",), dma="sync", tag="gldV")
                P("sync", (lambda s, ht=ht: s.dma_start(
                    out=brow[0][0:1, :],
                    in_=scr_bias[ht * 512:(ht + 1) * 512].rearrange("(o j) -> o j", o=1))),
                  reads=("sbias0",), writes=("brow0",), dma="sync", tag="gbld")
                P("sync", (lambda s, ht=ht: s.dma_start(
                    out=brow[1][0:1, :],
                    in_=scr_bias[HID + ht * 512:HID + (ht + 1) * 512].rearrange("(o j) -> o j", o=1))),
                  reads=("sbias1",), writes=("brow1",), dma="sync", tag="vbld")

                def fmm(t, ht=ht):
                    for t8 in range(4):
                        for dp in range(DP):
                            t.matmul(pA(t8), x2qfm[dp][:, t8 * 128:(t8 + 1) * 128],
                                     gvsetG[:, dp * 512:(dp + 1) * 512],
                                     start=(dp == 0), stop=False)
                        t.matmul(pA(t8), rowb[0][0:1, t8 * 128:(t8 + 1) * 128],
                                 brow[0][0:1, :], start=False, stop=True)
                        for dp in range(DP):
                            t.matmul(pB(t8), x2qfm[dp][:, t8 * 128:(t8 + 1) * 128],
                                     gvsetV[:, dp * 512:(dp + 1) * 512],
                                     start=(dp == 0), stop=False)
                        ins = t.matmul(pB(t8), rowb[1][0:1, t8 * 128:(t8 + 1) * 128],
                                       brow[1][0:1, :], start=False, stop=True)
                    return ins
                P("tensor", fmm,
                  reads=tuple([f"x2qfm{dp}" for dp in range(DP)]
                              + [f"gvsetG{dp}" for dp in range(DP)]
                              + [f"gvsetV{dp}" for dp in range(DP)]
                              + ["brow0", "brow1", "rowb0", "rowb1"]),
                  writes=("pA0", "pA1", "pA2", "pA3", "pB0", "pB1", "pB2", "pB3"),
                  tag=f"ffnmm{g}_{ht}")

                for t8 in range(4):
                    P("scalar", (lambda a, t8=t8: a.activation(
                        ffnew[0][:], pA(t8), AF.Silu, scale=coltmp[:, t8:t8 + 1])),
                      reads=(f"pA{t8}", "coltmp"), writes=("ffnew0",), tag="ffsilu")
                    P("scalar", (lambda a, t8=t8: a.activation(
                        ffnew[1][:], pB(t8), AF.Copy, scale=coltmp[:, 4 + t8:5 + t8])),
                      reads=(f"pB{t8}", "coltmp"), writes=("ffnew1",), tag="ffval")
                    P("vector", (lambda v, t8=t8, ht=ht: v.tensor_tensor(
                        ygrp[t8][:, ht * 512:(ht + 1) * 512], ffnew[0][:], ffnew[1][:], OP.mult)),
                      reads=("ffnew0", "ffnew1"), writes=(f"ygrp{t8}",), tag="ffy")
                if ht % 4 == 3:
                    drain_bg(1)

            for t8 in range(4):
                P("vector", (lambda v, t8=t8: v.reduce_max(
                    coltp2[:, t8:t8 + 1], ygrp[t8][:], AX.X, apply_absolute_value=True)),
                  reads=(f"ygrp{t8}",), writes=("coltp2",), tag="gymax")
            P("gpsimd", lambda g: g.tensor_copy(coltp2[:, 16:20], coltp2[:, 0:4]),
              reads=("coltp2",), writes=("coltp2",), tag="gyecho")

            P("vector", lambda v: v.tensor_scalar(coltp2[:, 0:4], coltp2[:, 16:20],
                                                  1e-5, None, OP.max),
              reads=("coltp2",), writes=("coltp2",), tag="gyclip")

            def f(a):
                a.activation(scl[:, 32:36], coltp2[:, 0:4], AF.Identity, scale=1.0 / 127)
                return a.activation(coltp2[:, 20:24], coltp2[:, 0:4], AF.Identity)
            P("scalar", f, reads=("coltp2",), writes=("coltp2", "scl"), tag="gysc")
            P("vector", lambda v: v.reciprocal(coltp2[:, 24:28], coltp2[:, 20:24]),
              reads=("coltp2",), writes=("coltp2",), tag="gyrec")
            P("gpsimd", lambda g: g.tensor_tensor(coltp2[:, 8:12], coltp2[:, 24:28],
                                                  cst[:, 0:4], OP.mult),
              reads=("coltp2", "cst"), writes=("coltp2",), tag="gys")

            for t8 in range(4):
                for half in range(4):
                    hsl = slice(half * 2048, (half + 1) * 2048)
                    P("scalar", (lambda a, t8=t8, hsl=hsl: a.activation(
                        f32a[:], ygrp[t8][:, hsl], AF.Copy, bias=MAGIC,
                        scale=coltp2[:, 8 + t8:9 + t8])),
                      reads=(f"ygrp{t8}", "coltp2"), writes=("f32a",), tag="yq1")

                    def f(v, t8=t8, hsl=hsl):
                        v.tensor_scalar(f32a[:], f32a[:], -MAGIC, -128.0, OP.add, OP.max)
                        return v.tensor_scalar(ygrp[t8][:, hsl], f32a[:], 127.0, None, OP.min)
                    P("vector", f, reads=("f32a",), writes=(f"ygrp{t8}", "f32a"), tag="yq2")
                P("scalar", (lambda s, t8=t8: s.dma_start(
                    out=scr_y[t8 * 128:(t8 + 1) * 128, :], in_=ygrp[t8][:])),
                  reads=(f"ygrp{t8}",), writes=(f"sy{t8}",), dma="scalq", tag="yst")

            for t8 in range(4):
                P("tensor", (lambda t, t8=t8: t.transpose(
                    pB(3)[0:1, t8 * 128:(t8 + 1) * 128],
                    coltp2[:, 8 + t8:9 + t8], identf[:])),
                  reads=("coltp2", "identf"), writes=("pB3",), tag="sytr")
            P("scalar", lambda a: a.activation(rowfP[0:1, :], pB(3)[0:1, 0:512], AF.Copy),
              reads=("pB3",), writes=("rowf0",), tag="syrow")

            def f(v):
                v.tensor_scalar(rowb[2][0:1, :], rowfP[0:1, :], grow[0:1, 6:7], None,
                                OP.mult)
                for t8 in range(4):
                    ins = v.tensor_tensor(coltmp[:, 8 + t8:9 + t8], gcol[:, 2:3],
                                          scl[:, 32 + t8:33 + t8], OP.mult)
                return ins
            P("vector", f, reads=("rowf0", "grow", "gcol", "scl"),
              writes=("rowb2", "coltmp"), tag="invsco")
            for half in range(2):
                do0 = half * 1024
                P("sync", (lambda s, do0=do0: s.dma_start(
                    out=brow[2][0:1, :],
                    in_=scr_bias[2 * HID + do0:2 * HID + do0 + 512].rearrange("(o j) -> o j", o=1))),
                  reads=("sbias2",), writes=("brow2b",), dma="sync", tag="obld0")
                P("sync", (lambda s, do0=do0: s.dma_start(
                    out=brow[3][0:1, :],
                    in_=scr_bias[2 * HID + do0 + 512:2 * HID + do0 + 1024].rearrange("(o j) -> o j", o=1))),
                  reads=("sbias2",), writes=("brow3b",), dma="sync", tag="obld1")
                for hc in range(64):
                    P("sync", (lambda s, hc=hc, g0=g0: s.dma_start(
                        out=yfm[hc % 4][:],
                        in_=scr_y[:, hc * 128:(hc + 1) * 128], transpose=True)),
                      reads=("sy0", "sy1", "sy2", "sy3"), writes=(f"yfm{hc % 4}",),
                      dma="sync", tag="yflip")
                    P("sync", (lambda s, hc=hc, do0=do0: s.dma_start(
                        out=owbuf[hc % 2][:],
                        in_=scr_to[do0:do0 + 1024, hc * 128:(hc + 1) * 128],
                        transpose=True)),
                      reads=tuple(f"to_{r * 8 + hc // 8}" for r in range(do0 // 128,
                                                                          do0 // 128 + 8)),
                      writes=(f"owb{hc % 2}",), dma="sync", tag="owld")

                    def fmm(t, hc=hc):
                        for t8 in range(4):
                            ps = (pA(t8), pB(t8))
                            for dq in range(2):
                                ins = t.matmul(
                                    (psA if dq == 0 else psB)[:, t8 * 512:(t8 + 1) * 512],
                                    yfm[hc % 4][:, t8 * 128:(t8 + 1) * 128],
                                    owbuf[hc % 2][:, dq * 512:(dq + 1) * 512],
                                    start=(hc == 0), stop=False)
                        return ins
                    P("tensor", fmm, reads=(f"yfm{hc % 4}", f"owb{hc % 2}"),
                      writes=("pA0", "pA1", "pA2", "pA3", "pB0", "pB1", "pB2", "pB3"),
                      tag=f"offmm{g}_{half}_{hc}")

                def fbias(t):
                    for t8 in range(4):
                        for dq in range(2):
                            ins = t.matmul(
                                (psA if dq == 0 else psB)[:, t8 * 512:(t8 + 1) * 512],
                                rowb[2][0:1, t8 * 128:(t8 + 1) * 128],
                                brow[2 + dq][0:1, :],
                                start=False, stop=True)
                    return ins
                P("tensor", fbias, reads=("rowb2", "brow2b", "brow3b"),
                  writes=("pA0", "pA1", "pA2", "pA3", "pB0", "pB1", "pB2", "pB3"),
                  tag=f"obias{g}_{half}")

                for t8 in range(4):
                    gt8 = g * 4 + t8
                    P("sync", (lambda s, t8=t8, g0=g0, do0=do0: s.dma_start(
                        out=xa[t8 % 2][:, 0:1024],
                        in_=scr_x1[g0 + t8 * 128:g0 + (t8 + 1) * 128, do0:do0 + 1024])),
                      reads=(f"sx1_{gt8}",), writes=(f"xa{t8 % 2}",), dma="sync", tag="x1ld")
                    def fev2(v, t8=t8):
                        v.scalar_tensor_tensor(
                            xa[t8 % 2][:, 0:512], psA[:, t8 * 512:(t8 + 1) * 512],
                            coltmp[:, 8 + t8:9 + t8], xa[t8 % 2][:, 0:512], OP.mult, OP.add)
                        return v.scalar_tensor_tensor(
                            xa[t8 % 2][:, 512:1024], psB[:, t8 * 512:(t8 + 1) * 512],
                            coltmp[:, 8 + t8:9 + t8], xa[t8 % 2][:, 512:1024], OP.mult, OP.add)
                    P("vector", fev2,
                      reads=(f"pA{t8}", f"pB{t8}", "coltmp", f"xa{t8 % 2}"),
                      writes=(f"xa{t8 % 2}",), tag="oev")
                    P("scalar", (lambda s, t8=t8, g0=g0, do0=do0: s.dma_start(
                        out=out_hbm[g0 + t8 * 128:g0 + (t8 + 1) * 128, do0:do0 + 1024],
                        in_=xa[t8 % 2][:, 0:1024])),
                      reads=(f"xa{t8 % 2}",), writes=(f"OUT_{g}_{half}_{t8}",),
                      dma="scalq", tag="outst")
                drain_bg(2)

        if STOP_AFTER != "prologue":
            for c in range(NCH):
                emit_chunk(c)
                if STOP_AFTER == f"chunk{c}":
                    break
            if STOP_AFTER is None or STOP_AFTER == "ffn":
                drain_bg(10**9)
                P("vector", lambda v: v.engine_nop(), reads=(), writes=POOL_KEYS,
                  tag="phase_barrier")
                for g in range(NGRP):
                    emit_ffn_group(g)

        if DEBUG_TAPS:
            allkeys = set()
            for st in prog.steps:
                allkeys.update(st.writes)
            for name in DEBUG_TAPS:
                src = scr_map[name]
                P("gpsimd", (lambda g, src=src, name=name: g.dma_start(
                    out=dbg_outs[name], in_=src)),
                  reads=tuple(allkeys), writes=(f"DBG_{name}",), dma="gpq", tag="dbg")

        out_keys = [k for st in prog.steps for k in st.writes
                    if k.startswith("OUT_") or k.startswith("DBG_")]
        P("gpsimd", lambda g: g.nop(), reads=tuple(set(out_keys)), tag="final")

        emit_all(nc, prog, ctx)

    return nc, dbg_outs


DMA_POOL = 16


def emit_all(nc, prog, ctx):
    deps = prog.resolve()
    sems = {}
    for e in ("tensor", "vector", "scalar", "gpsimd"):
        sems[e] = [ctx.enter_context(nc.semaphore(f"se_{e}")), 0]
    pools = {}
    for q in ("sync", "scalq", "gpq"):
        pools[q] = [[ctx.enter_context(nc.semaphore(f"sq_{q}{i}")), 0]
                    for i in range(DMA_POOL)]
    qidx = {q: 0 for q in pools}

    selfwait = {}
    for st in prog.steps:
        if st.dma is not None:
            pool = pools[st.dma]
            i = qidx[st.dma] % DMA_POOL
            qidx[st.dma] += 1
            sem = pool[i]
            if sem[1] > 0:
                selfwait[st.idx] = (sem[0], sem[1])
            sem[1] += 16
            st.sem, st.val = sem[0], sem[1]
            st.need_inc = True
        elif st.need_inc:
            se = sems[st.eng]
            se[1] += 1
            st.sem, st.val = se[0], se[1]

    warm = ctx.enter_context(nc.sbuf_tensor("warmup", [128, 16], BF16))
    swarm = ctx.enter_context(nc.semaphore("swarm"))
    block = ctx.enter_context(nc.Block())
    per_eng = {e: [] for e in ENGINES}
    for st in prog.steps:
        eng = DMA_QUEUE_ENGINE[st.dma] if st.dma else st.eng
        per_eng[eng].append(st)

    for eng_name in ENGINES:
        steps = per_eng[eng_name]
        if not steps:
            continue

        def body(eng, steps=steps, eng_name=eng_name):
            if eng_name == "gpsimd":
                eng.memset(warm[:], 0.0).then_inc(swarm, 1)
            elif eng_name == "vector":
                eng.memset(warm[0:128, 0:1], 0.0)
            elif eng_name == "scalar":
                eng.memzero(warm[0:128, 2:4])
            elif eng_name == "tensor":
                eng.wait_ge(swarm, 1)
                eng.ldweights(warm[:, 0:16])
            waited = {}
            for st in steps:
                for d in deps[st.idx]:
                    if d.sem is None:
                        continue
                    key = id(d.sem)
                    if waited.get(key, -1) >= d.val:
                        continue
                    eng.wait_ge(d.sem, d.val)
                    waited[key] = d.val
                if st.idx in selfwait:
                    ssem, sval = selfwait[st.idx]
                    if waited.get(id(ssem), -1) < sval:
                        eng.wait_ge(ssem, sval)
                        waited[id(ssem)] = sval
                ins = st.fn(eng)
                if st.need_inc:
                    assert ins is not None, f"step {st.tag} needs inc, returned None"
                    if st.dma is None and eng_name in ("vector", "scalar", "gpsimd"):
                        if eng_name == "vector":
                            eng.memset(warm[0:128, 4:5], 0.0).then_inc(st.sem, 1)
                        elif eng_name == "scalar":
                            eng.memzero(warm[0:128, 6:8]).then_inc(st.sem, 1)
                        else:
                            eng.memset(warm[0:128, 8:9], 0.0).then_inc(st.sem, 1)
                    else:
                        ins.then_inc(st.sem, 16 if st.dma else 1)
        getattr(block, eng_name)(body)


def kernel(**inputs):
    x = np.ascontiguousarray(np.asarray(inputs["L0_attn_x"], dtype=np.float32))
    base = {
        "w0": np.ascontiguousarray(inputs["L0_attn_w0"]),
        "w1": np.ascontiguousarray(inputs["L0_attn_w1"]),
        "w2": np.ascontiguousarray(inputs["L0_attn_w2"]),
        "wqkv": np.ascontiguousarray(inputs["L0_attn_W_qkv"]),
        "wlr": np.ascontiguousarray(inputs["L0_attn_W_lr"]),
        "blr": np.ascontiguousarray(inputs["L0_attn_b_lr"]),
        "wproj": np.ascontiguousarray(inputs["L0_attn_W_proj"]),
        "onw": np.ascontiguousarray(inputs["L0_attn_onorm_w"]),
        "gw": np.ascontiguousarray(inputs["L0_ffn_gate_w"]),
        "gb": np.ascontiguousarray(inputs["L0_ffn_gate_b"]),
        "vw": np.ascontiguousarray(inputs["L0_ffn_val_w"]),
        "vb": np.ascontiguousarray(inputs["L0_ffn_val_b"]),
        "ow": np.ascontiguousarray(inputs["L0_ffn_out_w"]),
        "ob": np.ascontiguousarray(inputs["L0_ffn_out_b"]),
    }
    nc, _ = build_nc()
    in_maps = [dict(base, x=np.ascontiguousarray(x[i])) for i in range(8)]
    res = run_bass_kernel_spmd(nc, in_maps, core_ids=list(range(8)))
    return np.stack([r["out"] for r in res.results], axis=0)


# version 03 (final)
# speedup vs baseline: 1.4021x; 1.2163x over previous
import math
import numpy as np
import ml_dtypes
from contextlib import ExitStack

import concourse.bass as bass
import concourse.mybir as mybir
from concourse.bass_utils import run_bass_kernel_spmd

F32 = mybir.dt.float32
BF16 = mybir.dt.bfloat16
OP = mybir.AluOpType
AF = mybir.ActivationFunctionType
AX = mybir.AxisListType

T = 2048
D = 2048
S = 512
H = 16
JQ = 6144
HID = 8192
DP = D // 128
NCH = T // S
NGRP = 4

MAGIC = 12582912.0
BASE_LR_INV = float(np.log(np.expm1(0.01)))
NS_A, NS_B, NS_C = 3.4445, -4.775, 2.0315

STOP_AFTER = None
STOP_SUB = None
DEBUG_TAPS = []

DMA_QUEUE_ENGINE = {"sync": "sync", "scalq": "scalar", "gpq": "gpsimd"}
ENGINES = ("sync", "tensor", "vector", "scalar", "gpsimd")


class Step:
    __slots__ = ("eng", "fn", "reads", "writes", "dma", "idx", "need_inc",
                 "sem", "val", "tag")

    def __init__(self, eng, fn, reads, writes, dma, idx, tag):
        self.eng, self.fn = eng, fn
        self.reads, self.writes = tuple(reads), tuple(writes)
        self.dma, self.idx, self.tag = dma, idx, tag
        self.need_inc = False
        self.sem = None
        self.val = None


class Prog:
    def __init__(self):
        self.steps = []

    def add(self, eng, fn, reads=(), writes=(), dma=None, tag=""):
        st = Step(eng, fn, reads, writes, dma, len(self.steps), tag)
        self.steps.append(st)
        return st

    def resolve(self):
        deps = [[] for _ in self.steps]
        lw = {}
        rd = {}
        for st in self.steps:
            eng = DMA_QUEUE_ENGINE[st.dma] if st.dma else st.eng
            dset = {}
            for r in st.reads:
                w = lw.get(r)
                if w is not None:
                    dset[w.idx] = w
            for wres in st.writes:
                w = lw.get(wres)
                if w is not None:
                    dset[w.idx] = w
                for r in rd.get(wres, ()):
                    dset[r.idx] = r
            for d in dset.values():
                if d is st:
                    continue
                d_eng = DMA_QUEUE_ENGINE[d.dma] if d.dma else d.eng
                if d.dma is None and st.dma is None and d_eng == eng:
                    continue
                d.need_inc = True
                deps[st.idx].append(d)
            for r in st.reads:
                rd.setdefault(r, []).append(st)
            for wres in st.writes:
                lw[wres] = st
                rd[wres] = []
        return deps


def build_nc():
    nc = bass.Bass("TRN2", target_bir_lowering=False, debug=False,
                   detect_race_conditions=False)

    x_in = nc.dram_tensor("x", [T, D], F32, kind="ExternalInput").ap()
    w0_in = nc.dram_tensor("w0", [H, 128, 128], F32, kind="ExternalInput").ap()
    w1_in = nc.dram_tensor("w1", [H, 128, 128], F32, kind="ExternalInput").ap()
    w2_in = nc.dram_tensor("w2", [H, 128, 128], F32, kind="ExternalInput").ap()
    wqkv_in = nc.dram_tensor("wqkv", [D, JQ], F32, kind="ExternalInput").ap()
    wlr_in = nc.dram_tensor("wlr", [D, 48], F32, kind="ExternalInput").ap()
    blr_in = nc.dram_tensor("blr", [48], F32, kind="ExternalInput").ap()
    wproj_in = nc.dram_tensor("wproj", [D, D], F32, kind="ExternalInput").ap()
    onw_in = nc.dram_tensor("onw", [128], F32, kind="ExternalInput").ap()
    gw_in = nc.dram_tensor("gw", [HID, D], F32, kind="ExternalInput").ap()
    gb_in = nc.dram_tensor("gb", [HID], F32, kind="ExternalInput").ap()
    vw_in = nc.dram_tensor("vw", [HID, D], F32, kind="ExternalInput").ap()
    vb_in = nc.dram_tensor("vb", [HID], F32, kind="ExternalInput").ap()
    ow_in = nc.dram_tensor("ow", [D, HID], F32, kind="ExternalInput").ap()
    ob_in = nc.dram_tensor("ob", [D], F32, kind="ExternalInput").ap()

    out_hbm = nc.dram_tensor("out", [T, D], F32, kind="ExternalOutput").ap()

    scr_wq = nc.dram_tensor("scr_wq", [48, DP, 128, 128], BF16).ap()
    scr_wp = nc.dram_tensor("scr_wp", [DP, 128, D], BF16).ap()
    scr_ln1 = nc.dram_tensor("scr_ln1", [S, D], BF16).ap()
    scr_qkv = nc.dram_tensor("scr_qkv", [JQ, S], BF16).ap()
    scr_attn = nc.dram_tensor("scr_attn", [H, 4, 128, 128], BF16).ap()
    scr_x1 = nc.dram_tensor("scr_x1", [T, D], F32).ap()
    scr_x2q = nc.dram_tensor("scr_x2q", [T, D], BF16).ap()
    scr_tg = nc.dram_tensor("scr_tg", [HID, D], BF16).ap()
    scr_tv = nc.dram_tensor("scr_tv", [HID, D], BF16).ap()
    scr_to = nc.dram_tensor("scr_to", [D, HID], BF16).ap()
    scr_y = nc.dram_tensor("scr_y", [S, HID], BF16).ap()
    scr_dbgc = nc.dram_tensor("scr_dbgc", [8, 128, 16], F32).ap()
    scr_bias = nc.dram_tensor("scr_bias", [2 * HID + D], BF16).ap()

    scr_map = {"scr_wq": scr_wq, "scr_wp": scr_wp, "scr_ln1": scr_ln1,
               "scr_qkv": scr_qkv, "scr_attn": scr_attn, "scr_x1": scr_x1,
               "scr_x2q": scr_x2q, "scr_tg": scr_tg, "scr_tv": scr_tv,
               "scr_to": scr_to, "scr_y": scr_y, "scr_dbgc": scr_dbgc}
    dbg_outs = {}
    for name in DEBUG_TAPS:
        src = scr_map[name]
        dbg_outs[name] = nc.dram_tensor("dbg_" + name, list(src.shape),
                                        src.dtype, kind="ExternalOutput").ap()

    prog = Prog()
    P = prog.add
    ctx = ExitStack()
    with ctx:
        sb = lambda name, shape, dt: ctx.enter_context(nc.sbuf_tensor(name, shape, dt))

        identf = sb("identf", [128, 128], F32)
        ones_colb = sb("ones_colb", [128, 1], BF16)
        ones_colf = sb("ones_colf", [128, 1], F32)
        ones_rowb = sb("ones_rowb", [1, 128], BF16)
        ones_rowf = sb("ones_rowf", [1, 128], F32)
        onw_rowt = sb("onw_rowt", [1, 128], F32)
        onw_bc = sb("onw_bc", [128, 512], BF16)
        blr2 = sb("blr2", [128, 1], F32)
        RL = sb("RL", [128, 512], F32)
        SCLR = sb("SCLR", [128, 256], F32)
        scl = sb("scl", [128, 64], F32)
        lnc = sb("lnc", [128, 32], F32)
        grow = sb("grow", [1, 16], F32)
        gcol = sb("gcol", [128, 8], F32)
        coltmp = sb("coltmp", [128, 32], F32)
        coltp2 = sb("coltp2", [128, 32], F32)
        cst = sb("cst", [128, 4], F32)
        epsc = sb("epsc", [128, 4], F32)
        nc.const_aps.register(F32, 1e-5, epsc[:, 0:1])
        nc.const_aps.register(F32, 1e-10, epsc[:, 1:2])
        nc.const_aps.register(F32, 1e-14, epsc[:, 2:3])
        W0c = sb("W0c", [128, 2048], F32)
        W1c = sb("W1c", [128, 2048], F32)
        W2c = sb("W2c", [128, 2048], F32)
        U0 = sb("U0", [128, 2048], BF16)
        U2 = sb("U2", [128, 2048], BF16)
        U1x = sb("U1x", [128, 2048], BF16)
        U1a = U1x
        U1b = U1x
        WN = sb("WN", [128, 48], F32)
        wlr_sb = sb("wlr_sb", [128, DP * 48], BF16)

        psA = ctx.enter_context(nc.psum_tensor("psA", [128, 2048], F32))
        psB = ctx.enter_context(nc.psum_tensor("psB", [128, 2048], F32))

        def pA(i):
            return psA[:, i * 512:(i + 1) * 512]

        def pB(i):
            return psB[:, i * 512:(i + 1) * 512]

        PO = [sb(f"PO{i}", [128, 8192], BF16) for i in range(8)]
        ktok = [PO[0][:, i * 2048:(i + 1) * 2048] for i in range(4)]
        vtok = [PO[1][:, i * 2048:(i + 1) * 2048] for i in range(4)]
        ln1_fm = [PO[2][:, i * 512:(i + 1) * 512] for i in range(DP)]
        Xns = [PO[3][:, i * 2048:(i + 1) * 2048] for i in range(3)]
        Tt_ns = PO[3][:, 3 * 2048:4 * 2048]
        A_ns = PO[4][:, 0:2048]
        B_ns = PO[4][:, 2048:4096]
        ln1tok = PO[4][:, 4096:6144]
        wqbuf = [PO[4][:, 6144:8192], PO[5][:, 0:2048]]
        wpbuf = wqbuf
        dgba_h = PO[5][:, 2048:2560]
        dhbm_h = PO[5][:, 2560:3072]
        hidn_h = PO[5][:, 3072:3584]
        attn_st = PO[5][:, 3584:4096]
        mfm = PO[5][:, 4096:4608]
        afm = [PO[5][:, 4608:5120], PO[5][:, 5120:5632]]
        qkst = [PO[5][:, 5632:6144], PO[5][:, 6144:6656]]
        kqst = qkst
        vst = [PO[5][:, 6656:7168], PO[5][:, 7168:7680]]
        sqst = PO[5][:, 7680:8192]
        invnE = PO[6][:, 0:512]
        ew = [PO[6][:, 512 + i * 512:512 + (i + 1) * 512] for i in range(6)]
        tstage = [PO[6][:, 3584 + i * 512:3584 + (i + 1) * 512] for i in range(4)]
        f32t = [PO[7][:, i * 1024:(i + 1) * 1024].bitcast(F32) for i in range(4)]
        gstage = [PO[7][:, 4096 + i * 1024:4096 + (i + 1) * 1024].bitcast(F32)
                  for i in range(4)]
        rowb = ([PO[6][0:1, 5632 + i * 512:5632 + (i + 1) * 512] for i in range(3)])
        ygrp = [PO[i][:, 0:8192] for i in range(4)]
        gvsetG = PO[4][:, 0:8192]
        gvsetV = PO[5][:, 0:8192]
        x2qfm = [PO[6][:, i * 512:(i + 1) * 512] for i in range(DP)]
        yfm = [PO[7][:, i * 512:(i + 1) * 512] for i in range(4)]
        owbuf = [PO[7][:, 2048 + i * 1024:2048 + (i + 1) * 1024] for i in range(2)]
        ffnew = [PO[7][:, 4096 + i * 512:4096 + (i + 1) * 512] for i in range(2)]
        brow = [PO[7][0:1, 5120 + i * 512:5120 + (i + 1) * 512] for i in range(4)]
        rowfP = PO[7][0:1, 7168:8192].bitcast(F32)
        POOL_KEYS = tuple(
            [f"ktok{i}" for i in range(4)] + [f"vtok{i}" for i in range(4)]
            + [f"lnfm{i}" for i in range(DP)] + [f"Xns{i}" for i in range(3)]
            + ["Tt", "A_ns", "B_ns", "ln1tok", "wqb0", "wqb1", "wpb0", "wpb1",
               "dgba_h", "dhbm_h", "hidn_h", "attn_st", "mfm", "afm0", "afm1",
               "qkst0", "qkst1", "kqst0", "kqst1", "vst0", "vst1", "sqst", "invnE"]
            + [f"ygrp{i}" for i in range(4)] + [f"gvsetG{i}" for i in range(DP)]
            + [f"gvsetV{i}" for i in range(DP)] + [f"x2qfm{i}" for i in range(DP)]
            + [f"yfm{i}" for i in range(4)] + ["owb0", "owb1"]
            + [f"ew{i}" for i in range(6)]
            + [f"tsg{i}" for i in range(4)] + [f"f32t{i}" for i in range(4)]
            + [f"gsg{i}" for i in range(4)] + [
               "brow0", "brow1", "brow2", "brow3", "rowf0", "ffnew0", "ffnew1",
               "rowb0", "rowb1", "rowb2"])
        xa = [sb(f"xa{i}", [128, 2048], F32) for i in range(2)]
        winit = [xa[0][:, 0:128], xa[1][:, 0:128]]
        f32a = sb("f32a", [128, 2048], F32)
        f32c = sb("f32c", [128, 512], F32)
        gacc = sb("gacc", [128, 768], F32)

        def seq(tag, *fns):
            pass

        def f_id(g):
            g.memset(identf[:], 0.0)
            g.affine_select(out=identf[:], in_=identf[:], compare_op=OP.not_equal,
                            fill=1.0, base=0, pattern=[[-1, 128]], channel_multiplier=1)
            g.memset(ones_colb[:], 1.0)
            g.memset(ones_colf[:], 1.0)
            g.memset(ones_rowb[:], 1.0)
            g.memset(ones_rowf[:], 1.0)
            g.memset(RL[:], 0.0)
            g.memset(cst[:], 127.0)
            g.memset(epsc[:, 0:1], 1e-5)
            g.memset(epsc[:, 1:2], 1e-10)
            g.memset(epsc[:, 2:3], 1e-14)
            g.memset(coltmp[:], 0.0)
            g.memset(coltp2[:], 0.0)
            g.memset(lnc[:], 0.0)
            return g.memset(f32c[:], 0.0)
        P("gpsimd", f_id, writes=("identf", "ones", "RL0", "f32c",
                                  "coltmp", "coltp2", "lnc", "cst"), tag="ident")

        for jb in range(48):
            P("gpsimd", (lambda g, jb=jb: g.dma_start(
                out=scr_wq[jb],
                in_=wqkv_in[:, jb * 128:(jb + 1) * 128].rearrange("(a p) j -> a p j", p=128))),
              writes=(f"swq{jb}",), dma="gpq", tag="wqcast")
        for db in range(DP):
            P("gpsimd", (lambda g, db=db: g.dma_start(
                out=scr_wp[db], in_=wproj_in[db * 128:(db + 1) * 128, :])),
              writes=(f"swp{db}",), dma="gpq", tag="wpcast")
        for db in range(DP):
            P("gpsimd", (lambda g, db=db: g.dma_start(
                out=wlr_sb[:, db * 48:(db + 1) * 48],
                in_=wlr_in[db * 128:(db + 1) * 128, :])),
              writes=("wlr_sb",), dma="gpq", tag="wlrcast")
        P("gpsimd", lambda g: g.dma_start(out=scr_bias[0:HID], in_=gb_in),
          writes=("sbias0",), dma="gpq", tag="bcast0")
        P("gpsimd", lambda g: g.dma_start(out=scr_bias[HID:2 * HID], in_=vb_in),
          writes=("sbias1",), dma="gpq", tag="bcast1")
        P("gpsimd", lambda g: g.dma_start(out=scr_bias[2 * HID:], in_=ob_in),
          writes=("sbias2",), dma="gpq", tag="bcast2")

        P("sync", lambda s: s.dma_start(out=blr2[0:48, 0:1],
                                        in_=blr_in.rearrange("(p o) -> p o", o=1)),
          writes=("blr2",), dma="sync", tag="blr")
        P("vector", lambda v: v.tensor_scalar(blr2[0:48, 0:1], blr2[0:48, 0:1],
                                              BASE_LR_INV, None, OP.add),
          reads=("blr2",), writes=("blr2",), tag="blradd")
        P("sync", lambda s: s.dma_start(out=onw_rowt[0:1, :],
                                        in_=onw_in.rearrange("(o j) -> o j", o=1)),
          writes=("onw_rowt",), dma="sync", tag="onwld")
        P("tensor", lambda t: t.matmul(psB[:, 0:128], ones_rowf[0:1, :], onw_rowt[0:1, :],
                                       start=True, stop=True),
          reads=("ones", "onw_rowt"), writes=("pB0",), tag="onwmm")

        def f_onwbc(v):
            for i in range(4):
                ins = v.tensor_copy(onw_bc[:, i * 128:(i + 1) * 128], psB[:, 0:128])
            return ins
        P("vector", f_onwbc, reads=("pB0",), writes=("onw_bc",), tag="onwbc")

        for wi, (w_in, wkey) in enumerate(((w0_in, "W0c"), (w1_in, "W1c"), (w2_in, "W2c"))):
            WC = (W0c, W1c, W2c)[wi]
            for h in range(H):
                idx = wi * H + h
                P("sync", (lambda s, w_in=w_in, idx=idx: s.dma_start(
                    out=winit[idx % 2][:], in_=w_in[idx % H])),
                  writes=(f"xa{idx % 2}",), dma="sync", tag="wld")
                P("tensor", (lambda t, idx=idx: t.transpose(
                    pA(idx % 2)[:, 0:128], winit[idx % 2][:], identf[:])),
                  reads=(f"xa{idx % 2}", "identf"), writes=(f"pA{idx % 2}",), tag="wtr")
                P("vector", (lambda v, WC=WC, h=h, idx=idx: v.tensor_copy(
                    WC[:, h * 128:(h + 1) * 128], pA(idx % 2)[:, 0:128])),
                  reads=(f"pA{idx % 2}",), writes=(wkey,), tag="wev")

        for wi, (WC, wkey) in enumerate(((W0c, "W0c"), (W1c, "W1c"), (W2c, "W2c"))):
            P("scalar", (lambda a, WC=WC: a.activation(f32a[:], WC[:], AF.Square)),
              reads=(wkey,), writes=("f32a",), tag="wsq")
            P("vector", (lambda v, wi=wi: v.tensor_reduce(
                WN[:, wi * 16:(wi + 1) * 16],
                f32a[:].rearrange("p (a b) -> p a b", b=128), AX.X, OP.add)),
              reads=("f32a",), writes=("WN",), tag="wnred")
        P("scalar", lambda a: a.activation(WN[:, 0:48], WN[:, 0:48], AF.Sqrt),
          reads=("WN",), writes=("WN",), tag="wnsqrt")

        def refresh_u1a():
            P("vector", lambda v: v.tensor_copy(U1a[:], W1c[:]),
              reads=("W1c",), writes=("U1",), tag="u1a")

        def emit_usage():
            for (WC, wkey, U, ukey) in ((W0c, "W0c", U0, "U0"), (W2c, "W2c", U2, "U2"),
                                        (W1c, "W1c", U1b, "U1")):
                for hq in range(4):
                    def ftr(t, WC=WC, hq=hq):
                        for i in range(4):
                            h = hq * 4 + i
                            ins = t.transpose(psB[:, h * 128 - hq * 512:(h + 1) * 128 - hq * 512]
                                              if False else pB(hq)[:, i * 128:(i + 1) * 128],
                                              WC[:, h * 128:(h + 1) * 128], identf[:])
                        return ins
                    P("tensor", ftr, reads=(wkey, "identf"), writes=(f"pB{hq}",), tag="utr")
                def fev(a, U=U):
                    return a.activation(U[:], psB[:], AF.Copy)
                P("scalar", fev, reads=("pB0", "pB1", "pB2", "pB3"), writes=(ukey,), tag="uev")
        emit_usage()
        refresh_u1a()

        bgq = []

        NGT = HID * D // (128 * 512)

        def tile_view(w_hbm):
            r, c = w_hbm.shape
            nb = c // 512
            wv = w_hbm.rearrange("(a p) (b d) -> a p b d", p=128, d=512)
            return lambda i: wv[i // nb][:, i % nb, :]

        def make_gamma_tile(w_hbm, wi, tag):
            tv = tile_view(w_hbm)
            def load(i, sl):
                P("gpsimd", (lambda g, i=i, sl=sl: g.dma_start(
                    out=gstage[sl][:], in_=tv(i))),
                  writes=(f"gsg{sl}",), dma="gpq", tag=f"gld{tag}")
            def comp(i, sl):
                P("scalar", (lambda a, sl=sl, col=wi * NGT + i: a.activation(
                    f32t[sl][:], gstage[sl][:], AF.Abs,
                    accum_out=gacc[:, col:col + 1])),
                  reads=(f"gsg{sl}",), writes=(f"f32t{sl}", "gacc"),
                  tag=f"gab{tag}")
            return load, comp

        def make_tern_tile(w_hbm, scr, wi, tag):
            tv = tile_view(w_hbm)
            sv = tile_view(scr)
            def load(i, sl):
                P("gpsimd", (lambda g, i=i, sl=sl: g.dma_start(
                    out=gstage[sl][:], in_=tv(i))),
                  writes=(f"gsg{sl}",), dma="gpq", tag=f"tld{tag}")
            def comp(i, sl):
                P("scalar", (lambda a, sl=sl: a.activation(
                    f32t[sl][:], gstage[sl][:], AF.Copy, bias=MAGIC,
                    scale=gcol[:, 4 + wi:5 + wi])),
                  reads=(f"gsg{sl}", "gcol"), writes=(f"f32t{sl}",),
                  tag=f"tq1{tag}")
                def f(v, sl=sl):
                    v.tensor_scalar(f32t[sl][:], f32t[sl][:], -MAGIC, -1.0,
                                    OP.add, OP.max)
                    return v.tensor_scalar(tstage[sl][:], f32t[sl][:], 1.0,
                                           None, OP.min)
                P("vector", f, reads=(f"f32t{sl}",),
                  writes=(f"tsg{sl}", f"f32t{sl}"), tag=f"tq2{tag}")
                P("scalar", (lambda s, i=i, sl=sl: s.dma_start(
                    out=sv(i), in_=tstage[sl][:])),
                  reads=(f"tsg{sl}",), writes=(f"{tag}_{i}",), dma="scalq",
                  tag=f"tst{tag}")
            return load, comp

        GAMMA_FIN = "GAMMA_FIN"
        for i in range(NGT):
            bgq.append((make_gamma_tile(gw_in, 0, "g"), i))
        for i in range(NGT):
            bgq.append((make_gamma_tile(vw_in, 1, "v"), i))
        for i in range(NGT):
            bgq.append((make_gamma_tile(ow_in, 2, "o"), i))
        for i in range(NGT):
            bgq.append((make_tern_tile(gw_in, scr_tg, 0, "tg"), i))
        for i in range(NGT):
            bgq.append((make_tern_tile(vw_in, scr_tv, 1, "tv"), i))
        for i in range(NGT):
            bgq.append((make_tern_tile(ow_in, scr_to, 2, "to"), i))
        GAMMA_FIN_AT = 3 * NGT

        def emit_gamma_final():
            for wi in range(3):
                P("vector", (lambda v, wi=wi: v.tensor_reduce(
                    coltmp[:, wi:wi + 1], gacc[:, wi * NGT:(wi + 1) * NGT], AX.X, OP.add)),
                  reads=("gacc",), writes=("coltmp",), tag="gred")
            P("tensor", lambda t: t.matmul(pB(3)[0:1, 0:8], ones_colf[:, 0:1],
                                           coltmp[:, 0:8], start=True, stop=True),
              reads=("coltmp", "ones"), writes=("pB3",), tag="gmm")

            def f(v):
                return v.tensor_scalar(grow[0:1, 0:8], pB(3)[0:1, 0:8], 1.0 / (HID * D), 1e-5,
                                       OP.mult, OP.max)
            P("vector", f, reads=("pB3",), writes=("grow",), tag="gfin")
            P("tensor", lambda t: t.matmul(pB(3)[:, 0:8], ones_rowf[0:1, :], grow[0:1, 0:8],
                                           start=True, stop=True),
              reads=("grow", "ones"), writes=("pB3",), tag="gbc")

            P("scalar", lambda a: a.activation(gcol[:, 0:8], pB(3)[:, 0:8], AF.Copy),
              reads=("pB3",), writes=("gcol",), tag="gcp")
            P("vector", lambda v: v.reciprocal(gcol[:, 4:7], gcol[:, 0:3]),
              reads=("gcol",), writes=("gcol",), tag="ginv")
            P("scalar", lambda a: a.activation(grow[0:1, 8:11], grow[0:1, 0:3], AF.Copy),
              reads=("grow",), writes=("grow",), tag="gbnc")
            P("vector", lambda v: v.reciprocal(grow[0:1, 4:7], grow[0:1, 8:11]),
              reads=("grow",), writes=("grow",), tag="grinv")

        bg_lp = [0]
        bg_cp = [0]
        BG_PF = 3

        def drain_bg(n):
            for _ in range(n):
                if bg_lp[0] >= len(bgq) and bg_cp[0] >= len(bgq):
                    return
                if bg_lp[0] < len(bgq):
                    (ld, _c), i = bgq[bg_lp[0]]
                    ld(i, bg_lp[0] % 4)
                    bg_lp[0] += 1
                if (bg_cp[0] < len(bgq)
                        and (bg_lp[0] - bg_cp[0] > BG_PF
                             or bg_lp[0] >= len(bgq))):
                    if bg_cp[0] == GAMMA_FIN_AT:
                        emit_gamma_final()
                    (_l, cp), i = bgq[bg_cp[0]]
                    cp(i, bg_cp[0] % 4)
                    bg_cp[0] += 1

        def emit_chunk(c):
            c0 = c * S
            for t8 in range(4):
                P("sync", (lambda s, t8=t8: s.dma_start(
                    out=xa[t8 % 2][:], in_=x_in[c0 + t8 * 128:c0 + (t8 + 1) * 128, :])),
                  writes=(f"xa{t8 % 2}",), dma="sync", tag="xld")
                P("scalar", (lambda a, t8=t8: a.activation(
                    f32a[:], xa[t8 % 2][:], AF.Square, accum_out=lnc[:, 1:2])),
                  reads=(f"xa{t8 % 2}",), writes=("f32a", "lnc"), tag="lnsq")
                P("vector", (lambda v, t8=t8: v.reduce_sum(lnc[:, 0:1], xa[t8 % 2][:], AX.X)),
                  reads=(f"xa{t8 % 2}",), writes=("lnc",), tag="lnsum")
                P("gpsimd", lambda g: g.tensor_copy(lnc[:, 20:21], lnc[:, 0:1]),
                  reads=("lnc",), writes=("lnc",), tag="lnecho")

                def f(v):
                    v.tensor_scalar(lnc[:, 2:3], lnc[:, 20:21], -1.0 / D, None, OP.mult)
                    return v.tensor_scalar(lnc[:, 3:4], lnc[:, 1:2], 1.0 / D, None, OP.mult)
                P("vector", f, reads=("lnc",), writes=("lnc",), tag="lnst")
                P("scalar", lambda a: a.activation(lnc[:, 21:22], lnc[:, 2:3], AF.Square),
                  reads=("lnc",), writes=("lnc",), tag="lnmu2")
                P("vector", lambda v: v.tensor_tensor(lnc[:, 4:5], lnc[:, 3:4],
                                                      lnc[:, 21:22], OP.subtract),
                  reads=("lnc",), writes=("lnc",), tag="lnv")
                P("scalar", lambda a: a.activation(lnc[:, 5:6], lnc[:, 4:5], AF.Sqrt,
                                                   bias=1e-5),
                  reads=("lnc",), writes=("lnc",), tag="lnsd")
                P("vector", lambda v: v.reciprocal(lnc[:, 6:7], lnc[:, 5:6]),
                  reads=("lnc",), writes=("lnc",), tag="lnrs")
                P("gpsimd", lambda g: g.tensor_tensor(lnc[:, 7:8], lnc[:, 2:3],
                                                      lnc[:, 6:7], OP.mult),
                  reads=("lnc",), writes=("lnc",), tag="lnb")
                P("scalar", (lambda a, t8=t8: a.activation(
                    ln1tok[:], xa[t8 % 2][:], AF.Identity,
                    bias=lnc[:, 7:8], scale=lnc[:, 6:7])),
                  reads=(f"xa{t8 % 2}", "lnc"), writes=("ln1tok",), tag="lnout")
                P("scalar", (lambda s, t8=t8: s.dma_start(
                    out=scr_ln1[t8 * 128:(t8 + 1) * 128, :], in_=ln1tok[:])),
                  reads=("ln1tok",), writes=(f"sln{t8}",), dma="scalq", tag="lnst8")
                if DEBUG_TAPS and c == 0:
                    P("scalar", (lambda s, t8=t8: s.dma_start(
                        out=scr_dbgc[t8], in_=lnc[:, 0:16])),
                      reads=("lnc",), writes=(f"DBGC{t8}",), dma="scalq", tag="dbgc")

            if STOP_SUB == "ln":
                return
            for dp in range(DP):
                P("sync", (lambda s, dp=dp: s.dma_start(
                    out=ln1_fm[dp][:], in_=scr_ln1[:, dp * 128:(dp + 1) * 128],
                    transpose=True)),
                  reads=("sln0", "sln1", "sln2", "sln3"), writes=(f"lnfm{dp}",),
                  dma="sync", tag="lnflip")

            jb_order = list(range(16, 32)) + list(range(0, 16)) + list(range(32, 48))
            for oi, jb in enumerate(jb_order):
                P("sync", (lambda s, jb=jb, oi=oi: s.dma_start(
                    out=wqbuf[oi % 2][:].rearrange("p (a j) -> p a j", a=DP),
                    in_=scr_wq[jb].transpose([1, 0, 2]))),
                  reads=(f"swq{jb}",), writes=(f"wqb{oi % 2}",), dma="sync", tag="wqld")

                def fmm(t, jb=jb, oi=oi):
                    for dp in range(DP):
                        ins = t.matmul(pA(oi % 4), wqbuf[oi % 2][:, dp * 128:(dp + 1) * 128],
                                       ln1_fm[dp][:], start=(dp == 0), stop=(dp == DP - 1))
                    return ins
                P("tensor", fmm,
                  reads=tuple([f"wqb{oi % 2}"] + [f"lnfm{dp}" for dp in range(DP)]),
                  writes=(f"pA{oi % 4}",), tag=f"qkvmm{jb}")
                P("scalar", (lambda a, oi=oi: a.activation(
                    qkst[oi % 2][:], pA(oi % 4), AF.Silu)),
                  reads=(f"pA{oi % 4}",), writes=(f"qkst{oi % 2}",), tag="qkvsilu")
                kind = 0 if jb < 16 else (1 if jb < 32 else 2)
                if kind != 2:
                    h = jb % 16
                    P("scalar", (lambda a, oi=oi: a.activation(
                        sqst[:], qkst[oi % 2][:], AF.Square)),
                      reads=(f"qkst{oi % 2}",), writes=("sqst",), tag="qsq")
                    P("tensor", (lambda t, oi=oi: t.matmul(
                        pB(2 + oi % 2)[0:1, :], ones_colb[:, 0:1], sqst[:],
                        start=True, stop=True)),
                      reads=("sqst", "ones"), writes=(f"pB{2 + oi % 2}",), tag="qnmm")
                    P("vector", (lambda v, oi=oi: v.tensor_copy(
                        rowb[oi % 2][0:1, :], pB(2 + oi % 2)[0:1, :])),
                      reads=(f"pB{2 + oi % 2}",), writes=(f"rowb{oi % 2}",), tag="qnrow")
                    P("tensor", (lambda t, oi=oi: t.matmul(
                        pB(2 + oi % 2)[:, :], ones_rowb[0:1, :], rowb[oi % 2][0:1, :],
                        start=True, stop=True)),
                      reads=(f"rowb{oi % 2}", "ones"), writes=(f"pB{2 + oi % 2}",), tag="qnex")
                    P("scalar", (lambda a, oi=oi: a.activation(
                        f32c[:], pB(2 + oi % 2)[:, :], AF.Sqrt)),
                      reads=(f"pB{2 + oi % 2}",), writes=("f32c",), tag="qnsqrt")
                    def fqn(v):
                        v.tensor_scalar(f32c[:], f32c[:], 1e-5, None, OP.add)
                        return v.reciprocal(f32c[:], f32c[:])
                    P("vector", fqn, reads=("f32c",), writes=("f32c",), tag="qnrec")
                    P("vector", (lambda v, oi=oi: v.tensor_tensor(
                        qkst[oi % 2][:], qkst[oi % 2][:], f32c[:], OP.mult)),
                      reads=(f"qkst{oi % 2}", "f32c"), writes=(f"qkst{oi % 2}",), tag="qnorm")
                P("scalar", (lambda s, jb=jb, oi=oi: s.dma_start(
                    out=scr_qkv[jb * 128:(jb + 1) * 128, :], in_=qkst[oi % 2][:])),
                  reads=(f"qkst{oi % 2}",), writes=(f"sqkv{jb}",), dma="scalq", tag="qkvst")
                drain_bg(6)

            if STOP_SUB == "qkv":
                return
            def fmm(t):
                for dp in range(DP):
                    ins = t.matmul(pB(2)[0:48, :], wlr_sb[:, dp * 48:(dp + 1) * 48],
                                   ln1_fm[dp][:], start=(dp == 0), stop=(dp == DP - 1))
                return ins
            P("tensor", fmm,
              reads=tuple(["wlr_sb"] + [f"lnfm{dp}" for dp in range(DP)]),
              writes=("pB2",), tag="lrmm")
            P("scalar", lambda a: a.activation(RL[0:48, :], pB(2)[0:48, :], AF.Exp,
                                               bias=blr2[0:48, 0:1]),
              reads=("pB2", "blr2"), writes=("RL0",), tag="lrexp")
            P("scalar", lambda a: a.activation(RL[0:48, :], RL[0:48, :], AF.Ln, bias=1.0),
              reads=("RL0",), writes=("RL0",), tag="lrln")
            for t8 in range(4):
                P("tensor", (lambda t, t8=t8: t.transpose(
                    pB(3)[:, 0:128], RL[:, t8 * 128:(t8 + 1) * 128], identf[:])),
                  reads=("RL0", "identf"), writes=("pB3",), tag="lrtr")
                P("vector", (lambda v, t8=t8: v.tensor_copy(
                    SCLR[:, t8 * 64:t8 * 64 + 48], pB(3)[:, 0:48])),
                  reads=("pB3",), writes=("SCLR",), tag="lrsc")

            if STOP_SUB == "lr":
                return
            for t8 in range(4):
                P("sync", (lambda s, t8=t8: s.dma_start(
                    out=ktok[t8][:], in_=scr_qkv[2048:4096, t8 * 128:(t8 + 1) * 128],
                    transpose=True)),
                  reads=tuple(f"sqkv{16 + h}" for h in range(16)),
                  writes=(f"ktok{t8}",), dma="sync", tag="kflip")
                P("sync", (lambda s, t8=t8: s.dma_start(
                    out=vtok[t8][:], in_=scr_qkv[4096:6144, t8 * 128:(t8 + 1) * 128],
                    transpose=True)),
                  reads=tuple(f"sqkv{32 + h}" for h in range(16)),
                  writes=(f"vtok{t8}",), dma="sync", tag="vflip")

            if STOP_SUB == "flip":
                return
            for h in range(H):
                hs = slice(h * 128, (h + 1) * 128)
                P("sync", (lambda s, h=h: s.dma_start(
                    out=kqst[h % 2][:], in_=scr_qkv[(16 + h) * 128:(17 + h) * 128, :])),
                  reads=(f"sqkv{16 + h}",), writes=(f"kqst{h % 2}",), dma="sync", tag="krld")
                P("sync", (lambda s, h=h: s.dma_start(
                    out=vst[h % 2][:], in_=scr_qkv[(32 + h) * 128:(33 + h) * 128, :])),
                  reads=(f"sqkv{32 + h}",), writes=(f"vst{h % 2}",), dma="sync", tag="vrld")

                slg = h % 2
                slh = 2 + h % 2
                sld = h % 2

                def fmm(t, h=h, hs=hs, slg=slg, slh=slh, sld=sld):
                    for t8 in range(4):
                        t.matmul(pA(slg)[:, t8 * 128:(t8 + 1) * 128],
                                 kqst[h % 2][:, t8 * 128:(t8 + 1) * 128], U0[:, hs],
                                 start=True, stop=True)
                    for t8 in range(4):
                        t.matmul(pA(slh)[:, t8 * 128:(t8 + 1) * 128],
                                 kqst[h % 2][:, t8 * 128:(t8 + 1) * 128], U2[:, hs],
                                 start=True, stop=True)
                    for t8 in range(4):
                        ins = t.matmul(pB(sld)[:, t8 * 128:(t8 + 1) * 128],
                                       vst[h % 2][:, t8 * 128:(t8 + 1) * 128], U1a[:, hs],
                                       start=True, stop=True)
                    return ins
                P("tensor", fmm,
                  reads=(f"kqst{h % 2}", f"vst{h % 2}", "U0", "U2", "U1"),
                  writes=(f"pA{slg}", f"pA{slh}", f"pB{sld}"), tag=f"fwdk{h}")

                P("scalar", (lambda a, slg=slg: a.activation(ew[0][:], pA(slg), AF.Copy)),
                  reads=(f"pA{slg}",), writes=("ew0",), tag="evg")
                P("scalar", (lambda a, slh=slh: a.activation(ew[1][:], pA(slh), AF.Copy)),
                  reads=(f"pA{slh}",), writes=("ew1",), tag="evh")
                P("scalar", (lambda a, sld=sld: a.activation(ew[2][:], pB(sld), AF.Copy)),
                  reads=(f"pB{sld}",), writes=("ew2",), tag="evd")
                P("scalar", lambda a: a.activation(ew[3][:], ew[0][:], AF.Silu),
                  reads=("ew0",), writes=("ew3",), tag="sg")
                P("scalar", lambda a: a.activation(ew[4][:], ew[0][:], AF.Sigmoid),
                  reads=("ew0",), writes=("ew4",), tag="sig")

                def fv1(v, h=h):
                    v.tensor_tensor(ew[5][:], ew[3][:], ew[1][:], OP.mult)
                    for t8 in range(4):
                        v.tensor_scalar(hidn_h[:, t8 * 128:(t8 + 1) * 128],
                                        ew[5][:, t8 * 128:(t8 + 1) * 128],
                                        SCLR[:, t8 * 64 + 16 + h:t8 * 64 + 17 + h], None, OP.mult)
                    v.tensor_tensor(ew[5][:], ew[2][:], ew[3][:], OP.mult)
                    for t8 in range(4):
                        v.tensor_scalar(dhbm_h[:, t8 * 128:(t8 + 1) * 128],
                                        ew[5][:, t8 * 128:(t8 + 1) * 128],
                                        SCLR[:, t8 * 64 + 32 + h:t8 * 64 + 33 + h], None, OP.mult)
                    v.tensor_tensor(ew[2][:], ew[2][:], ew[1][:], OP.mult)
                    v.tensor_tensor(ew[1][:], ew[0][:], ew[4][:], OP.mult)
                    v.scalar_tensor_tensor(ew[0][:], ew[0][:], 1.0, ew[1][:], OP.add, OP.subtract)
                    v.tensor_tensor(ew[2][:], ew[2][:], ew[4][:], OP.mult)
                    v.tensor_tensor(ew[2][:], ew[2][:], ew[0][:], OP.mult)
                    for t8 in range(4):
                        ins = v.tensor_scalar(dgba_h[:, t8 * 128:(t8 + 1) * 128],
                                              ew[2][:, t8 * 128:(t8 + 1) * 128],
                                              SCLR[:, t8 * 64 + h:t8 * 64 + 1 + h], None, OP.mult)
                    return ins
                P("vector", fv1,
                  reads=("ew0", "ew1", "ew2", "ew3", "ew4", "SCLR"),
                  writes=("ew0", "ew1", "ew2", "ew5", "hidn_h", "dhbm_h", "dgba_h"),
                  tag=f"ttew{h}")

                gslot = 2 + h % 2

                def fg(t, h=h, hs=hs, gslot=gslot):
                    for t8 in range(4):
                        ts8 = slice(t8 * 128, (t8 + 1) * 128)
                        t.matmul(pB(gslot)[:, 0:128], dgba_h[:, ts8], ktok[t8][:, hs],
                                 start=(t8 == 0), stop=(t8 == 3))
                    for t8 in range(4):
                        ts8 = slice(t8 * 128, (t8 + 1) * 128)
                        t.matmul(pB(gslot)[:, 128:256], vtok[t8][:, hs], hidn_h[:, ts8],
                                 start=(t8 == 0), stop=(t8 == 3))
                    for t8 in range(4):
                        ts8 = slice(t8 * 128, (t8 + 1) * 128)
                        ins = t.matmul(pB(gslot)[:, 256:384], dhbm_h[:, ts8], ktok[t8][:, hs],
                                       start=(t8 == 0), stop=(t8 == 3))
                    return ins
                P("tensor", fg,
                  reads=tuple(["dgba_h", "hidn_h", "dhbm_h"]
                              + [f"ktok{i}" for i in range(4)]
                              + [f"vtok{i}" for i in range(4)]),
                  writes=(f"pB{gslot}",), tag=f"gmm{h}")

                def fge(v, h=h, hs=hs, gslot=gslot):
                    v.tensor_copy(Xns[0][:, hs], pB(gslot)[:, 0:128])
                    v.tensor_copy(Xns[1][:, hs], pB(gslot)[:, 128:256])
                    return v.tensor_copy(Xns[2][:, hs], pB(gslot)[:, 256:384])
                P("vector", fge, reads=(f"pB{gslot}",),
                  writes=("Xns0", "Xns1", "Xns2"), tag=f"gev{h}")
                if h % 4 == 3:
                    drain_bg(8)

            if STOP_SUB == "fwd":
                return
            for w in range(3):
                xk = f"Xns{w}"
                P("scalar", (lambda a, w=w: a.activation(f32a[:], Xns[w][:], AF.Square)),
                  reads=(xk,), writes=("f32a",), tag="nsq")
                P("vector", lambda v: v.tensor_reduce(
                    coltmp[:, 0:16], f32a[:].rearrange("p (a b) -> p a b", b=128),
                    AX.X, OP.add),
                  reads=("f32a",), writes=("coltmp",), tag="nsred")
                P("scalar", lambda a: a.activation(ew[5][:, 0:16], coltmp[:, 0:16], AF.Copy),
                  reads=("coltmp",), writes=("ew5",), tag="nscast")
                P("tensor", lambda t: t.matmul(pB(3)[0:1, 0:16], ones_colb[:, 0:1],
                                               ew[5][:, 0:16], start=True, stop=True),
                  reads=("ew5", "ones"), writes=("pB3",), tag="nsfmm")
                P("scalar", lambda a: a.activation(f32c[0:1, 0:16], pB(3)[0:1, 0:16],
                                                   AF.Sqrt, bias=1e-14),
                  reads=("pB3",), writes=("f32c",), tag="nsfsq")
                P("vector", lambda v: v.reciprocal(f32c[0:1, 16:32], f32c[0:1, 0:16]),
                  reads=("f32c",), writes=("f32c",), tag="nsfrec")
                P("scalar", lambda a: a.activation(rowb[2][0:1, 0:16], f32c[0:1, 16:32],
                                                   AF.Copy),
                  reads=("f32c",), writes=("rowb2",), tag="nsfrow")
                P("tensor", lambda t: t.matmul(pB(3)[:, 0:16], ones_rowb[0:1, :],
                                               rowb[2][0:1, 0:16], start=True, stop=True),
                  reads=("rowb2", "ones"), writes=("pB3",), tag="nsfex")
                P("scalar", lambda a: a.activation(coltmp[:, 0:16], pB(3)[:, 0:16], AF.Copy),
                  reads=("pB3",), writes=("coltmp",), tag="nsfcol")

                def fsc(v, w=w):
                    for h in range(H):
                        ins = v.tensor_scalar(Xns[w][:, h * 128:(h + 1) * 128],
                                              Xns[w][:, h * 128:(h + 1) * 128],
                                              coltmp[:, h:h + 1], None, OP.mult)
                    return ins
                P("vector", fsc, reads=(xk, "coltmp"), writes=(xk,), tag="nsscale")

                for it in range(2):
                    P("scalar", (lambda a, w=w: a.activation(f32a[:], Xns[w][:], AF.Copy)),
                      reads=(xk,), writes=("f32a",), tag="nscp")

                    def ftr(t):
                        for h in range(H):
                            ins = t.transpose(
                                psA[:, h * 128:(h + 1) * 128],
                                f32a[:, h * 128:(h + 1) * 128], identf[:])
                        return ins
                    P("tensor", ftr, reads=("f32a", "identf"),
                      writes=("pA0", "pA1", "pA2", "pA3"), tag="nstr")
                    P("vector", lambda v: v.tensor_copy(Tt_ns[:], psA[:]),
                      reads=("pA0", "pA1", "pA2", "pA3"), writes=("Tt",), tag="nstte")

                    def fa(t, w=w):
                        for h in range(H):
                            hs2 = slice(h * 128, (h + 1) * 128)
                            ins = t.matmul(psB[:, hs2], Xns[w][:, hs2], Xns[w][:, hs2],
                                           start=True, stop=True)
                        return ins
                    P("tensor", fa, reads=(xk,),
                      writes=("pB0", "pB1", "pB2", "pB3"), tag="nsA")
                    P("vector", lambda v: v.tensor_copy(A_ns[:], psB[:]),
                      reads=("pB0", "pB1", "pB2", "pB3"), writes=("A_ns",), tag="nsAe")

                    def fa2(t):
                        for h in range(H):
                            hs2 = slice(h * 128, (h + 1) * 128)
                            ins = t.matmul(psA[:, hs2], A_ns[:, hs2], A_ns[:, hs2],
                                           start=True, stop=True)
                        return ins
                    P("tensor", fa2, reads=("A_ns",),
                      writes=("pA0", "pA1", "pA2", "pA3"), tag="nsA2")
                    P("scalar", lambda a: a.activation(B_ns[:], psA[:], AF.Copy, scale=NS_C),
                      reads=("pA0", "pA1", "pA2", "pA3"), writes=("B_ns",), tag="nsA2e")
                    P("vector", lambda v: v.scalar_tensor_tensor(
                        B_ns[:], A_ns[:], NS_B, B_ns[:], OP.mult, OP.add),
                      reads=("A_ns", "B_ns"), writes=("B_ns",), tag="nsB")

                    def fxb(t):
                        for h in range(H):
                            hs2 = slice(h * 128, (h + 1) * 128)
                            ins = t.matmul(psB[:, hs2], Tt_ns[:, hs2], B_ns[:, hs2],
                                           start=True, stop=True)
                        return ins
                    P("tensor", fxb, reads=("Tt", "B_ns"),
                      writes=("pB0", "pB1", "pB2", "pB3"), tag="nsXB")
                    P("vector", (lambda v, w=w: v.scalar_tensor_tensor(
                        Xns[w][:], Xns[w][:], NS_A, psB[:], OP.mult, OP.add)),
                      reads=(xk, "pB0", "pB1", "pB2", "pB3"), writes=(xk,), tag="nsX")

                WC, wkey = ((W0c, "W0c"), (W1c, "W1c"), (W2c, "W2c"))[w]
                P("vector", (lambda v, WC=WC, w=w: v.tensor_tensor(
                    WC[:], WC[:], Xns[w][:], OP.add)),
                  reads=(wkey, xk), writes=(wkey,), tag="wup")
                P("scalar", (lambda a, WC=WC: a.activation(f32a[:], WC[:], AF.Square)),
                  reads=(wkey,), writes=("f32a",), tag="wnsq")
                P("vector", lambda v: v.tensor_reduce(
                    coltmp[:, 0:16], f32a[:].rearrange("p (a b) -> p a b", b=128),
                    AX.X, OP.add),
                  reads=("f32a",), writes=("coltmp",), tag="wnred")
                P("scalar", lambda a: a.activation(coltmp[:, 16:32], coltmp[:, 0:16],
                                                   AF.Sqrt, bias=1e-10),
                  reads=("coltmp",), writes=("coltmp",), tag="wnsq2")
                P("vector", lambda v: v.reciprocal(coltmp[:, 0:16], coltmp[:, 16:32]),
                  reads=("coltmp",), writes=("coltmp",), tag="wnrec")
                P("gpsimd", (lambda g, w=w: g.tensor_tensor(
                    coltmp[:, 16:32], coltmp[:, 0:16],
                    WN[:, w * 16:(w + 1) * 16], OP.mult)),
                  reads=("coltmp", "WN"), writes=("coltmp",), tag="wnsc")

                def fws(v, WC=WC):
                    for h in range(H):
                        ins = v.tensor_scalar(WC[:, h * 128:(h + 1) * 128],
                                              WC[:, h * 128:(h + 1) * 128],
                                              coltmp[:, 16 + h:17 + h], None, OP.mult)
                    return ins
                P("vector", fws, reads=(wkey, "coltmp"), writes=(wkey,), tag="wsc")

            emit_usage()
            drain_bg(8)

            if STOP_SUB == "ns":
                return
            for h in range(H):
                hs = slice(h * 128, (h + 1) * 128)
                P("sync", (lambda s, h=h: s.dma_start(
                    out=kqst[h % 2][:], in_=scr_qkv[h * 128:(h + 1) * 128, :])),
                  reads=(f"sqkv{h}",), writes=(f"kqst{h % 2}",), dma="sync", tag="qrld")
                slg = h % 2
                slh = 2 + h % 2

                def fmm(t, h=h, hs=hs, slg=slg, slh=slh):
                    for t8 in range(4):
                        t.matmul(pA(slg)[:, t8 * 128:(t8 + 1) * 128],
                                 kqst[h % 2][:, t8 * 128:(t8 + 1) * 128], U0[:, hs],
                                 start=True, stop=True)
                    for t8 in range(4):
                        ins = t.matmul(pA(slh)[:, t8 * 128:(t8 + 1) * 128],
                                       kqst[h % 2][:, t8 * 128:(t8 + 1) * 128], U2[:, hs],
                                       start=True, stop=True)
                    return ins
                P("tensor", fmm, reads=(f"kqst{h % 2}", "U0", "U2"),
                  writes=(f"pA{slg}", f"pA{slh}"), tag=f"fwdq{h}")
                P("scalar", (lambda a, slg=slg: a.activation(ew[0][:], pA(slg), AF.Silu)),
                  reads=(f"pA{slg}",), writes=("ew0",), tag="qsilu")
                P("scalar", (lambda a, slh=slh: a.activation(ew[1][:], pA(slh), AF.Copy)),
                  reads=(f"pA{slh}",), writes=("ew1",), tag="qcp")
                P("vector", lambda v: v.tensor_tensor(f32c[:], ew[0][:], ew[1][:], OP.mult),
                  reads=("ew0", "ew1"), writes=("f32c",), tag="mtok")

                def ftr(t):
                    for t8 in range(4):
                        ins = t.transpose(pB(2)[:, t8 * 128:(t8 + 1) * 128],
                                          f32c[:, t8 * 128:(t8 + 1) * 128], identf[:])
                    return ins
                P("tensor", ftr, reads=("f32c", "identf"), writes=("pB2",), tag="mtr")
                P("scalar", lambda a: a.activation(mfm[:], pB(2), AF.Copy),
                  reads=("pB2",), writes=("mfm",), tag="mev")

                def fo(t, hs=hs):
                    for t8 in range(4):
                        ins = t.matmul(pB(3)[:, t8 * 128:(t8 + 1) * 128],
                                       mfm[:, t8 * 128:(t8 + 1) * 128], U1b[:, hs],
                                       start=True, stop=True)
                    return ins
                P("tensor", fo, reads=("mfm", "U1"), writes=("pB3",), tag=f"omm{h}")
                P("scalar", lambda a: a.activation(f32c[:], pB(3), AF.Square),
                  reads=("pB3",), writes=("f32c",), tag="osq")
                P("vector", lambda v: v.tensor_reduce(
                    coltp2[:, 0:4], f32c[:].rearrange("p (a b) -> p a b", b=128),
                    AX.X, OP.add),
                  reads=("f32c",), writes=("coltp2",), tag="ored")
                P("scalar", lambda a: a.activation(coltp2[:, 4:8], coltp2[:, 0:4],
                                                   AF.Sqrt, bias=1e-5, scale=1.0 / 128),
                  reads=("coltp2",), writes=("coltp2",), tag="orsd")
                P("vector", lambda v: v.reciprocal(coltp2[:, 8:12], coltp2[:, 4:8]),
                  reads=("coltp2",), writes=("coltp2",), tag="orsq")

                def fsc(a):
                    for t8 in range(4):
                        ins = a.activation(attn_st[:, t8 * 128:(t8 + 1) * 128],
                                           pB(3)[:, t8 * 128:(t8 + 1) * 128],
                                           AF.Copy, scale=coltp2[:, 8 + t8:9 + t8])
                    return ins
                P("scalar", fsc, reads=("pB3", "coltp2"),
                  writes=("attn_st",), tag="oscale")
                P("vector", lambda v: v.tensor_tensor(attn_st[:], attn_st[:], onw_bc[:],
                                                      OP.mult),
                  reads=("attn_st", "onw_bc"), writes=("attn_st",), tag="oscl2")
                P("scalar", (lambda s, h=h: s.dma_start(
                    out=scr_attn[h].transpose([1, 0, 2]),
                    in_=attn_st[:].rearrange("p (a j) -> p a j", a=4))),
                  reads=("attn_st",), writes=(f"sattn{h}",), dma="scalq", tag="attnst")
                if h % 4 == 3:
                    drain_bg(8)

            if STOP_SUB == "q":
                return
            refresh_u1a()

            for tp in range(2):
                t8a, t8b = 2 * tp, 2 * tp + 1
                for dp in range(DP):
                    P("sync", (lambda s, dp=dp: s.dma_start(
                        out=afm[dp % 2][:], in_=scr_attn[dp].rearrange("a p j -> (a p) j"),
                        transpose=True)),
                      reads=(f"sattn{dp}",), writes=(f"afm{dp % 2}",), dma="sync", tag="aflip")
                    P("sync", (lambda s, dp=dp: s.dma_start(
                        out=wpbuf[dp % 2][:], in_=scr_wp[dp])),
                      reads=(f"swp{dp}",), writes=(f"wpb{dp % 2}",), dma="sync", tag="wpld")

                    def fmm(t, dp=dp, t8a=t8a, t8b=t8b):
                        for do in range(4):
                            t.matmul(pA(do), afm[dp % 2][:, t8a * 128:(t8a + 1) * 128],
                                     wpbuf[dp % 2][:, do * 512:(do + 1) * 512],
                                     start=(dp == 0), stop=(dp == DP - 1))
                        for do in range(4):
                            ins = t.matmul(pB(do), afm[dp % 2][:, t8b * 128:(t8b + 1) * 128],
                                           wpbuf[dp % 2][:, do * 512:(do + 1) * 512],
                                           start=(dp == 0), stop=(dp == DP - 1))
                        return ins
                    P("tensor", fmm, reads=(f"afm{dp % 2}", f"wpb{dp % 2}"),
                      writes=("pA0", "pA1", "pA2", "pA3", "pB0", "pB1", "pB2", "pB3"),
                      tag=f"projmm{tp}_{dp}")
                for (t8, ps, pkeys) in ((t8a, psA, ("pA0", "pA1", "pA2", "pA3")),
                                        (t8b, psB, ("pB0", "pB1", "pB2", "pB3"))):
                    gt8 = c * 4 + t8
                    P("sync", (lambda s, t8=t8: s.dma_start(
                        out=xa[t8 % 2][:], in_=x_in[c0 + t8 * 128:c0 + (t8 + 1) * 128, :])),
                      writes=(f"xa{t8 % 2}",), dma="sync", tag="xrld")
                    P("vector", (lambda v, ps=ps, t8=t8: v.scalar_tensor_tensor(
                        xa[t8 % 2][:], ps[:], 1.0, xa[t8 % 2][:], OP.mult, OP.add)),
                      reads=pkeys + (f"xa{t8 % 2}",), writes=(f"xa{t8 % 2}",), tag="resadd")
                    P("scalar", (lambda s, t8=t8: s.dma_start(
                        out=scr_x1[c0 + t8 * 128:c0 + (t8 + 1) * 128, :], in_=xa[t8 % 2][:])),
                      reads=(f"xa{t8 % 2}",), writes=(f"sx1_{gt8}",), dma="scalq", tag="x1st")
                    P("scalar", (lambda a, t8=t8: a.activation(
                        f32a[:], xa[t8 % 2][:], AF.Square, accum_out=lnc[:, 1:2])),
                      reads=(f"xa{t8 % 2}",), writes=("f32a", "lnc"), tag="l2sq")

                    def f(v, t8=t8):
                        v.reduce_sum(lnc[:, 0:1], xa[t8 % 2][:], AX.X)
                        v.reduce_max(lnc[:, 8:9], xa[t8 % 2][:], AX.X)
                        return v.tensor_reduce(lnc[:, 9:10], xa[t8 % 2][:], AX.X, OP.min)
                    P("vector", f, reads=(f"xa{t8 % 2}",), writes=("lnc",), tag="l2red")

                    def f(g):
                        g.tensor_copy(lnc[:, 20:21], lnc[:, 0:1])
                        g.tensor_copy(lnc[:, 21:22], lnc[:, 8:9])
                        return g.tensor_copy(lnc[:, 22:23], lnc[:, 9:10])
                    P("gpsimd", f, reads=("lnc",), writes=("lnc",), tag="l2echo")

                    def f(v):
                        v.tensor_scalar(lnc[:, 2:3], lnc[:, 20:21], -1.0 / D, None, OP.mult)
                        return v.tensor_scalar(lnc[:, 3:4], lnc[:, 1:2], 1.0 / D, None, OP.mult)
                    P("vector", f, reads=("lnc",), writes=("lnc",), tag="l2st")
                    P("scalar", lambda a: a.activation(lnc[:, 23:24], lnc[:, 2:3], AF.Square),
                      reads=("lnc",), writes=("lnc",), tag="l2mu2")

                    def f(v):
                        v.tensor_tensor(lnc[:, 4:5], lnc[:, 3:4], lnc[:, 23:24], OP.subtract)
                        v.tensor_tensor(lnc[:, 10:11], lnc[:, 21:22], lnc[:, 2:3], OP.add)
                        return v.scalar_tensor_tensor(lnc[:, 11:12], lnc[:, 2:3], -1.0,
                                                      lnc[:, 22:23], OP.mult, OP.subtract)
                    P("vector", f, reads=("lnc",), writes=("lnc",), tag="l2v")
                    P("scalar", lambda a: a.activation(lnc[:, 5:6], lnc[:, 4:5], AF.Sqrt,
                                                       bias=1e-5),
                      reads=("lnc",), writes=("lnc",), tag="l2sd")
                    def f(v):
                        v.reciprocal(lnc[:, 6:7], lnc[:, 5:6])
                        return v.tensor_tensor(lnc[:, 12:13], lnc[:, 10:11],
                                               lnc[:, 11:12], OP.max)
                    P("vector", f, reads=("lnc",), writes=("lnc",), tag="l2rs")
                    P("gpsimd", lambda g: g.tensor_tensor(lnc[:, 13:14], lnc[:, 6:7],
                                                          lnc[:, 12:13], OP.mult),
                      reads=("lnc",), writes=("lnc",), tag="l2gr")
                    P("vector", lambda v: v.tensor_scalar(lnc[:, 14:15], lnc[:, 13:14],
                                                          1e-5, None, OP.max),
                      reads=("lnc",), writes=("lnc",), tag="l2gx")

                    def f(a, gt8=gt8):
                        a.activation(scl[:, gt8:gt8 + 1], lnc[:, 14:15], AF.Identity,
                                     scale=1.0 / 127)
                        return a.activation(lnc[:, 24:25], lnc[:, 14:15], AF.Identity)
                    P("scalar", f, reads=("lnc",), writes=("lnc", "scl"), tag="l2sc")
                    P("vector", lambda v: v.reciprocal(lnc[:, 15:16], lnc[:, 24:25]),
                      reads=("lnc",), writes=("lnc",), tag="l2ig")
                    P("scalar", (lambda a, gt8=gt8: a.activation(
                        scl[:, 16 + gt8:17 + gt8], lnc[:, 15:16], AF.Identity, scale=127.0)),
                      reads=("lnc",), writes=("scl",), tag="l2sx")
                    P("gpsimd", lambda g: g.tensor_tensor(lnc[:, 16:17], lnc[:, 6:7],
                                                          lnc[:, 15:16], OP.mult),
                      reads=("lnc",), writes=("lnc",), tag="l2ra")
                    P("vector", lambda v: v.tensor_scalar(lnc[:, 17:18], lnc[:, 16:17],
                                                          127.0, None, OP.mult),
                      reads=("lnc",), writes=("lnc",), tag="l2a")
                    P("gpsimd", lambda g: g.tensor_tensor(lnc[:, 18:19], lnc[:, 2:3],
                                                          lnc[:, 17:18], OP.mult),
                      reads=("lnc",), writes=("lnc",), tag="l2b")
                    P("scalar", (lambda a, t8=t8: a.activation(
                        f32a[:], xa[t8 % 2][:], AF.Identity, bias=lnc[:, 18:19],
                        scale=lnc[:, 17:18])),
                      reads=(f"xa{t8 % 2}", "lnc"), writes=("f32a",), tag="l2aff")

                    def f(v):
                        v.tensor_scalar(f32a[:], f32a[:], MAGIC, -MAGIC, OP.add, OP.add)
                        return v.tensor_scalar(ln1tok[:], f32a[:], -128.0, 127.0, OP.max, OP.min)
                    P("vector", f, reads=("f32a",), writes=("f32a", "ln1tok"), tag="l2rnd")
                    P("scalar", (lambda s, t8=t8: s.dma_start(
                        out=scr_x2q[c0 + t8 * 128:c0 + (t8 + 1) * 128, :], in_=ln1tok[:])),
                      reads=("ln1tok",), writes=(f"sx2q_{gt8}",), dma="scalq", tag="x2qst")
                drain_bg(8)

        def emit_ffn_group(g):
            g0 = g * S
            for dp in range(DP):
                P("sync", (lambda s, dp=dp, g0=g0: s.dma_start(
                    out=x2qfm[dp][:], in_=scr_x2q[g0:g0 + S, dp * 128:(dp + 1) * 128],
                    transpose=True)),
                  reads=tuple(f"sx2q_{g * 4 + i}" for i in range(4)),
                  writes=(f"x2qfm{dp}",), dma="sync", tag="x2flip")

            for t8 in range(4):
                P("tensor", (lambda t, t8=t8, g=g: t.transpose(
                    pB(3)[0:1, t8 * 128:(t8 + 1) * 128],
                    scl[:, 16 + g * 4 + t8:17 + g * 4 + t8], identf[:])),
                  reads=("scl", "identf"), writes=("pB3",), tag="sxtr")
            P("scalar", lambda a: a.activation(rowfP[0:1, :], pB(3)[0:1, 0:512], AF.Copy),
              reads=("pB3",), writes=("rowf0",), tag="sxrow")

            def f(v):
                v.tensor_scalar(rowb[0][0:1, :], rowfP[0:1, :], grow[0:1, 4:5], None,
                                OP.mult)
                return v.tensor_scalar(rowb[1][0:1, :], rowfP[0:1, :], grow[0:1, 5:6],
                                       None, OP.mult)
            P("vector", f, reads=("rowf0", "grow"),
              writes=("rowb0", "rowb1"), tag="invscg")
            def f(v, g=g):
                for t8 in range(4):
                    v.tensor_tensor(coltmp[:, t8:t8 + 1], gcol[:, 0:1],
                                    scl[:, g * 4 + t8:g * 4 + t8 + 1], OP.mult)
                    ins = v.tensor_tensor(coltmp[:, 4 + t8:5 + t8], gcol[:, 1:2],
                                          scl[:, g * 4 + t8:g * 4 + t8 + 1], OP.mult)
                return ins
            P("vector", f, reads=("gcol", "scl"), writes=("coltmp",), tag="scgv")

            for ht in range(16):
                hrows = slice(ht * 512, (ht + 1) * 512)
                for dp in range(DP):
                    P("sync", (lambda s, ht=ht, dp=dp: s.dma_start(
                        out=gvsetG[:, dp * 512:(dp + 1) * 512],
                        in_=scr_tg[ht * 512:(ht + 1) * 512, dp * 128:(dp + 1) * 128],
                        transpose=True)),
                      reads=tuple(f"tg_{(ht * 4 + r) * 2 + dp // 8}" for r in range(4)),
                      writes=(f"gvsetG{dp}",), dma="sync", tag="gldG")
                    P("sync", (lambda s, ht=ht, dp=dp: s.dma_start(
                        out=gvsetV[:, dp * 512:(dp + 1) * 512],
                        in_=scr_tv[ht * 512:(ht + 1) * 512, dp * 128:(dp + 1) * 128],
                        transpose=True)),
                      reads=tuple(f"tv_{(ht * 4 + r) * 2 + dp // 8}" for r in range(4)),
                      writes=(f"gvsetV{dp}",), dma="sync", tag="gldV")
                P("sync", (lambda s, ht=ht: s.dma_start(
                    out=brow[0][0:1, :],
                    in_=scr_bias[ht * 512:(ht + 1) * 512].rearrange("(o j) -> o j", o=1))),
                  reads=("sbias0",), writes=("brow0",), dma="sync", tag="gbld")
                P("sync", (lambda s, ht=ht: s.dma_start(
                    out=brow[1][0:1, :],
                    in_=scr_bias[HID + ht * 512:HID + (ht + 1) * 512].rearrange("(o j) -> o j", o=1))),
                  reads=("sbias1",), writes=("brow1",), dma="sync", tag="vbld")

                for dp in range(DP):
                    def fmmdp(t, dp=dp):
                        for t8 in range(4):
                            t.matmul(pA(t8), x2qfm[dp][:, t8 * 128:(t8 + 1) * 128],
                                     gvsetG[:, dp * 512:(dp + 1) * 512],
                                     start=(dp == 0), stop=False)
                        for t8 in range(4):
                            ins = t.matmul(pB(t8), x2qfm[dp][:, t8 * 128:(t8 + 1) * 128],
                                           gvsetV[:, dp * 512:(dp + 1) * 512],
                                           start=(dp == 0), stop=False)
                        return ins
                    P("tensor", fmmdp,
                      reads=(f"x2qfm{dp}", f"gvsetG{dp}", f"gvsetV{dp}"),
                      writes=("pA0", "pA1", "pA2", "pA3",
                              "pB0", "pB1", "pB2", "pB3"),
                      tag=f"ffnmm{g}_{ht}_{dp}")

                def fmmb(t):
                    for t8 in range(4):
                        t.matmul(pA(t8), rowb[0][0:1, t8 * 128:(t8 + 1) * 128],
                                 brow[0][0:1, :], start=False, stop=True)
                        ins = t.matmul(pB(t8), rowb[1][0:1, t8 * 128:(t8 + 1) * 128],
                                       brow[1][0:1, :], start=False, stop=True)
                    return ins
                P("tensor", fmmb,
                  reads=("brow0", "brow1", "rowb0", "rowb1"),
                  writes=("pA0", "pA1", "pA2", "pA3", "pB0", "pB1", "pB2", "pB3"),
                  tag=f"ffnmmb{g}_{ht}")

                for t8 in range(4):
                    P("scalar", (lambda a, t8=t8: a.activation(
                        ffnew[0][:], pA(t8), AF.Silu, scale=coltmp[:, t8:t8 + 1])),
                      reads=(f"pA{t8}", "coltmp"), writes=("ffnew0",), tag="ffsilu")
                    P("scalar", (lambda a, t8=t8: a.activation(
                        ffnew[1][:], pB(t8), AF.Copy, scale=coltmp[:, 4 + t8:5 + t8])),
                      reads=(f"pB{t8}", "coltmp"), writes=("ffnew1",), tag="ffval")
                    P("vector", (lambda v, t8=t8, ht=ht: v.tensor_tensor(
                        ygrp[t8][:, ht * 512:(ht + 1) * 512], ffnew[0][:], ffnew[1][:], OP.mult)),
                      reads=("ffnew0", "ffnew1"), writes=(f"ygrp{t8}",), tag="ffy")
                if ht % 4 == 3:
                    drain_bg(1)

            for t8 in range(4):
                P("vector", (lambda v, t8=t8: v.reduce_max(
                    coltp2[:, t8:t8 + 1], ygrp[t8][:], AX.X, apply_absolute_value=True)),
                  reads=(f"ygrp{t8}",), writes=("coltp2",), tag="gymax")
            P("gpsimd", lambda g: g.tensor_copy(coltp2[:, 16:20], coltp2[:, 0:4]),
              reads=("coltp2",), writes=("coltp2",), tag="gyecho")

            P("vector", lambda v: v.tensor_scalar(coltp2[:, 0:4], coltp2[:, 16:20],
                                                  1e-5, None, OP.max),
              reads=("coltp2",), writes=("coltp2",), tag="gyclip")

            def f(a):
                a.activation(scl[:, 32:36], coltp2[:, 0:4], AF.Identity, scale=1.0 / 127)
                return a.activation(coltp2[:, 20:24], coltp2[:, 0:4], AF.Identity)
            P("scalar", f, reads=("coltp2",), writes=("coltp2", "scl"), tag="gysc")
            P("vector", lambda v: v.reciprocal(coltp2[:, 24:28], coltp2[:, 20:24]),
              reads=("coltp2",), writes=("coltp2",), tag="gyrec")
            P("gpsimd", lambda g: g.tensor_tensor(coltp2[:, 8:12], coltp2[:, 24:28],
                                                  cst[:, 0:4], OP.mult),
              reads=("coltp2", "cst"), writes=("coltp2",), tag="gys")

            for t8 in range(4):
                for half in range(4):
                    hsl = slice(half * 2048, (half + 1) * 2048)
                    P("scalar", (lambda a, t8=t8, hsl=hsl: a.activation(
                        f32a[:], ygrp[t8][:, hsl], AF.Copy, bias=MAGIC,
                        scale=coltp2[:, 8 + t8:9 + t8])),
                      reads=(f"ygrp{t8}", "coltp2"), writes=("f32a",), tag="yq1")

                    def f(v, t8=t8, hsl=hsl):
                        v.tensor_scalar(f32a[:], f32a[:], -MAGIC, -128.0, OP.add, OP.max)
                        return v.tensor_scalar(ygrp[t8][:, hsl], f32a[:], 127.0, None, OP.min)
                    P("vector", f, reads=("f32a",), writes=(f"ygrp{t8}", "f32a"), tag="yq2")
                P("scalar", (lambda s, t8=t8: s.dma_start(
                    out=scr_y[t8 * 128:(t8 + 1) * 128, :], in_=ygrp[t8][:])),
                  reads=(f"ygrp{t8}",), writes=(f"sy{t8}",), dma="scalq", tag="yst")

            for t8 in range(4):
                P("tensor", (lambda t, t8=t8: t.transpose(
                    pB(3)[0:1, t8 * 128:(t8 + 1) * 128],
                    coltp2[:, 8 + t8:9 + t8], identf[:])),
                  reads=("coltp2", "identf"), writes=("pB3",), tag="sytr")
            P("scalar", lambda a: a.activation(rowfP[0:1, :], pB(3)[0:1, 0:512], AF.Copy),
              reads=("pB3",), writes=("rowf0",), tag="syrow")

            def f(v):
                v.tensor_scalar(rowb[2][0:1, :], rowfP[0:1, :], grow[0:1, 6:7], None,
                                OP.mult)
                for t8 in range(4):
                    ins = v.tensor_tensor(coltmp[:, 8 + t8:9 + t8], gcol[:, 2:3],
                                          scl[:, 32 + t8:33 + t8], OP.mult)
                return ins
            P("vector", f, reads=("rowf0", "grow", "gcol", "scl"),
              writes=("rowb2", "coltmp"), tag="invsco")
            for half in range(2):
                do0 = half * 1024
                P("sync", (lambda s, do0=do0: s.dma_start(
                    out=brow[2][0:1, :],
                    in_=scr_bias[2 * HID + do0:2 * HID + do0 + 512].rearrange("(o j) -> o j", o=1))),
                  reads=("sbias2",), writes=("brow2b",), dma="sync", tag="obld0")
                P("sync", (lambda s, do0=do0: s.dma_start(
                    out=brow[3][0:1, :],
                    in_=scr_bias[2 * HID + do0 + 512:2 * HID + do0 + 1024].rearrange("(o j) -> o j", o=1))),
                  reads=("sbias2",), writes=("brow3b",), dma="sync", tag="obld1")
                for hc in range(64):
                    P("sync", (lambda s, hc=hc, g0=g0: s.dma_start(
                        out=yfm[hc % 4][:],
                        in_=scr_y[:, hc * 128:(hc + 1) * 128], transpose=True)),
                      reads=("sy0", "sy1", "sy2", "sy3"), writes=(f"yfm{hc % 4}",),
                      dma="sync", tag="yflip")
                    P("sync", (lambda s, hc=hc, do0=do0: s.dma_start(
                        out=owbuf[hc % 2][:],
                        in_=scr_to[do0:do0 + 1024, hc * 128:(hc + 1) * 128],
                        transpose=True)),
                      reads=tuple(f"to_{r * 8 + hc // 8}" for r in range(do0 // 128,
                                                                          do0 // 128 + 8)),
                      writes=(f"owb{hc % 2}",), dma="sync", tag="owld")

                    def fmm(t, hc=hc):
                        for t8 in range(4):
                            ps = (pA(t8), pB(t8))
                            for dq in range(2):
                                ins = t.matmul(
                                    (psA if dq == 0 else psB)[:, t8 * 512:(t8 + 1) * 512],
                                    yfm[hc % 4][:, t8 * 128:(t8 + 1) * 128],
                                    owbuf[hc % 2][:, dq * 512:(dq + 1) * 512],
                                    start=(hc == 0), stop=False)
                        return ins
                    P("tensor", fmm, reads=(f"yfm{hc % 4}", f"owb{hc % 2}"),
                      writes=("pA0", "pA1", "pA2", "pA3", "pB0", "pB1", "pB2", "pB3"),
                      tag=f"offmm{g}_{half}_{hc}")

                def fbias(t):
                    for t8 in range(4):
                        for dq in range(2):
                            ins = t.matmul(
                                (psA if dq == 0 else psB)[:, t8 * 512:(t8 + 1) * 512],
                                rowb[2][0:1, t8 * 128:(t8 + 1) * 128],
                                brow[2 + dq][0:1, :],
                                start=False, stop=True)
                    return ins
                P("tensor", fbias, reads=("rowb2", "brow2b", "brow3b"),
                  writes=("pA0", "pA1", "pA2", "pA3", "pB0", "pB1", "pB2", "pB3"),
                  tag=f"obias{g}_{half}")

                for t8 in range(4):
                    gt8 = g * 4 + t8
                    P("sync", (lambda s, t8=t8, g0=g0, do0=do0: s.dma_start(
                        out=xa[t8 % 2][:, 0:1024],
                        in_=scr_x1[g0 + t8 * 128:g0 + (t8 + 1) * 128, do0:do0 + 1024])),
                      reads=(f"sx1_{gt8}",), writes=(f"xa{t8 % 2}",), dma="sync", tag="x1ld")
                    def fev2(v, t8=t8):
                        v.scalar_tensor_tensor(
                            xa[t8 % 2][:, 0:512], psA[:, t8 * 512:(t8 + 1) * 512],
                            coltmp[:, 8 + t8:9 + t8], xa[t8 % 2][:, 0:512], OP.mult, OP.add)
                        return v.scalar_tensor_tensor(
                            xa[t8 % 2][:, 512:1024], psB[:, t8 * 512:(t8 + 1) * 512],
                            coltmp[:, 8 + t8:9 + t8], xa[t8 % 2][:, 512:1024], OP.mult, OP.add)
                    P("vector", fev2,
                      reads=(f"pA{t8}", f"pB{t8}", "coltmp", f"xa{t8 % 2}"),
                      writes=(f"xa{t8 % 2}",), tag="oev")
                    P("scalar", (lambda s, t8=t8, g0=g0, do0=do0: s.dma_start(
                        out=out_hbm[g0 + t8 * 128:g0 + (t8 + 1) * 128, do0:do0 + 1024],
                        in_=xa[t8 % 2][:, 0:1024])),
                      reads=(f"xa{t8 % 2}",), writes=(f"OUT_{g}_{half}_{t8}",),
                      dma="scalq", tag="outst")
                drain_bg(2)

        if STOP_AFTER != "prologue":
            for c in range(NCH):
                emit_chunk(c)
                if STOP_AFTER == f"chunk{c}":
                    break
            if STOP_AFTER is None or STOP_AFTER == "ffn":
                drain_bg(10**9)
                P("vector", lambda v: v.engine_nop(), reads=(), writes=POOL_KEYS,
                  tag="phase_barrier")
                for g in range(NGRP):
                    emit_ffn_group(g)

        if DEBUG_TAPS:
            allkeys = set()
            for st in prog.steps:
                allkeys.update(st.writes)
            for name in DEBUG_TAPS:
                src = scr_map[name]
                P("gpsimd", (lambda g, src=src, name=name: g.dma_start(
                    out=dbg_outs[name], in_=src)),
                  reads=tuple(allkeys), writes=(f"DBG_{name}",), dma="gpq", tag="dbg")

        out_keys = [k for st in prog.steps for k in st.writes
                    if k.startswith("OUT_") or k.startswith("DBG_")]
        P("gpsimd", lambda g: g.nop(), reads=tuple(set(out_keys)), tag="final")

        emit_all(nc, prog, ctx)

    return nc, dbg_outs


DMA_POOL = 16


def emit_all(nc, prog, ctx):
    deps = prog.resolve()
    sems = {}
    for e in ("tensor", "vector", "scalar", "gpsimd"):
        sems[e] = [ctx.enter_context(nc.semaphore(f"se_{e}")), 0]
    pools = {}
    for q in ("sync", "scalq", "gpq"):
        pools[q] = [[ctx.enter_context(nc.semaphore(f"sq_{q}{i}")), 0]
                    for i in range(DMA_POOL)]
    qidx = {q: 0 for q in pools}

    selfwait = {}
    for st in prog.steps:
        if st.dma is not None:
            pool = pools[st.dma]
            i = qidx[st.dma] % DMA_POOL
            qidx[st.dma] += 1
            sem = pool[i]
            if sem[1] > 0:
                selfwait[st.idx] = (sem[0], sem[1])
            sem[1] += 16
            st.sem, st.val = sem[0], sem[1]
            st.need_inc = True
        elif st.need_inc:
            se = sems[st.eng]
            se[1] += 1
            st.sem, st.val = se[0], se[1]

    warm = ctx.enter_context(nc.sbuf_tensor("warmup", [128, 16], BF16))
    swarm = ctx.enter_context(nc.semaphore("swarm"))
    block = ctx.enter_context(nc.Block())
    per_eng = {e: [] for e in ENGINES}
    for st in prog.steps:
        eng = DMA_QUEUE_ENGINE[st.dma] if st.dma else st.eng
        per_eng[eng].append(st)

    for eng_name in ENGINES:
        steps = per_eng[eng_name]
        if not steps:
            continue

        def body(eng, steps=steps, eng_name=eng_name):
            if eng_name == "gpsimd":
                eng.memset(warm[:], 0.0).then_inc(swarm, 1)
            elif eng_name == "vector":
                eng.memset(warm[0:128, 0:1], 0.0)
            elif eng_name == "scalar":
                eng.memzero(warm[0:128, 2:4])
            elif eng_name == "tensor":
                eng.wait_ge(swarm, 1)
                eng.ldweights(warm[:, 0:16])
            waited = {}
            for st in steps:
                for d in deps[st.idx]:
                    if d.sem is None:
                        continue
                    key = id(d.sem)
                    if waited.get(key, -1) >= d.val:
                        continue
                    eng.wait_ge(d.sem, d.val)
                    waited[key] = d.val
                if st.idx in selfwait:
                    ssem, sval = selfwait[st.idx]
                    if waited.get(id(ssem), -1) < sval:
                        eng.wait_ge(ssem, sval)
                        waited[id(ssem)] = sval
                ins = st.fn(eng)
                if st.need_inc:
                    assert ins is not None, f"step {st.tag} needs inc, returned None"
                    if st.dma is None and eng_name in ("vector", "scalar", "gpsimd"):
                        if eng_name == "vector":
                            eng.memset(warm[0:128, 4:5], 0.0).then_inc(st.sem, 1)
                        elif eng_name == "scalar":
                            eng.memzero(warm[0:128, 6:8]).then_inc(st.sem, 1)
                        else:
                            eng.memset(warm[0:128, 8:9], 0.0).then_inc(st.sem, 1)
                    else:
                        ins.then_inc(st.sem, 16 if st.dma else 1)
        getattr(block, eng_name)(body)


def kernel(**inputs):
    x = np.ascontiguousarray(np.asarray(inputs["L0_attn_x"], dtype=np.float32))
    base = {
        "w0": np.ascontiguousarray(inputs["L0_attn_w0"]),
        "w1": np.ascontiguousarray(inputs["L0_attn_w1"]),
        "w2": np.ascontiguousarray(inputs["L0_attn_w2"]),
        "wqkv": np.ascontiguousarray(inputs["L0_attn_W_qkv"]),
        "wlr": np.ascontiguousarray(inputs["L0_attn_W_lr"]),
        "blr": np.ascontiguousarray(inputs["L0_attn_b_lr"]),
        "wproj": np.ascontiguousarray(inputs["L0_attn_W_proj"]),
        "onw": np.ascontiguousarray(inputs["L0_attn_onorm_w"]),
        "gw": np.ascontiguousarray(inputs["L0_ffn_gate_w"]),
        "gb": np.ascontiguousarray(inputs["L0_ffn_gate_b"]),
        "vw": np.ascontiguousarray(inputs["L0_ffn_val_w"]),
        "vb": np.ascontiguousarray(inputs["L0_ffn_val_b"]),
        "ow": np.ascontiguousarray(inputs["L0_ffn_out_w"]),
        "ob": np.ascontiguousarray(inputs["L0_ffn_out_b"]),
    }
    nc, _ = build_nc()
    in_maps = [dict(base, x=np.ascontiguousarray(x[i])) for i in range(8)]
    res = run_bass_kernel_spmd(nc, in_maps, core_ids=list(range(8)))
    return np.stack([r["out"] for r in res.results], axis=0)
